# version 01
# baseline (speedup 1.0000x reference)
import numpy as np
from contextlib import ExitStack

import concourse.bass as bass
import concourse.mybir as mybir

F32 = mybir.dt.float32
BF16 = mybir.dt.bfloat16
I32 = mybir.dt.int32
AF = mybir.ActivationFunctionType
OP = mybir.AluOpType
AX = mybir.AxisListType

T, D, H, HD, DH = 2048, 1024, 8, 128, 4096
NT = T // 128
NDC = D // 128
NCC = DH // 128
NQ = 4
QT = NT // NQ
EPS = 1e-5

ENGINES = ("sync", "scalar", "vector", "tensor", "gpsimd")


class Prog:
    def __init__(self, nc):
        self.nc = nc
        self.ops = {e: [] for e in ENGINES}
        self.count = {}
        self.sems = {}

    def sem(self, stack, name):
        h = stack.enter_context(self.nc.semaphore(name))
        self.count[name] = 0
        self.sems[name] = h
        return h

    def emit(self, eng, fn, sem=None, n=1):
        if sem is None:
            self.ops[eng].append(fn)
            return None
        self.count[sem.name] += n
        tgt = self.count[sem.name]
        self.ops[eng].append(lambda e, fn=fn, sem=sem, n=n: fn(e).then_inc(sem, n))
        return tgt

    def wait(self, eng, sem, val):
        if val is None or val <= 0:
            return
        self.ops[eng].append(lambda e, sem=sem, val=val: e.wait_ge(sem, val))

    def cur(self, sem):
        return self.count[sem.name]


def build(stage=99, dump=()):
    nc = bass.Bass("TRN2", target_bir_lowering=False, debug=False)
    dump = set(dump)

    d_x = nc.dram_tensor("x", [T, D], F32, kind="ExternalInput").ap()
    d_wq = nc.dram_tensor("wq", [D, D], F32, kind="ExternalInput").ap()
    d_wk = nc.dram_tensor("wk", [D, D], F32, kind="ExternalInput").ap()
    d_wv = nc.dram_tensor("wv", [D, D], F32, kind="ExternalInput").ap()
    d_wb = nc.dram_tensor("wb", [D, H], F32, kind="ExternalInput").ap()
    d_lnw = nc.dram_tensor("lnw", [1, D], F32, kind="ExternalInput").ap()
    d_lnb = nc.dram_tensor("lnb", [1, D], F32, kind="ExternalInput").ap()
    d_w1 = nc.dram_tensor("w1", [DH, D], F32, kind="ExternalInput").ap()
    d_cw = nc.dram_tensor("cw", [DH, 3], F32, kind="ExternalInput").ap()
    d_cb = nc.dram_tensor("cb", [1, DH], F32, kind="ExternalInput").ap()
    d_w2 = nc.dram_tensor("w2", [D, DH], F32, kind="ExternalInput").ap()
    d_masks = nc.dram_tensor("masks", [128, 512], BF16, kind="ExternalInput").ap()
    d_out = nc.dram_tensor("out", [T, D], F32, kind="ExternalOutput").ap()
    d_qf = nc.dram_tensor("qf_s", [NT, 128, H * 128], BF16).ap()
    d_xq = nc.dram_tensor("xq_s", [NT, 128, D], BF16).ap()
    d_y1 = nc.dram_tensor("y1_s", [T, D], F32).ap()
    d_h2 = nc.dram_tensor("h2_s", [NQ - 1, QT, 128, DH], BF16).ap()

    dbg = {}

    def dbg_out(name, shape, dtype=F32):
        dbg[name] = nc.dram_tensor("dbg_" + name, list(shape), dtype,
                                   kind="ExternalOutput").ap()
        return dbg[name]

    prog = Prog(nc)
    stk = ExitStack()
    with stk:
        e = stk.enter_context

        s = [e(nc.sbuf_tensor(f"s{i}", [128, 16384], BF16)) for i in range(5)]
        fsl = e(nc.sbuf_tensor("fsl", [128, 8192], F32))
        q_i32 = e(nc.sbuf_tensor("q_i32", [128, 2048], I32))

        xt2 = s[2][:].rearrange("p (c i d) -> p c i d", c=NDC, i=NT)
        xf = s[1][:].rearrange("p (c t) -> p c t", c=NDC)
        Ktn = s[2][:].rearrange("p (h i d) -> p h i d", h=H, i=NT)
        Vpp = s[4][:].rearrange("p (h i d) -> p h i d", h=H, i=NT)
        Kf = s[0][:].rearrange("p (h t) -> p h t", h=H)
        wgt = s[3][:, 0:8192].rearrange("p (c d) -> p c d", c=NDC)
        wgtb = s[3][:, 8192:8256].rearrange("p (c h) -> p c h", c=NDC)
        beta_fm = s[3][0:16, 9280:11328]
        beta_bf = s[3][:, 11328:11584].rearrange("p (i h) -> p i h", i=NT)
        pw = s[3][:, 0:5120].rearrange("p (q l v) -> p q l v", q=4, l=5)
        onesb = s[3][:, 5120:5248]
        nlp = s[3][:, 5632:7168].rearrange("p (q v) -> p q v", q=4)
        fch = s[3][:, 7168:7680].rearrange("p (q v) -> p q v", q=4)
        ech = s[3][:, 7680:8192].rearrange("p (q v) -> p q v", q=4)
        bbuf = s[3][:, 8192:8704].rearrange("p (q v) -> p q v", q=4)
        rbuf = s[3][:, 8704:9216].rearrange("p (q v) -> p q v", q=4)
        kf2c = s[3][:, 9216:9728].rearrange("p (q v) -> p q v", q=4)
        S_bf = s[3][:, 9728:10752].rearrange("p (h v) -> p h v", h=H)
        qf_ch = s[3][:, 10752:12800].rearrange("p (b v) -> p b v", b=2)
        masks = s[3][:, 12800:13312]
        lnwb = s[3][:, 13312:15360]
        xq_st = s[3][:, 15360:16384]
        wmld = s[1][:].rearrange("p (b v) -> p b v", b=2)
        conv_y = s[3][:, 0:2048]
        w1t_cc = s[3][:, 2048:4096].rearrange("p (b c d) -> p b c d", b=2, c=NDC)
        hfc = s[3][:, 4096:8296].rearrange("p (b v) -> p b v", b=2)
        h2st = s[3][:, 8320:9344].rearrange("p (b v) -> p b v", b=2)
        w2q_m = s[3][:, 9344:13440]
        w1q_cc = s[3][:, 13440:15488].rearrange("p (b v) -> p b v", b=2)
        xq_ld = s[3][:, 15360:16384]
        sif = s[3][:, 0:8192].rearrange("p (b c d) -> p b c d", b=2, c=NCC)
        xqf = s[1][:].rearrange("p (c t) -> p c t", c=NDC)
        hq = s[4][:].rearrange("p (i v) -> p i v", i=QT)
        w2qt = [s[0][:].rearrange("p (c v) -> p c v", c=NCC),
                s[2][:].rearrange("p (c v) -> p c v", c=NCC)]
        ld_a = fsl[:, 0:1024]
        ld_b = fsl[:, 1024:2048]
        kt_f32 = fsl[:, 2048:4096].rearrange("p (b v) -> p b v", b=2)
        v_f32 = fsl[:, 4096:6144].rearrange("p (b v) -> p b v", b=2)
        S_f32 = fsl[:, 6144:7168].rearrange("p (h v) -> p h v", h=8)
        stat = fsl[:, 7168:7296]
        beta_t = fsl[:, 7296:7552].rearrange("p (i h) -> p i h", i=NT)
        junk = fsl[:, 7568:7696]
        wm_acc = fsl[:, 7696:7760]
        cwb = fsl[:, 7760:7888].rearrange("p (j c) -> p j c", c=32)
        wthr = fsl[:, 7888:7904]
        q_sc = fsl[:, 0:2048]
        o_t = fsl[:, 2048:3072]
        y1_t = fsl[:, 3072:4096]
        x_re = fsl[:, 4096:6144].rearrange("p (b v) -> p b v", b=2)
        wld = fsl[:, 0:2048].rearrange("p (b v) -> p b v", b=2)
        wqa = fsl[:, 2048:3072]
        wqb = fsl[:, 3072:4096]
        ysc = fsl[:, 0:2048]
        usc = fsl[:, 2048:4096]
        u2s = fsl[:, 4096:6144]
        y1in = fsl[:, 4096:6144].rearrange("p (b v) -> p b v", b=2)
        y2_t = fsl[:, 6144:7168]

        psP = [e(nc.psum_tensor("psP0", [128, 2048], F32)),
               e(nc.psum_tensor("psP1", [128, 2048], F32))]

        def bank(i):
            return psP[i // 4][:, (i % 4) * 512:(i % 4 + 1) * 512]

        def R(par, lo, hi):
            return psP[par][:, lo:hi]

        s_v = prog.sem(stk, "s_v")
        s_sc = prog.sem(stk, "s_sc")
        s_t = prog.sem(stk, "s_t")
        s_g = prog.sem(stk, "s_g")
        sL = [prog.sem(stk, f"sL{i}") for i in range(2)]
        sQ = [prog.sem(stk, f"sQ{i}") for i in range(2)]
        sX = [prog.sem(stk, f"sX{i}") for i in range(4)]
        sT = [prog.sem(stk, f"sT{i}") for i in range(4)]
        sM = [prog.sem(stk, f"sM{i}") for i in range(6)]
        sS = [prog.sem(stk, f"sS{i}") for i in range(2)]
        s_sty = prog.sem(stk, "s_sty")
        s_stx = prog.sem(stk, "s_stx")
        s_stq = prog.sem(stk, "s_stq")
        s_sth = prog.sem(stk, "s_sth")
        s_sto = prog.sem(stk, "s_sto")
        s_gld = prog.sem(stk, "s_gld")
        s_gl2 = [prog.sem(stk, "s_gl0"), prog.sem(stk, "s_gl1")]

        tpi = [0]

        def dma(dst, src, sem, slow=False):
            return prog.emit("sync", lambda e, dst=dst, src=src, sl=slow: e.dma_start(
                out=dst, in_=src, allow_slow_non_contiguous=sl), sem, 16)

        def gdma(dst, src, sem):
            prog.wait("gpsimd", sem, prog.cur(sem))
            return prog.emit("gpsimd", lambda e, dst=dst, src=src: e.dma_start(
                out=dst, in_=src), sem, 16)

        def transpose(dst, src):
            sm = sT[tpi[0] % 4]
            tpi[0] += 1
            prog.wait("sync", sm, prog.cur(sm))
            tgt = prog.emit("sync", lambda e, dst=dst, src=src: e.dma_start_transpose(
                out=dst, in_=src), sm, 16)
            return (sm, tgt)

        def mm(out_ap, lhsT, rhs, start=True, stop=True, sem=None):
            return prog.emit("tensor", lambda e, o=out_ap, l=lhsT, r=rhs,
                             st=start, sp=stop: e.matmul(out=o, lhsT=l, rhs=r,
                                                         start=st, stop=sp), sem)

        def vop(fn, sem=None):
            return prog.emit("vector", fn, sem)

        def scop(fn, sem=None):
            return prog.emit("scalar", fn, sem)

        def vch(fn, sem=None):
            prog.wait("vector", s_v, prog.cur(s_v))
            return prog.emit("vector", fn, s_v)

        def gch(fn, sem=None):
            prog.wait("gpsimd", s_g, prog.cur(s_g))
            return prog.emit("gpsimd", fn, s_g)

        dma(masks, d_masks, sM[0])
        cast_done = []
        for i in range(NT):
            par = i % 2
            buf = (ld_a, ld_b)[par]
            if i >= 2:
                prog.wait("sync", s_v, cast_done[i - 2])
            dma(buf, d_x[i * 128:(i + 1) * 128, :], sL[par])
            prog.wait("vector", sL[par], prog.cur(sL[par]))
            vop(lambda e, buf=buf, i=i: e.tensor_copy(
                out=xt2[:, :, i, :], in_=buf.rearrange("p (c d) -> p c d", d=128)), s_v)
            cast_done.append(prog.cur(s_v))
        prog.wait("sync", s_v, cast_done[-1])
        tp_xf = transpose(xf.rearrange("p c (i r) -> p (c i) r", r=128), s[2][:])

        if "xf" in dump:
            dd = dbg_out("xf", [D, T], F32)
            prog.wait("vector", *tp_xf)
            for i in range(2 * NDC):
                vop(lambda e, i=i: e.tensor_copy(
                    out=ld_a, in_=xf[:, i // 2, (i % 2) * 1024:(i % 2 + 1) * 1024]), s_v)
                prog.wait("sync", s_v, prog.cur(s_v))
                dma(dd[(i // 2) * 128:(i // 2 + 1) * 128,
                       (i % 2) * 1024:(i % 2 + 1) * 1024], ld_a, sS[0])
                prog.wait("vector", sS[0], prog.cur(sS[0]))
        if stage < 1:
            _finish(prog)
            return nc, dbg

        def load_weight_bf16(dsrc, prev_user_done=None):
            done = []
            entry_v = prog.cur(s_v)
            for dr in range(NDC):
                par = dr % 2
                buf = (ld_a, ld_b)[par]
                if dr < 2:
                    prog.wait("sync", s_v, entry_v)
                else:
                    prog.wait("sync", s_v, done[dr - 2])
                dma(buf, dsrc[dr * 128:(dr + 1) * 128, :], sL[par])
                prog.wait("vector", sL[par], prog.cur(sL[par]))
                if dr == 0 and prev_user_done is not None:
                    prog.wait("vector", s_t, prev_user_done)
                vop(lambda e, buf=buf, dr=dr: e.tensor_copy(out=wgt[:, dr, :], in_=buf), s_v)
                done.append(prog.cur(s_v))
            return done[-1]

        prog.wait("sync", s_v, cast_done[-1])
        wb_ld = dma(ld_a[:, 0:H * NDC].rearrange("p (c h) -> p c h", c=NDC),
                    d_wb.rearrange("(c p) h -> p c h", p=128), sM[1])
        prog.wait("vector", sM[1], wb_ld)
        vop(lambda e: e.tensor_copy(
            out=wgtb[:, :, :],
            in_=ld_a[:, 0:H * NDC].rearrange("p (c h) -> p c h", c=NDC)), s_v)
        vop(lambda e: e.memset(beta_fm, 0.0), s_v)
        wbc = prog.cur(s_v)
        prog.wait("tensor", s_v, wbc)
        prog.wait("tensor", *tp_xf)
        for t4 in range(4):
            for dr in range(NDC):
                mm(bank(t4)[0:H, :], wgtb[:, dr, :], xf[:, dr, t4 * 512:(t4 + 1) * 512],
                   start=(dr == 0), stop=(dr == NDC - 1),
                   sem=s_t if dr == NDC - 1 else None)
            prog.wait("scalar", s_t, prog.cur(s_t))
            scop(lambda e, t4=t4: e.activation(
                out=beta_fm[0:H, t4 * 512:(t4 + 1) * 512], in_=bank(t4)[0:H, :],
                func=AF.Sigmoid), s_sc)
        prog.wait("sync", s_sc, prog.cur(s_sc))
        tp_b = transpose(beta_bf.rearrange("p i h -> p (i h)").rearrange(
            "p (m r) -> p m r", r=16), beta_fm)
        prog.wait("vector", *tp_b)
        vop(lambda e: e.tensor_copy(
            out=beta_t[:, :, :].rearrange("p i h -> p (i h)"),
            in_=beta_bf[:, :, :].rearrange("p i h -> p (i h)")), s_v)
        beta_ready = prog.cur(s_v)

        wq_done = load_weight_bf16(d_wq, prev_user_done=prog.cur(s_t))
        prog.wait("tensor", s_v, wq_done)
        ev_tgt = {}
        st_par = {}
        for t4 in range(4):
            for h in range(H):
                bk = bank(4 + (h % 4))
                prev = (h - 4, t4) if h >= 4 else (h + 4, t4 - 1)
                if prev in ev_tgt:
                    prog.wait("tensor", s_v, ev_tgt[prev])
                for dr in range(NDC):
                    mm(bk, wgt[:, dr, h * 128:(h + 1) * 128],
                       xf[:, dr, t4 * 512:(t4 + 1) * 512],
                       start=(dr == 0), stop=(dr == NDC - 1),
                       sem=s_t if dr == NDC - 1 else None)
                prog.wait("vector", s_t, prog.cur(s_t))
                par = (t4 * H + h) % 2
                if st_par:
                    prog.wait("vector", s_stq, prog.cur(s_stq))
                vop(lambda e, bk=bk, par=par: e.tensor_copy(
                    out=qf_ch[:, par, 0:512], in_=bk), s_v)
                ev_tgt[(h, t4)] = prog.cur(s_v)
                prog.wait("sync", s_v, ev_tgt[(h, t4)])
                for j in range(4):
                    st_par[par] = dma(d_qf[4 * t4 + j, :, h * 128:(h + 1) * 128],
                                      qf_ch[:, par, j * 128:(j + 1) * 128], s_stq)

        wk_done = load_weight_bf16(d_wk, prev_user_done=prog.cur(s_t))
        prog.wait("tensor", s_v, wk_done)
        kv_ev = []
        for ti in range(NT):
            kb = ti % 2
            if ti >= 2:
                prog.wait("tensor", s_v, kv_ev[ti - 2])
            for half in range(2):
                bk = bank(kb * 2 + half)
                for dr in range(NDC):
                    mm(bk, xf[:, dr, ti * 128:(ti + 1) * 128],
                       wgt[:, dr, half * 512:(half + 1) * 512],
                       start=(dr == 0), stop=(dr == NDC - 1),
                       sem=s_t if dr == NDC - 1 else None)
                prog.wait("vector", s_t, prog.cur(s_t))
                vop(lambda e, kb=kb, half=half, bk=bk: e.tensor_copy(
                    out=kt_f32[:, kb, half * 512:(half + 1) * 512], in_=bk), s_v)
            vch(lambda e, kb=kb: e.scalar_tensor_tensor(
                out=v_f32[:, kb, :], in0=kt_f32[:, kb, :], scalar=1.0,
                in1=kt_f32[:, kb, :], op0=OP.mult, op1=OP.mult), None)
            vch(lambda e, kb=kb: e.tensor_reduce(
                out=stat[:, 0:8].rearrange("p (h o) -> p h o", o=1),
                in_=v_f32[:, kb, :].rearrange("p (h d) -> p h d", d=128),
                axis=AX.X, op=OP.add), s_v)
            prog.wait("scalar", s_v, prog.cur(s_v))
            scop(lambda e: e.activation(out=stat[:, 8:16], in_=stat[:, 0:8],
                                        func=AF.Sqrt), s_sc)
            prog.wait("vector", s_sc, prog.cur(s_sc))
            vch(lambda e: e.tensor_scalar(out=stat[:, 16:24], in0=stat[:, 8:16],
                                          scalar1=1e-12, scalar2=None, op0=OP.max), None)
            vch(lambda e: e.reciprocal(out=stat[:, 24:32], in_=stat[:, 16:24]), s_v)
            prog.wait("vector", s_v, prog.cur(s_v))
            prog.wait("vector", s_v, beta_ready)
            for h in range(H):
                vch(lambda e, kb=kb, ti=ti, h=h: e.tensor_scalar(
                    out=Ktn[:, h, ti, :], in0=kt_f32[:, kb, h * 128:(h + 1) * 128],
                    scalar1=stat[:, 24 + h:25 + h], scalar2=None, op0=OP.mult), s_v)
            kv_ev.append(prog.cur(s_v))

        wv_done = load_weight_bf16(d_wv, prev_user_done=prog.cur(s_t))
        prog.wait("tensor", s_v, wv_done)
        v_ev = []
        for ti in range(NT):
            kb = ti % 2
            if ti >= 2:
                prog.wait("tensor", s_v, v_ev[ti - 2])
            for half in range(2):
                bk = bank(kb * 2 + half)
                for dr in range(NDC):
                    mm(bk, xf[:, dr, ti * 128:(ti + 1) * 128],
                       wgt[:, dr, half * 512:(half + 1) * 512],
                       start=(dr == 0), stop=(dr == NDC - 1),
                       sem=s_t if dr == NDC - 1 else None)
                prog.wait("scalar", s_t, prog.cur(s_t))
                scop(lambda e, kb=kb, half=half, bk=bk: e.activation(
                    out=v_f32[:, kb, half * 512:(half + 1) * 512], in_=bk,
                    func=AF.Silu), s_sc)
            prog.wait("vector", s_sc, prog.cur(s_sc))
            prog.wait("vector", s_v, beta_ready)
            for h in range(H):
                vop(lambda e, kb=kb, ti=ti, h=h: e.tensor_scalar(
                    out=Vpp[:, h, ti, :], in0=v_f32[:, kb, h * 128:(h + 1) * 128],
                    scalar1=beta_t[:, ti, h:h + 1], scalar2=None, op0=OP.mult), s_v)
            v_ev.append(prog.cur(s_v))
        vpp_done = prog.cur(s_v)

        prog.wait("sync", s_v, kv_ev[-1])
        tp_kf = transpose(Kf.rearrange("p h (i r) -> p (h i) r", r=128), s[2][:])

        if "proj" in dump:
            dd_kf = dbg_out("kf", [H * 128, T], F32)
            dd_vpp = dbg_out("vpp", [T, D], F32)
            dd_ktn = dbg_out("ktn", [T, D], F32)
            dd_beta = dbg_out("beta", [128, NT * 16], F32)
            dd_qf = dbg_out("qf", [NT, 128, H * 128], F32)
            prog.wait("vector", *tp_kf)
            prog.wait("vector", s_v, vpp_done)
            for i in range(2 * H):
                vop(lambda e, i=i: e.tensor_copy(
                    out=ld_a, in_=Kf[:, i // 2, (i % 2) * 1024:(i % 2 + 1) * 1024]), s_v)
                prog.wait("sync", s_v, prog.cur(s_v))
                dma(dd_kf[(i // 2) * 128:(i // 2 + 1) * 128,
                          (i % 2) * 1024:(i % 2 + 1) * 1024], ld_a, sS[0])
                prog.wait("vector", sS[0], prog.cur(sS[0]))
            for ti in range(NT):
                vop(lambda e, ti=ti: e.tensor_copy(
                    out=ld_a.rearrange("p (h d) -> p h d", h=H), in_=Vpp[:, :, ti, :]), s_v)
                prog.wait("sync", s_v, prog.cur(s_v))
                dma(dd_vpp[ti * 128:(ti + 1) * 128, :], ld_a, sS[0])
                prog.wait("vector", sS[0], prog.cur(sS[0]))
                vop(lambda e, ti=ti: e.tensor_copy(
                    out=ld_a.rearrange("p (h d) -> p h d", h=H), in_=Ktn[:, :, ti, :]), s_v)
                prog.wait("sync", s_v, prog.cur(s_v))
                dma(dd_ktn[ti * 128:(ti + 1) * 128, :], ld_a, sS[0])
                prog.wait("vector", sS[0], prog.cur(sS[0]))
            vop(lambda e: e.tensor_copy(
                out=ld_a[:, 0:NT * 16],
                in_=beta_t[:, :, :].rearrange("p i h -> p (i h)")), s_v)
            prog.wait("sync", s_v, prog.cur(s_v))
            dma(dd_beta[:, :], ld_a[:, 0:NT * 16], sS[0])
            prog.wait("vector", sS[0], prog.cur(sS[0]))
            prog.wait("sync", s_stq, prog.cur(s_stq))
            for c in range(NT):
                dma(qf_ch[:, 0, :], d_qf[c, :, :], sQ[0])
                prog.wait("vector", sQ[0], prog.cur(sQ[0]))
                vop(lambda e: e.tensor_copy(out=ld_a, in_=qf_ch[:, 0, :]), s_v)
                prog.wait("sync", s_v, prog.cur(s_v))
                dma(dd_qf[c, :, :], ld_a, sS[0])
                prog.wait("vector", sS[0], prog.cur(sS[0]))

        if stage < 2:
            _finish(prog)
            return nc, dbg

        vop(lambda e: e.memset(S_f32[:, :, :].rearrange("p h v -> p (h v)"), 0.0), None)
        vop(lambda e: e.memset(S_bf[:, :, :].rearrange("p h v -> p (h v)"), 0.0), None)
        vop(lambda e: e.memset(onesb, 1.0), s_v)
        init_done = prog.cur(s_v)

        prog.wait("sync", s_v, prog.cur(s_v))
        ln_lda = dma(ld_a[0:1, 0:1024], d_lnw, sM[2])
        ln_ldb = dma(ld_b[0:1, 0:1024], d_lnb, sM[3])
        prog.wait("vector", sM[2], ln_lda)
        prog.wait("vector", sM[3], ln_ldb)
        vop(lambda e: e.tensor_copy(out=xq_st[0:1, 0:1024], in_=ld_a[0:1, 0:1024]), s_v)
        prog.wait("tensor", s_v, prog.cur(s_v))
        prog.wait("tensor", s_v, init_done)
        for q in range(2):
            mm(bank(6 + q), onesb[0:1, :], xq_st[0:1, q * 512:(q + 1) * 512], sem=s_t)
            prog.wait("vector", s_t, prog.cur(s_t))
            vop(lambda e, q=q: e.tensor_copy(
                out=lnwb[:, q * 512:(q + 1) * 512], in_=bank(6 + q)), s_v)
        prog.wait("vector", s_v, prog.cur(s_v))
        vop(lambda e: e.tensor_copy(out=xq_st[0:1, 0:1024], in_=ld_b[0:1, 0:1024]), s_v)
        prog.wait("tensor", s_v, prog.cur(s_v))
        for q in range(2):
            mm(bank(6 + q), onesb[0:1, :], xq_st[0:1, q * 512:(q + 1) * 512], sem=s_t)
            prog.wait("vector", s_t, prog.cur(s_t))
            vop(lambda e, q=q: e.tensor_copy(
                out=lnwb[:, 1024 + q * 512:1024 + (q + 1) * 512], in_=bank(6 + q)), s_v)
        lnwb_ready = prog.cur(s_v)

        prog.wait("gpsimd", s_t, prog.cur(s_t))
        wm_part = []
        for cc in range(NCC):
            par = cc % 2
            sl = wmld[:, par, 0:1024]
            absj = wmld[:, par, 5120:6144]
            if cc >= 2:
                prog.wait("gpsimd", s_sc, wm_part[cc - 2])
            gdma(sl, d_w1[cc * 128:(cc + 1) * 128, :], s_gl2[par])
            prog.wait("scalar", s_gl2[par], prog.cur(s_gl2[par]))
            scop(lambda e, sl=sl, absj=absj, cc=cc: e.activation(
                out=absj, in_=sl, func=AF.Abs,
                accum_out=wm_acc[:, cc:cc + 1]), s_sc)
            wm_part.append(prog.cur(s_sc))
        for mt in range(8):
            par = mt % 2
            sl = wmld[:, par, 1024:5120]
            absj = wmld[:, par, 5120:8192][:, 0:3072]
            prog.wait("gpsimd", s_sc, wm_part[NCC + mt - 2])
            gdma(sl, d_w2.rearrange("(a p) k -> p a k", p=128)[:, mt, :], s_gl2[par])
            prog.wait("scalar", s_gl2[par], prog.cur(s_gl2[par]))
            scop(lambda e, sl=sl, mt=mt, par=par: e.activation(
                out=wmld[:, par, 1024:5120], in_=sl, func=AF.Abs,
                accum_out=wm_acc[:, 32 + mt:33 + mt]), s_sc)
            wm_part.append(prog.cur(s_sc))
        wm_scal = prog.cur(s_sc)
        wm_done = None

        prog.wait("tensor", *tp_kf)
        prog.wait("tensor", s_v, lnwb_ready)
        prog.wait("tensor", s_sc, prog.cur(s_sc))
        prog.wait("vector", sM[0], 16)

        qld = [None, None]
        qfree = [None, None]
        xre_t = [None, None]
        xre_used = [None, None]
        y1st_prev = None
        xqst_prev = None
        ostats_prev = None
        par_free = [None, None]
        par_free_sc = [None, None]

        prog.wait("sync", s_stq, prog.cur(s_stq))
        qld[0] = dma(qf_ch[:, 0, :], d_qf[0, :, :], sQ[0])
        xre_t[0] = dma(x_re[:, 0, :], d_x[0:128, :], sX[0])

        NT_RUN = NT
        SUB = {"A", "B", "C", "D", "T"}
        if 20 <= stage < 30:
            NT_RUN = {21: 0, 22: 1, 23: 2, 24: 4, 25: 8}.get(stage, 1)
            if stage in (21, 22, 23, 24, 25):
                pass
        if 220 <= stage <= 225:
            NT_RUN = 1
            SUB = [set(), {"A"}, {"A", "B"}, {"A", "B", "C"},
                   {"A", "B", "C", "D"}, {"A", "B", "C", "D", "T"}][stage - 220]
        DSUB = 9
        if 231 <= stage <= 234:
            NT_RUN = 1
            SUB = {"A", "B", "C", "D"}
            DSUB = stage - 230
        for c in range(NT_RUN):
            cb = c % 2
            if c + 1 < NT:
                nb = (c + 1) % 2
                if qfree[nb] is not None:
                    prog.wait("sync", s_t, qfree[nb])
                qld[nb] = dma(qf_ch[:, nb, :], d_qf[c + 1, :, :], sQ[nb])
                if xre_used[nb] is not None:
                    prog.wait("sync", s_v, xre_used[nb])
                xre_t[nb] = dma(x_re[:, nb, :], d_x[(c + 1) * 128:(c + 2) * 128, :], sX[nb])

            for g in range(2):
                par = g
                heads = list(range(4 * g, 4 * g + 4))
                if "A" not in SUB:
                    continue
                if par_free[par] is not None:
                    prog.wait("tensor", s_v, par_free[par])
                    prog.wait("tensor", s_sc, par_free_sc[par])
                prog.wait("tensor", sQ[cb], qld[cb])
                for hs, h in enumerate(heads):
                    vch(lambda e, hs=hs, h=h, c=c: e.tensor_scalar(
                        out=bbuf[:, hs, :], in0=masks[:, 384:512],
                        scalar1=beta_t[:, c, h:h + 1], scalar2=None, op0=OP.mult))
                diag_t = prog.cur(s_v)
                prog.wait("tensor", s_v, diag_t)
                for hs, h in enumerate(heads):
                    mm(R(par, 1536 + hs * 128, 1664 + hs * 128),
                       Ktn[:, h, c, :], bbuf[:, hs, :],
                       sem=s_t if hs == 3 else None)
                prog.wait("vector", s_t, prog.cur(s_t))
                for hs, h in enumerate(heads):
                    vop(lambda e, hs=hs, par=par: e.tensor_copy(
                        out=kf2c[:, hs, :],
                        in_=R(par, 1536 + hs * 128, 1664 + hs * 128)),
                        s_v if hs == 3 else None)
                kf2_t = prog.cur(s_v)
                prog.wait("tensor", s_v, kf2_t)
                for hs, h in enumerate(heads):
                    go = hs * 384
                    mm(R(par, go, go + 128), Kf[:, h, c * 128:(c + 1) * 128], kf2c[:, hs, :])
                    mm(R(par, go + 128, go + 256), kf2c[:, hs, :],
                       Kf[:, h, c * 128:(c + 1) * 128])
                    mm(R(par, go + 256, go + 384), Kf[:, h, c * 128:(c + 1) * 128],
                       qf_ch[:, cb, h * 128:(h + 1) * 128],
                       sem=s_t if hs == 3 else None)
                prog.wait("vector", s_t, prog.cur(s_t))
                for hs in range(4):
                    go = hs * 384
                    vop(lambda e, hs=hs, go=go, par=par: e.tensor_tensor(
                        out=nlp[:, hs, 0:128], in0=R(par, go, go + 128),
                        in1=masks[:, 0:128], op=OP.mult), None)
                    vop(lambda e, hs=hs, go=go, par=par: e.tensor_tensor(
                        out=nlp[:, hs, 128:256], in0=R(par, go + 128, go + 256),
                        in1=masks[:, 128:256], op=OP.mult), None)
                    vop(lambda e, hs=hs, go=go, par=par: e.tensor_tensor(
                        out=nlp[:, hs, 256:384], in0=R(par, go + 256, go + 384),
                        in1=masks[:, 256:384], op=OP.mult),
                        s_v if hs == 3 else None)
                mask_t = prog.cur(s_v)
                if "B" not in SUB:
                    continue
                sq_ev = {}
                for l in range(5):
                    lo = 1024 * (l % 2)
                    prog.wait("tensor", s_v, mask_t if l == 0 else sq_ev[l - 1])
                    for hs in range(4):
                        if l == 0:
                            lsrc, nsrc = nlp[:, hs, 128:256], nlp[:, hs, 0:128]
                        else:
                            lsrc = pw[:, hs, l - 1, 0:128]
                            nsrc = pw[:, hs, l - 1, 128:256]
                        mm(R(par, lo + hs * 256, lo + hs * 256 + 128), nsrc, lsrc)
                        mm(R(par, lo + hs * 256 + 128, lo + hs * 256 + 256), lsrc, nsrc,
                           sem=s_t if hs == 3 else None)
                    prog.wait("vector", s_t, prog.cur(s_t))
                    for hs in range(4):
                        vop(lambda e, hs=hs, l=l, lo=lo, par=par: e.tensor_copy(
                            out=pw[:, hs, l, :],
                            in_=R(par, lo + hs * 256, lo + hs * 256 + 256)),
                            s_v if hs == 3 else None)
                    sq_ev[l] = prog.cur(s_v)
                prog.wait("tensor", s_v, sq_ev[4])
                for hs in range(4):
                    mm(R(par, 1024 + hs * 128, 1152 + hs * 128), pw[:, hs, 4, 0:128],
                       pw[:, hs, 4, 128:256], sem=s_t if hs == 3 else None)
                prog.wait("vector", s_t, prog.cur(s_t))
                for hs in range(4):
                    vop(lambda e, hs=hs, par=par: e.tensor_tensor(
                        out=fch[:, hs, :], in0=R(par, 1024 + hs * 128, 1152 + hs * 128),
                        in1=masks[:, 384:512], op=OP.add), s_v if hs == 3 else None)
                f_t = prog.cur(s_v)
                if "C" not in SUB:
                    continue
                for step, l in enumerate([4, 3, 2, 1, 0]):
                    lo = 512 * (step % 2)
                    prog.wait("tensor", s_v, f_t)
                    for hs in range(4):
                        mm(R(par, lo + hs * 128, lo + hs * 128 + 128), pw[:, hs, l, 0:128],
                           fch[:, hs, :], sem=s_t if hs == 3 else None)
                    prog.wait("vector", s_t, prog.cur(s_t))
                    for hs in range(4):
                        vop(lambda e, hs=hs, lo=lo, par=par: e.tensor_tensor(
                            out=fch[:, hs, :],
                            in0=R(par, lo + hs * 128, lo + hs * 128 + 128),
                            in1=fch[:, hs, :], op=OP.add), s_v if hs == 3 else None)
                    f_t = prog.cur(s_v)
                prog.wait("tensor", s_v, f_t)
                for hs in range(4):
                    mm(R(par, 1024 + hs * 128, 1152 + hs * 128), nlp[:, hs, 128:256],
                       fch[:, hs, :], sem=s_t if hs == 3 else None)
                prog.wait("vector", s_t, prog.cur(s_t))
                for hs in range(4):
                    vop(lambda e, hs=hs, par=par: e.tensor_tensor(
                        out=ech[:, hs, :], in0=fch[:, hs, :],
                        in1=R(par, 1024 + hs * 128, 1152 + hs * 128),
                        op=OP.subtract), s_v if hs == 3 else None)
                e_t = prog.cur(s_v)
                if "D" not in SUB:
                    continue
                prog.wait("tensor", s_v, e_t)
                if ostats_prev is not None:
                    prog.wait("scalar", s_v, ostats_prev)
                for hs, h in enumerate(heads):
                    b0 = hs * 512
                    mm(R(par, b0, b0 + 128), kf2c[:, hs, :], S_bf[:, h, :], sem=s_t)
                    prog.wait("vector", s_t, prog.cur(s_t))
                    vop(lambda e, hs=hs, h=h, c=c, b0=b0, par=par: e.tensor_tensor(
                        out=bbuf[:, hs, :], in0=Vpp[:, h, c, :],
                        in1=R(par, b0, b0 + 128), op=OP.subtract), s_v)
                    prog.wait("tensor", s_v, prog.cur(s_v))
                    if DSUB < 2:
                        continue
                    mm(R(par, b0 + 128, b0 + 256), ech[:, hs, :], bbuf[:, hs, :], sem=s_t)
                    prog.wait("scalar", s_t, prog.cur(s_t))
                    scop(lambda e, hs=hs, b0=b0, par=par: e.activation(
                        out=rbuf[:, hs, :], in_=R(par, b0 + 128, b0 + 256),
                        func=AF.Identity), s_sc)
                    prog.wait("tensor", s_sc, prog.cur(s_sc))
                    if DSUB < 3:
                        continue
                    mm(R(par, b0 + 384, b0 + 512), qf_ch[:, cb, h * 128:(h + 1) * 128],
                       S_bf[:, h, :], start=True, stop=False)
                    mm(R(par, b0 + 384, b0 + 512), nlp[:, hs, 256:384], rbuf[:, hs, :],
                       start=False, stop=True)
                    mm(R(par, b0 + 256, b0 + 384), Ktn[:, h, c, :], rbuf[:, hs, :], sem=s_t)
                    seq_t = prog.cur(s_t)
                    prog.wait("vector", s_t, seq_t)
                    if DSUB >= 4:
                        vch(lambda e, h=h, b0=b0, par=par: e.tensor_tensor(
                            out=S_f32[:, h, :], in0=S_f32[:, h, :],
                            in1=R(par, b0 + 256, b0 + 384), op=OP.add))
                        vch(lambda e, h=h: e.tensor_copy(
                            out=S_bf[:, h, :], in_=S_f32[:, h, :]))
                    prog.wait("scalar", s_t, seq_t)
                    prog.wait("scalar", s_v, prog.cur(s_v))
                    scop(lambda e, hs=hs, h=h, b0=b0, par=par: e.activation(
                        out=o_t[:, h * 128:(h + 1) * 128],
                        in_=R(par, b0 + 384, b0 + 512), func=AF.Identity), s_sc)
                par_free[par] = prog.cur(s_v)
                par_free_sc[par] = prog.cur(s_sc)
            qfree[cb] = prog.cur(s_t)

            if "T" not in SUB:
                continue
            o_done = prog.cur(s_sc)
            prog.wait("vector", s_sc, o_done)
            vch(lambda e: e.tensor_reduce(
                out=stat[:, 40:41], in_=o_t, axis=AX.X, op=OP.add), None)
            vch(lambda e: e.scalar_tensor_tensor(
                out=q_sc[:, 0:1024], in0=o_t, scalar=1.0, in1=o_t,
                op0=OP.mult, op1=OP.mult), None)
            vch(lambda e: e.tensor_reduce(
                out=stat[:, 41:42], in_=q_sc[:, 0:1024], axis=AX.X, op=OP.add), s_v)
            ostats_prev = prog.cur(s_v)
            vch(lambda e: e.tensor_scalar(
                out=stat[:, 42:43], in0=stat[:, 40:41], scalar1=1.0 / D, scalar2=None,
                op0=OP.mult), None)
            vch(lambda e: e.scalar_tensor_tensor(
                out=stat[:, 43:44], in0=stat[:, 42:43], scalar=-1.0, in1=stat[:, 42:43],
                op0=OP.mult, op1=OP.mult), None)
            vch(lambda e: e.scalar_tensor_tensor(
                out=stat[:, 44:45], in0=stat[:, 41:42], scalar=1.0 / D, in1=stat[:, 43:44],
                op0=OP.mult, op1=OP.add), s_v)
            prog.wait("scalar", s_v, prog.cur(s_v))
            vch(lambda e: e.tensor_scalar(out=stat[:, 44:45], in0=stat[:, 44:45],
                                        scalar1=float(EPS), scalar2=None, op0=OP.add))
            prog.wait("scalar", s_v, prog.cur(s_v))
            scop(lambda e: e.activation(out=stat[:, 45:46], in_=stat[:, 44:45],
                                        func=AF.Sqrt), s_sc)
            prog.wait("vector", s_sc, prog.cur(s_sc))
            vch(lambda e: e.reciprocal(out=stat[:, 46:47], in_=stat[:, 45:46]), None)
            vch(lambda e: e.scalar_tensor_tensor(
                out=stat[:, 47:48], in0=stat[:, 42:43], scalar=-1.0, in1=stat[:, 46:47],
                op0=OP.mult, op1=OP.mult), s_v)
            ln1_t = prog.cur(s_v)
            prog.wait("scalar", s_v, ln1_t)
            scop(lambda e: e.activation(out=q_sc[:, 0:1024], in_=o_t, func=AF.Identity,
                                        bias=stat[:, 47:48], scale=stat[:, 46:47]), s_sc)
            an_t = prog.cur(s_sc)
            prog.wait("vector", s_v, lnwb_ready)
            prog.wait("vector", sX[cb], xre_t[cb])
            if y1st_prev is not None:
                prog.wait("vector", s_sty, y1st_prev)
            vch(lambda e, cb=cb: e.scalar_tensor_tensor(
                out=y1_t, in0=lnwb[:, 1024:2048], scalar=1.0, in1=x_re[:, cb, :],
                op0=OP.mult, op1=OP.add), s_v)
            xre_used[cb] = prog.cur(s_v)
            prog.wait("vector", s_sc, an_t)
            vch(lambda e: e.tensor_tensor(
                out=q_sc[:, 1024:2048], in0=q_sc[:, 0:1024], in1=lnwb[:, 0:1024],
                op=OP.mult), None)
            vch(lambda e: e.tensor_tensor(
                out=y1_t, in0=y1_t, in1=q_sc[:, 1024:2048], op=OP.add), None)
            vch(lambda e: e.tensor_reduce(
                out=stat[:, 48:49], in_=y1_t, axis=AX.X, op=OP.add), None)
            vch(lambda e: e.scalar_tensor_tensor(
                out=q_sc[:, 0:1024], in0=y1_t, scalar=1.0, in1=y1_t,
                op0=OP.mult, op1=OP.mult), None)
            vch(lambda e: e.tensor_reduce(
                out=stat[:, 49:50], in_=q_sc[:, 0:1024], axis=AX.X, op=OP.add), None)
            vch(lambda e: e.tensor_scalar(
                out=stat[:, 50:51], in0=stat[:, 48:49], scalar1=1.0 / D, scalar2=None,
                op0=OP.mult), None)
            vch(lambda e: e.scalar_tensor_tensor(
                out=stat[:, 51:52], in0=stat[:, 50:51], scalar=-1.0, in1=stat[:, 50:51],
                op0=OP.mult, op1=OP.mult), None)
            vch(lambda e: e.scalar_tensor_tensor(
                out=stat[:, 52:53], in0=stat[:, 49:50], scalar=1.0 / D, in1=stat[:, 51:52],
                op0=OP.mult, op1=OP.add), s_v)
            prog.wait("scalar", s_v, prog.cur(s_v))
            vch(lambda e: e.tensor_scalar(out=stat[:, 52:53], in0=stat[:, 52:53],
                                        scalar1=float(EPS), scalar2=None, op0=OP.add))
            prog.wait("scalar", s_v, prog.cur(s_v))
            scop(lambda e: e.activation(out=stat[:, 53:54], in_=stat[:, 52:53],
                                        func=AF.Sqrt), s_sc)
            prog.wait("vector", s_sc, prog.cur(s_sc))
            vch(lambda e: e.reciprocal(out=stat[:, 54:55], in_=stat[:, 53:54]), None)
            vch(lambda e: e.scalar_tensor_tensor(
                out=stat[:, 55:56], in0=stat[:, 50:51], scalar=-1.0, in1=stat[:, 54:55],
                op0=OP.mult, op1=OP.mult), s_v)
            prog.wait("scalar", s_v, prog.cur(s_v))
            scop(lambda e: e.activation(out=y1_t, in_=y1_t, func=AF.Identity,
                                        bias=stat[:, 55:56], scale=stat[:, 54:55]), s_sc)
            y1_done = prog.cur(s_sc)
            prog.wait("sync", s_sc, y1_done)
            y1st_prev = dma(d_y1[c * 128:(c + 1) * 128, :], y1_t, s_sty)
            prog.wait("vector", s_sc, y1_done)
            vch(lambda e: e.tensor_reduce(
                out=stat[:, 56:57], in_=y1_t, axis=AX.X, op=OP.max,
                apply_absolute_value=True), None)
            vch(lambda e: e.tensor_scalar(
                out=stat[:, 57:58], in0=stat[:, 56:57], scalar1=1e-5, scalar2=None,
                op0=OP.max), s_v)
            prog.wait("vector", s_v, prog.cur(s_v))
            vch(lambda e: e.reciprocal(out=stat[:, 58:59], in_=stat[:, 57:58]), None)
            vch(lambda e: e.tensor_scalar(
                out=stat[:, 59:60], in0=stat[:, 58:59], scalar1=127.0, scalar2=None,
                op0=OP.mult), None)
            vch(lambda e: e.tensor_scalar(
                out=stat[:, 60:61], in0=stat[:, 57:58], scalar1=1.0 / 127.0,
                scalar2=None, op0=OP.mult), s_v)
            prog.wait("vector", s_v, prog.cur(s_v))
            vch(lambda e: e.tensor_scalar(
                out=q_sc[:, 0:1024], in0=y1_t, scalar1=stat[:, 59:60], scalar2=None,
                op0=OP.mult), None)
            vch(lambda e: e.tensor_scalar(
                out=q_i32[:, 0:1024], in0=q_sc[:, 0:1024], scalar1=-0.5, scalar2=None,
                op0=OP.add), None)
            vch(lambda e: e.scalar_tensor_tensor(
                out=q_sc[:, 1024:2048], in0=q_i32[:, 0:1024], scalar=0.5,
                in1=q_sc[:, 0:1024], op0=OP.add, op1=OP.is_le), None)
            vch(lambda e: e.scalar_tensor_tensor(
                out=q_sc[:, 0:1024], in0=q_i32[:, 0:1024], scalar=1.5,
                in1=q_sc[:, 0:1024], op0=OP.add, op1=OP.is_le), None)
            vch(lambda e: e.tensor_tensor(
                out=q_sc[:, 0:1024], in0=q_sc[:, 0:1024], in1=q_sc[:, 1024:2048],
                op=OP.add), None)
            vch(lambda e: e.tensor_tensor(
                out=q_sc[:, 1024:2048], in0=q_sc[:, 0:1024], in1=q_i32[:, 0:1024],
                op=OP.add), None)
            if xqst_prev is not None:
                prog.wait("vector", s_stx, xqst_prev)
            vch(lambda e: e.tensor_scalar(
                out=xq_st, in0=q_sc[:, 1024:2048], scalar1=stat[:, 60:61],
                scalar2=None, op0=OP.mult), s_v)
            prog.wait("sync", s_v, prog.cur(s_v))
            xqst_prev = dma(d_xq[c, :, :], xq_st, s_stx)

        if 20 <= stage <= 25 or 220 <= stage <= 234:
            _finish(prog)
            return nc, dbg
        if "p2" in dump:
            dd_y1 = dbg_out("y1d", [T, D], F32)
            dd_xq = dbg_out("xqd", [NT, 128, D], F32)
            prog.wait("sync", s_sty, prog.cur(s_sty))
            prog.wait("sync", s_stx, prog.cur(s_stx))
            for c2 in range(NT):
                dma(x_re[:, 0, :], d_y1[c2 * 128:(c2 + 1) * 128, :], sX[0])
                prog.wait("vector", sX[0], prog.cur(sX[0]))
                vop(lambda e: e.tensor_copy(out=ld_a, in_=x_re[:, 0, :]), s_v)
                prog.wait("sync", s_v, prog.cur(s_v))
                dma(dd_y1[c2 * 128:(c2 + 1) * 128, :], ld_a, sS[0])
                prog.wait("vector", sS[0], prog.cur(sS[0]))
                dma(qf_ch[:, 0, :], d_xq[c2, :, :], sQ[0])
                prog.wait("vector", sQ[0], prog.cur(sQ[0]))
                vop(lambda e: e.tensor_copy(out=ld_a, in_=qf_ch[:, 0, :]), s_v)
                prog.wait("sync", s_v, prog.cur(s_v))
                dma(dd_xq[c2, :, :], ld_a, sS[0])
                prog.wait("vector", sS[0], prog.cur(sS[0]))

        if stage < 3:
            _finish(prog)
            return nc, dbg

        prog.wait("vector", s_sc, wm_scal)
        vch(lambda e: e.tensor_reduce(
            out=wm_acc[:, 40:41], in_=wm_acc[:, 0:32], axis=AX.X, op=OP.add))
        vch(lambda e: e.tensor_reduce(
            out=wm_acc[:, 41:42], in_=wm_acc[:, 32:40], axis=AX.X, op=OP.add))
        vch(lambda e: e.memset(junk[:, 0:1], 1.0), s_v)
        prog.wait("tensor", s_v, prog.cur(s_v))
        mm(R(0, 0, 1)[0:1, :], wm_acc[:, 40:41], junk[:, 0:1], sem=s_t)
        mm(R(0, 512, 513)[0:1, :], wm_acc[:, 41:42], junk[:, 0:1], sem=s_t)
        prog.wait("vector", s_t, prog.cur(s_t))
        vch(lambda e: e.tensor_scalar(
            out=stat[0:1, 61:62], in0=R(0, 0, 1)[0:1, :], scalar1=1.0 / (DH * D),
            scalar2=None, op0=OP.mult), None)
        vch(lambda e: e.tensor_scalar(
            out=stat[0:1, 62:63], in0=R(0, 512, 513)[0:1, :], scalar1=1.0 / (DH * D),
            scalar2=None, op0=OP.mult), s_v)
        prog.wait("vector", s_v, prog.cur(s_v))
        vch(lambda e: e.tensor_copy(out=xq_st[0:1, 0:2], in_=stat[0:1, 61:63]), s_v)
        prog.wait("tensor", s_v, prog.cur(s_v))
        mm(R(0, 1024, 1026), onesb[0:1, :], xq_st[0:1, 0:2], sem=s_t)
        prog.wait("vector", s_t, prog.cur(s_t))
        vch(lambda e: e.tensor_copy(out=wthr[:, 2:3], in_=R(0, 1024, 1025)), None)
        vch(lambda e: e.tensor_copy(out=wthr[:, 5:6], in_=R(0, 1025, 1026)), None)
        vch(lambda e: e.tensor_scalar(out=wthr[:, 0:1], in0=R(0, 1024, 1025),
                                      scalar1=0.5, scalar2=None, op0=OP.mult), None)
        vch(lambda e: e.tensor_scalar(out=wthr[:, 1:2], in0=R(0, 1024, 1025),
                                      scalar1=-0.5, scalar2=None, op0=OP.mult), None)
        vch(lambda e: e.tensor_scalar(out=wthr[:, 3:4], in0=R(0, 1025, 1026),
                                      scalar1=0.5, scalar2=None, op0=OP.mult), None)
        vch(lambda e: e.tensor_scalar(out=wthr[:, 4:5], in0=R(0, 1025, 1026),
                                      scalar1=-0.5, scalar2=None, op0=OP.mult), s_v)
        prog.wait("vector", s_v, prog.cur(s_v))
        wthr_ready = prog.cur(s_v)

        cw_t = dma(ld_b[:, 0:96].rearrange("p (c j) -> p c j", j=3),
                   d_cw.rearrange("(c p) j -> p c j", p=128), sM[4], slow=True)
        cb_t = dma(cwb[:, 3, :], d_cb.rearrange("a (c p) -> p c a", p=128)[
            :, :, 0], sM[5], slow=True)
        prog.wait("vector", sM[4], cw_t)
        prog.wait("vector", sM[5], cb_t)
        vch(lambda e: e.tensor_copy(
            out=cwb[:, 0:3, :], in_=ld_b[:, 0:96].rearrange("p (c j) -> p j c", j=3)))

        prog.wait("sync", s_stx, prog.cur(s_stx))
        prog.wait("sync", s_sc, wm_scal)
        prog.wait("sync", s_gl2[0], prog.cur(s_gl2[0]))
        prog.wait("sync", s_gl2[1], prog.cur(s_gl2[1]))
        xq_tp = []
        for i in range(NT):
            par = i % 2
            if i >= 2:
                prog.wait("sync", *xq_tp[i - 2])
            ldt = dma(qf_ch[:, par, 0:1024], d_xq[i, :, :], sQ[par])
            prog.wait("sync", sQ[par], ldt)
            xq_tp.append(transpose(xqf[:, :, i * 128:(i + 1) * 128],
                                   qf_ch[:, par, 0:1024]))
        for smx in sT:
            prog.wait("tensor", smx, prog.cur(smx))

        vop(lambda e: e.memset(hfc[:, 0, 0:1], 0.0), None)
        vop(lambda e: e.memset(hfc[:, 0, 2049:2050], 0.0), None)
        vop(lambda e: e.memset(hfc[:, 1, 0:1], 0.0), None)
        vop(lambda e: e.memset(hfc[:, 1, 2049:2050], 0.0), s_v)
        prog.wait("tensor", s_v, prog.cur(s_v))

        w1q_tp = [None, None]
        w1q_used = [None, None]
        hfc_free = [None, None]
        h2st_free = [None, None]
        w2cast = []
        w2_tp = []
        conv_done_v = None
        mm_ev = {}

        def w2_mtile(mt):
            for j4 in range(4):
                par = (mt * 4 + j4) % 2
                buf = wld[:, par, :][:, 0:1024]
                idx = mt * 4 + j4
                if idx >= 2:
                    prog.wait("sync", s_v, w2cast[idx - 2])
                dma(buf, d_w2.rearrange("(a p) k -> p a k", p=128)[:, mt,
                                                                  j4 * 1024:(j4 + 1) * 1024], sL[par])
                prog.wait("vector", sL[par], prog.cur(sL[par]))
                vch(lambda e, buf=buf: e.tensor_scalar(
                    out=wqa, in0=buf, scalar1=wthr[:, 3:4], scalar2=wthr[:, 5:6],
                    op0=OP.is_ge, op1=OP.mult), None)
                vch(lambda e, buf=buf: e.tensor_scalar(
                    out=wqb, in0=buf, scalar1=wthr[:, 4:5], scalar2=wthr[:, 5:6],
                    op0=OP.is_le, op1=OP.mult), None)
                vch(lambda e, j4=j4: e.tensor_tensor(
                    out=w2q_m[:, j4 * 1024:(j4 + 1) * 1024], in0=wqa, in1=wqb,
                    op=OP.subtract), s_v)
                w2cast.append(prog.cur(s_v))
            prog.wait("sync", s_v, w2cast[-1])
            if len(w2_tp) >= 2:
                prog.wait("sync", *w2_tp[-2])
            half, r0_ = mt // 4, (mt % 4) * 128
            w2_tp.append(transpose(w2qt[half][:, :, r0_:r0_ + 128], w2q_m[:]))

        prog.wait("vector", s_sty, prog.cur(s_sty))
        prog.wait("vector", s_stx, prog.cur(s_stx))
        prog.wait("vector", s_sc, prog.cur(s_sc))
        for cc in range(NCC):
            par = cc % 2
            buf = wld[:, par, :][:, 0:1024]
            if cc >= 2:
                prog.wait("sync", s_v, mm_ev.get(("w1c", cc - 2)))
            dma(buf, d_w1[cc * 128:(cc + 1) * 128, :], sL[par])
            prog.wait("vector", sL[par], prog.cur(sL[par]))
            if cc == 0:
                prog.wait("vector", s_v, wthr_ready)
            vch(lambda e, buf=buf: e.tensor_scalar(
                out=wqa, in0=buf, scalar1=wthr[:, 0:1], scalar2=wthr[:, 2:3],
                op0=OP.is_ge, op1=OP.mult), None)
            vch(lambda e, buf=buf: e.tensor_scalar(
                out=wqb, in0=buf, scalar1=wthr[:, 1:2], scalar2=wthr[:, 2:3],
                op0=OP.is_le, op1=OP.mult), None)
            if w1q_tp[par] is not None:
                prog.wait("vector", *w1q_tp[par])
            vch(lambda e, par=par: e.tensor_tensor(
                out=w1q_cc[:, par, :], in0=wqa, in1=wqb, op=OP.subtract), s_v)
            mm_ev[("w1c", cc)] = prog.cur(s_v)
            prog.wait("sync", s_v, mm_ev[("w1c", cc)])
            if w1q_used[par] is not None:
                prog.wait("sync", s_t, w1q_used[par])
            w1q_tp[par] = transpose(
                w1t_cc[:, par, :, :].rearrange("p c d -> p (c d)").rearrange(
                    "p (m r) -> p m r", r=128), w1q_cc[:, par, :])
            prog.wait("tensor", *w1q_tp[par])
            if hfc_free[par] is not None:
                for sm, tg in hfc_free[par]:
                    prog.wait("scalar", sm, tg)
            for t4 in range(4):
                bk = bank(par * 2 + t4 % 2)
                if t4 >= 2:
                    prog.wait("tensor", s_sc, mm_ev[("hfe", cc, t4 - 2)])
                for dr in range(NDC):
                    mm(bk, w1t_cc[:, par, dr, :], xqf[:, dr, t4 * 512:(t4 + 1) * 512],
                       start=(dr == 0), stop=(dr == NDC - 1),
                       sem=s_t if dr == NDC - 1 else None)
                prog.wait("scalar", s_t, prog.cur(s_t))
                scop(lambda e, par=par, t4=t4, bk=bk: e.copy(
                    out=hfc[:, par, 1 + t4 * 512:1 + (t4 + 1) * 512], in_=bk), s_sc)
                mm_ev[("hfe", cc, t4)] = prog.cur(s_sc)
            w1q_used[par] = prog.cur(s_t)
            prog.wait("vector", s_sc, mm_ev[("hfe", cc, 3)])
            if cc == 0:
                prog.wait("vector", sM[4], cw_t)
                prog.wait("vector", sM[5], cb_t)
            if conv_done_v is not None:
                for smx, tgx in conv_done_v:
                    prog.wait("vector", smx, tgx)
            vch(lambda e, par=par, cc=cc: e.tensor_scalar(
                out=conv_y, in0=hfc[:, par, 1:2049], scalar1=cwb[:, 1, cc:cc + 1],
                scalar2=cwb[:, 3, cc:cc + 1], op0=OP.mult, op1=OP.add), None)
            vch(lambda e, par=par, cc=cc: e.scalar_tensor_tensor(
                out=conv_y, in0=hfc[:, par, 0:2048], scalar=cwb[:, 0, cc:cc + 1],
                in1=conv_y, op0=OP.mult, op1=OP.add), None)
            vch(lambda e, par=par, cc=cc: e.scalar_tensor_tensor(
                out=conv_y, in0=hfc[:, par, 2:2050], scalar=cwb[:, 2, cc:cc + 1],
                in1=conv_y, op0=OP.mult, op1=OP.add), s_v)
            conv_t = prog.cur(s_v)
            prog.wait("scalar", s_v, conv_t)
            scop(lambda e: e.activation(
                out=conv_y, in_=conv_y, func=AF.Silu), s_sc)
            silu_t = prog.cur(s_sc)
            prog.wait("sync", s_sc, silu_t)
            tps = []
            tps.append(transpose(hq[:, :, cc * 128:(cc + 1) * 128],
                                 conv_y[:, 0:512]))
            for q in range(1, NQ):
                hb = q % 2
                if h2st_free[hb] is not None:
                    prog.wait("sync", s_sth, prog.cur(s_sth))
                tq = transpose(h2st[:, hb, 0:512].rearrange("p (m r) -> p m r", r=128),
                               conv_y[:, q * 512:(q + 1) * 512])
                prog.wait("sync", *tq)
                h2st_free[hb] = dma(
                    d_h2[q - 1, :, :, cc * 128:(cc + 1) * 128].rearrange(
                        "i p r -> p i r"),
                    h2st[:, hb, 0:512].rearrange("p (i r) -> p i r", r=128), s_sth)
                tps.append(tq)
            hfc_free[par] = tps
            conv_done_v = tps
            if cc % 4 == 3:
                w2_mtile(cc // 4)

        w2qt_done = w2_tp[-1]
        hq0_done = hfc_free

        if "p3" in dump:
            dd_hf = dbg_out("hfd", [QT * 128, DH], F32)
            for sm_tg in hfc_free:
                if sm_tg:
                    for sm, tg in sm_tg:
                        prog.wait("vector", sm, tg)
            for i in range(QT):
                for hhalf in range(4):
                    vop(lambda e, i=i, hhalf=hhalf: e.tensor_copy(
                        out=ld_a, in_=hq[:, i, hhalf * 1024:(hhalf + 1) * 1024]), s_v)
                    prog.wait("sync", s_v, prog.cur(s_v))
                    dma(dd_hf[i * 128:(i + 1) * 128, hhalf * 1024:(hhalf + 1) * 1024],
                        ld_a, sS[0])
                    prog.wait("vector", sS[0], prog.cur(sS[0]))

        if stage < 4:
            _finish(prog)
            return nc, dbg

        prog.wait("tensor", *w2qt_done)
        prog.wait("vector", *w2qt_done)
        prog.wait("sync", s_sth, prog.cur(s_sth))
        hload = [None, None, None, None]
        y1ld = [None, None]
        sif_tp = [None, None]
        sif_used = [None, None]
        out_prev = None

        for q in range(NQ):
            if q > 0:
                for i in range(QT):
                    prog.wait("sync", s_v, prog.cur(s_v))
                    hload[i] = dma(hq[:, i, :], d_h2[q - 1, i, :, :], sX[i])
            for i in range(QT):
                ti = q * QT + i
                prog.wait("vector", s_sc, prog.cur(s_sc))
                prog.wait("vector", s_sto, prog.cur(s_sto))
                if q > 0:
                    prog.wait("vector", sX[i], hload[i])
                else:
                    for sm_tg in hfc_free:
                        if sm_tg:
                            for sm, tg in sm_tg:
                                prog.wait("vector", sm, tg)
                vch(lambda e, i=i: e.tensor_reduce(
                    out=stat[:, 63:64], in_=hq[:, i, :], axis=AX.X, op=OP.max,
                    apply_absolute_value=True), None)
                vch(lambda e: e.tensor_scalar(
                    out=stat[:, 64:65], in0=stat[:, 63:64], scalar1=1e-5, scalar2=None,
                    op0=OP.max), s_v)
                prog.wait("vector", s_v, prog.cur(s_v))
                vch(lambda e: e.reciprocal(out=stat[:, 65:66], in_=stat[:, 64:65]), None)
                vch(lambda e: e.tensor_scalar(
                    out=stat[:, 66:67], in0=stat[:, 65:66], scalar1=127.0, scalar2=None,
                    op0=OP.mult), None)
                vch(lambda e: e.tensor_scalar(
                    out=stat[:, 67:68], in0=stat[:, 64:65], scalar1=1.0 / 127.0,
                    scalar2=None, op0=OP.mult), s_v)
                prog.wait("vector", s_v, prog.cur(s_v))
                for hf2 in range(2):
                    sl = hq[:, i, hf2 * 2048:(hf2 + 1) * 2048]
                    vch(lambda e, sl=sl: e.tensor_scalar(
                        out=ysc, in0=sl, scalar1=stat[:, 66:67], scalar2=None,
                        op0=OP.mult), None)
                    vch(lambda e: e.tensor_scalar(
                        out=q_i32[:, 0:2048], in0=ysc, scalar1=-0.5, scalar2=None,
                        op0=OP.add), None)
                    vch(lambda e: e.scalar_tensor_tensor(
                        out=usc, in0=q_i32[:, 0:2048], scalar=0.5, in1=ysc,
                        op0=OP.add, op1=OP.is_le), None)
                    vch(lambda e: e.scalar_tensor_tensor(
                        out=u2s, in0=q_i32[:, 0:2048], scalar=1.5, in1=ysc,
                        op0=OP.add, op1=OP.is_le), None)
                    vch(lambda e: e.tensor_tensor(
                        out=usc, in0=usc, in1=u2s, op=OP.add), None)
                    vch(lambda e: e.tensor_tensor(
                        out=usc, in0=usc, in1=q_i32[:, 0:2048], op=OP.add), None)
                    vch(lambda e, sl=sl: e.tensor_scalar(
                        out=sl, in0=usc, scalar1=stat[:, 67:68], scalar2=None,
                        op0=OP.mult), s_v)
                si_done = prog.cur(s_v)
                spar = ti % 2
                if sif_used[spar] is not None:
                    prog.wait("sync", s_t, sif_used[spar])
                prog.wait("sync", s_v, si_done)
                sif_tp[spar] = transpose(
                    sif[:, spar, :, :].rearrange("p c d -> p (c d)").rearrange(
                        "p (m r) -> p m r", r=128), hq[:, i, :])
                ypar = ti % 2
                if ti >= 2:
                    prog.wait("sync", s_v, mm_ev.get(("y3", ti - 2)))
                y1ld[ypar] = dma(y1in[:, ypar, :], d_y1[ti * 128:(ti + 1) * 128, :],
                                 sQ[ypar])
                prog.wait("tensor", *sif_tp[spar])
                for mh in range(2):
                    bk = bank(4 * (ti % 2) + mh)
                    if ("f2e", ti - 1, mh) in mm_ev:
                        pass
                    if ("f2e", ti - 2, mh) in mm_ev:
                        prog.wait("tensor", s_sc, mm_ev[("f2e", ti - 2, mh)])
                    for c2 in range(NCC):
                        mm(bk, sif[:, spar, c2, :], w2qt[mh][:, c2, :],
                           start=(c2 == 0), stop=(c2 == NCC - 1),
                           sem=s_t if c2 == NCC - 1 else None)
                    prog.wait("scalar", s_t, prog.cur(s_t))
                    scop(lambda e, mh=mh, bk=bk: e.copy(
                        out=y2_t[:, mh * 512:(mh + 1) * 512], in_=bk), s_sc)
                    mm_ev[("f2e", ti, mh)] = prog.cur(s_sc)
                sif_used[spar] = prog.cur(s_t)
                prog.wait("vector", s_sc, mm_ev[("f2e", ti, 1)])
                prog.wait("vector", sQ[ypar], y1ld[ypar])
                vch(lambda e, ypar=ypar: e.tensor_tensor(
                    out=ysc[:, 0:1024], in0=y1in[:, ypar, :], in1=y2_t, op=OP.add), s_v)
                mm_ev[("y3", ti)] = prog.cur(s_v)
                vch(lambda e: e.tensor_reduce(
                    out=stat[:, 70:71], in_=ysc[:, 0:1024], axis=AX.X, op=OP.add), None)
                vch(lambda e: e.scalar_tensor_tensor(
                    out=usc[:, 0:1024], in0=ysc[:, 0:1024], scalar=1.0,
                    in1=ysc[:, 0:1024], op0=OP.mult, op1=OP.mult), None)
                vch(lambda e: e.tensor_reduce(
                    out=stat[:, 71:72], in_=usc[:, 0:1024], axis=AX.X, op=OP.add), s_v)
                vch(lambda e: e.tensor_scalar(
                    out=stat[:, 72:73], in0=stat[:, 70:71], scalar1=1.0 / D,
                    scalar2=None, op0=OP.mult), None)
                vch(lambda e: e.scalar_tensor_tensor(
                    out=stat[:, 73:74], in0=stat[:, 72:73], scalar=-1.0,
                    in1=stat[:, 72:73], op0=OP.mult, op1=OP.mult), None)
                vch(lambda e: e.scalar_tensor_tensor(
                    out=stat[:, 74:75], in0=stat[:, 71:72], scalar=1.0 / D,
                    in1=stat[:, 73:74], op0=OP.mult, op1=OP.add), s_v)
                prog.wait("scalar", s_v, prog.cur(s_v))
                vch(lambda e: e.tensor_scalar(out=stat[:, 74:75], in0=stat[:, 74:75],
                                            scalar1=float(EPS), scalar2=None, op0=OP.add))
                prog.wait("scalar", s_v, prog.cur(s_v))
                scop(lambda e: e.activation(out=stat[:, 75:76], in_=stat[:, 74:75],
                                            func=AF.Sqrt), s_sc)
                prog.wait("vector", s_sc, prog.cur(s_sc))
                vch(lambda e: e.reciprocal(out=stat[:, 76:77], in_=stat[:, 75:76]), None)
                vch(lambda e: e.scalar_tensor_tensor(
                    out=stat[:, 77:78], in0=stat[:, 72:73], scalar=-1.0,
                    in1=stat[:, 76:77], op0=OP.mult, op1=OP.mult), s_v)
                prog.wait("scalar", s_v, prog.cur(s_v))
                if out_prev is not None:
                    prog.wait("scalar", s_sto, out_prev)
                scop(lambda e: e.activation(
                    out=usc[:, 0:1024], in_=ysc[:, 0:1024], func=AF.Identity,
                    bias=stat[:, 77:78], scale=stat[:, 76:77]), s_sc)
                prog.wait("sync", s_sc, prog.cur(s_sc))
                out_prev = dma(d_out[ti * 128:(ti + 1) * 128, :], usc[:, 0:1024], s_sto)

        _finish(prog)
        return nc, dbg


def _finish(prog):
    nc = prog.nc
    for nm, h in prog.sems.items():
        if nm in ("s_gld", "s_gl0", "s_gl1"):
            prog.wait("gpsimd", h, prog.cur(h))
        elif nm[0] == "s" and nm[1] in "LQXTMS" or nm.startswith("s_st"):
            prog.wait("sync", h, prog.cur(h))
    with nc.Block() as block0:
        @block0.gpsimd
        def _(g):
            for nm in list(prog.count):
                g.sem_clear(prog.sems[nm])
    with nc.Block() as block:
        @block.sync
        def _(eng):
            for f in prog.ops["sync"]:
                f(eng)

        @block.scalar
        def _(eng):
            for f in prog.ops["scalar"]:
                f(eng)

        @block.vector
        def _(eng):
            for f in prog.ops["vector"]:
                f(eng)

        @block.tensor
        def _(eng):
            for f in prog.ops["tensor"]:
                f(eng)

        @block.gpsimd
        def _(eng):
            for f in prog.ops["gpsimd"]:
                f(eng)


def make_consts():
    import ml_dtypes
    iu = np.triu(np.ones((128, 128), np.float32), 1)
    il = np.tril(np.ones((128, 128), np.float32), -1)
    ui = np.triu(np.ones((128, 128), np.float32), 0)
    ident = np.eye(128, dtype=np.float32)
    return np.concatenate([iu, il, ui, ident], 1).astype(ml_dtypes.bfloat16)


def in_map_for(inputs, b):
    return {
        "x": np.ascontiguousarray(np.asarray(inputs["L0_attn_x"])[b]),
        "wq": np.asarray(inputs["L0_attn_Wq"]), "wk": np.asarray(inputs["L0_attn_Wk"]),
        "wv": np.asarray(inputs["L0_attn_Wv"]), "wb": np.asarray(inputs["L0_attn_Wb"]),
        "lnw": np.asarray(inputs["L0_attn_ln_w"]).reshape(1, -1),
        "lnb": np.asarray(inputs["L0_attn_ln_b"]).reshape(1, -1),
        "w1": np.asarray(inputs["L0_ffn_w1"]),
        "cw": np.asarray(inputs["L0_ffn_conv_w"]).reshape(DH, 3),
        "cb": np.asarray(inputs["L0_ffn_conv_b"]).reshape(1, DH),
        "w2": np.asarray(inputs["L0_ffn_w2"]),
        "masks": make_consts(),
    }


def kernel(**inputs):
    from concourse.bass_utils import run_bass_kernel_spmd
    nc, _ = build(stage=99)
    in_maps = [in_map_for(inputs, b) for b in range(8)]
    res = run_bass_kernel_spmd(nc, in_maps, core_ids=list(range(8)))
    out = np.stack([np.asarray(res.results[b]["out"]) for b in range(8)], 0)
    return out.astype(np.float32, copy=False)


# version 02
# speedup vs baseline: 1.0865x; 1.0865x over previous
import numpy as np
from contextlib import ExitStack

import concourse.bass as bass
import concourse.mybir as mybir

F32 = mybir.dt.float32
BF16 = mybir.dt.bfloat16
I32 = mybir.dt.int32
AF = mybir.ActivationFunctionType
OP = mybir.AluOpType
AX = mybir.AxisListType

T, D, H, HD, DH = 2048, 1024, 8, 128, 4096
NT = T // 128
NDC = D // 128
NCC = DH // 128
NQ = 4
QT = NT // NQ
EPS = 1e-5

ENGINES = ("sync", "scalar", "vector", "tensor", "gpsimd")


class Prog:
    def __init__(self, nc):
        self.nc = nc
        self.ops = {e: [] for e in ENGINES}
        self.count = {}
        self.sems = {}

    def sem(self, stack, name):
        h = stack.enter_context(self.nc.semaphore(name))
        self.count[name] = 0
        self.sems[name] = h
        return h

    def emit(self, eng, fn, sem=None, n=1):
        if sem is None:
            self.ops[eng].append(fn)
            return None
        self.count[sem.name] += n
        tgt = self.count[sem.name]
        self.ops[eng].append(lambda e, fn=fn, sem=sem, n=n: fn(e).then_inc(sem, n))
        return tgt

    def wait(self, eng, sem, val):
        if val is None or val <= 0:
            return
        self.ops[eng].append(lambda e, sem=sem, val=val: e.wait_ge(sem, val))

    def cur(self, sem):
        return self.count[sem.name]


def build(stage=99, dump=()):
    nc = bass.Bass("TRN2", target_bir_lowering=False, debug=False)
    dump = set(dump)

    d_x = nc.dram_tensor("x", [T, D], F32, kind="ExternalInput").ap()
    d_wq = nc.dram_tensor("wq", [D, D], F32, kind="ExternalInput").ap()
    d_wk = nc.dram_tensor("wk", [D, D], F32, kind="ExternalInput").ap()
    d_wv = nc.dram_tensor("wv", [D, D], F32, kind="ExternalInput").ap()
    d_wb = nc.dram_tensor("wb", [D, H], F32, kind="ExternalInput").ap()
    d_lnw = nc.dram_tensor("lnw", [1, D], F32, kind="ExternalInput").ap()
    d_lnb = nc.dram_tensor("lnb", [1, D], F32, kind="ExternalInput").ap()
    d_w1 = nc.dram_tensor("w1", [DH, D], F32, kind="ExternalInput").ap()
    d_cw = nc.dram_tensor("cw", [DH, 3], F32, kind="ExternalInput").ap()
    d_cb = nc.dram_tensor("cb", [1, DH], F32, kind="ExternalInput").ap()
    d_w2 = nc.dram_tensor("w2", [D, DH], F32, kind="ExternalInput").ap()
    d_masks = nc.dram_tensor("masks", [128, 2560], BF16, kind="ExternalInput").ap()
    d_out = nc.dram_tensor("out", [T, D], F32, kind="ExternalOutput").ap()
    d_qf = nc.dram_tensor("qf_s", [NT, 128, H * 128], BF16).ap()
    d_xq = nc.dram_tensor("xq_s", [NT, 128, D], BF16).ap()
    d_y1 = nc.dram_tensor("y1_s", [T, D], F32).ap()
    d_h2 = nc.dram_tensor("h2_s", [NQ - 1, QT, 128, DH], BF16).ap()

    dbg = {}

    def dbg_out(name, shape, dtype=F32):
        dbg[name] = nc.dram_tensor("dbg_" + name, list(shape), dtype,
                                   kind="ExternalOutput").ap()
        return dbg[name]

    prog = Prog(nc)
    stk = ExitStack()
    with stk:
        e = stk.enter_context

        s = [e(nc.sbuf_tensor(f"s{i}", [128, 16384], BF16)) for i in range(5)]
        fsl = e(nc.sbuf_tensor("fsl", [128, 8192], F32))
        q_i32 = e(nc.sbuf_tensor("q_i32", [128, 2048], I32))

        xt2 = s[2][:].rearrange("p (c i d) -> p c i d", c=NDC, i=NT)
        xf = s[1][:].rearrange("p (c t) -> p c t", c=NDC)
        Ktn = s[2][:].rearrange("p (h i d) -> p h i d", h=H, i=NT)
        Vpp = s[4][:].rearrange("p (h i d) -> p h i d", h=H, i=NT)
        Kf = s[0][:].rearrange("p (h t) -> p h t", h=H)
        wgt = s[3][:, 0:8192].rearrange("p (c d) -> p c d", c=NDC)
        wgtb = s[3][:, 8192:8256].rearrange("p (c h) -> p c h", c=NDC)
        beta_fm = s[3][0:16, 9280:11328]
        beta_bf = s[3][:, 11328:11584].rearrange("p (i h) -> p i h", i=NT)
        pw = s[3][:, 0:5120].rearrange("p (q l v) -> p q l v", q=4, l=5)
        onesb = s[3][:, 5120:5248]
        nlp = s[3][:, 5632:7168].rearrange("p (q v) -> p q v", q=4)
        fch = s[3][:, 7168:7680].rearrange("p (q v) -> p q v", q=4)
        ech = s[3][:, 7680:8192].rearrange("p (q v) -> p q v", q=4)
        bbuf = s[3][:, 8192:8704].rearrange("p (q v) -> p q v", q=4)
        rbuf = s[3][:, 8704:9216].rearrange("p (q v) -> p q v", q=4)
        kf2c = s[3][:, 9216:9728].rearrange("p (q v) -> p q v", q=4)
        S_bf = s[3][:, 9728:10752].rearrange("p (h v) -> p h v", h=H)
        qf_ch = s[3][:, 10752:12800].rearrange("p (b v) -> p b v", b=2)
        masks = s[3][:, 12800:13312]
        lnwb = s[3][:, 13312:15360]
        xq_st = s[3][:, 15360:16384]
        wmld = s[1][:].rearrange("p (b v) -> p b v", b=2)
        conv_y = s[3][:, 0:2048]
        w1t_cc = s[3][:, 2048:4096].rearrange("p (b c d) -> p b c d", b=2, c=NDC)
        hfc = s[3][:, 4096:8296].rearrange("p (b v) -> p b v", b=2)
        h2st = s[3][:, 8320:9344].rearrange("p (b v) -> p b v", b=2)
        w2q_m = s[3][:, 9344:13440]
        w1q_cc = s[3][:, 13440:15488].rearrange("p (b v) -> p b v", b=2)
        xq_ld = s[3][:, 15360:16384]
        sif = s[3][:, 0:8192].rearrange("p (b c d) -> p b c d", b=2, c=NCC)
        xqf = s[1][:].rearrange("p (c t) -> p c t", c=NDC)
        hq = s[4][:].rearrange("p (i v) -> p i v", i=QT)
        w2qt = [s[0][:].rearrange("p (c v) -> p c v", c=NCC),
                s[2][:].rearrange("p (c v) -> p c v", c=NCC)]
        ld_a = fsl[:, 0:1024]
        ld_b = fsl[:, 1024:2048]
        kt_f32 = fsl[:, 2048:4096].rearrange("p (b v) -> p b v", b=2)
        v_f32 = fsl[:, 4096:6144].rearrange("p (b v) -> p b v", b=2)
        S_f32 = fsl[:, 6144:7168].rearrange("p (h v) -> p h v", h=8)
        stat = fsl[:, 7168:7296]
        beta_t = fsl[:, 7296:7552].rearrange("p (i h) -> p i h", i=NT)
        junk = fsl[:, 7568:7696]
        wm_acc = fsl[:, 7696:7760]
        cwb = fsl[:, 7760:7888].rearrange("p (j c) -> p j c", c=32)
        wthr = fsl[:, 7888:7904]
        q_sc = fsl[:, 0:2048]
        o_t = fsl[:, 2048:3072]
        y1_t = fsl[:, 3072:4096]
        x_re = fsl[:, 4096:6144].rearrange("p (b v) -> p b v", b=2)
        wld = fsl[:, 0:2048].rearrange("p (b v) -> p b v", b=2)
        wqa = fsl[:, 2048:3072]
        wqb = fsl[:, 3072:4096]
        ysc = fsl[:, 0:2048]
        usc = fsl[:, 2048:4096]
        u2s = fsl[:, 4096:6144]
        y1in = fsl[:, 4096:6144].rearrange("p (b v) -> p b v", b=2)
        y2_t = fsl[:, 6144:7168]

        psP = [e(nc.psum_tensor("psP0", [128, 2048], F32)),
               e(nc.psum_tensor("psP1", [128, 2048], F32))]

        def bank(i):
            return psP[i // 4][:, (i % 4) * 512:(i % 4 + 1) * 512]

        def R(par, lo, hi):
            return psP[par][:, lo:hi]

        s_v = prog.sem(stk, "s_v")
        s_sc = prog.sem(stk, "s_sc")
        s_t = prog.sem(stk, "s_t")
        s_g = prog.sem(stk, "s_g")
        sL = [prog.sem(stk, f"sL{i}") for i in range(2)]
        sQ = [prog.sem(stk, f"sQ{i}") for i in range(2)]
        sX = [prog.sem(stk, f"sX{i}") for i in range(4)]
        sT = [prog.sem(stk, f"sT{i}") for i in range(4)]
        sM = [prog.sem(stk, f"sM{i}") for i in range(7)]
        sS = [prog.sem(stk, f"sS{i}") for i in range(2)]
        s_sty = prog.sem(stk, "s_sty")
        s_stx = prog.sem(stk, "s_stx")
        s_stq = prog.sem(stk, "s_stq")
        s_sth = prog.sem(stk, "s_sth")
        s_sto = prog.sem(stk, "s_sto")
        s_gld = prog.sem(stk, "s_gld")
        s_gl2 = [prog.sem(stk, "s_gl0"), prog.sem(stk, "s_gl1")]

        tpi = [0]

        def dma(dst, src, sem, slow=False):
            return prog.emit("sync", lambda e, dst=dst, src=src, sl=slow: e.dma_start(
                out=dst, in_=src, allow_slow_non_contiguous=sl), sem, 16)

        def gdma(dst, src, sem):
            prog.wait("gpsimd", sem, prog.cur(sem))
            return prog.emit("gpsimd", lambda e, dst=dst, src=src: e.dma_start(
                out=dst, in_=src), sem, 16)

        def transpose(dst, src):
            sm = sT[tpi[0] % 4]
            tpi[0] += 1
            prog.wait("sync", sm, prog.cur(sm))
            tgt = prog.emit("sync", lambda e, dst=dst, src=src: e.dma_start_transpose(
                out=dst, in_=src), sm, 16)
            return (sm, tgt)

        def mm(out_ap, lhsT, rhs, start=True, stop=True, sem=None):
            return prog.emit("tensor", lambda e, o=out_ap, l=lhsT, r=rhs,
                             st=start, sp=stop: e.matmul(out=o, lhsT=l, rhs=r,
                                                         start=st, stop=sp), sem)

        def vop(fn, sem=None):
            return prog.emit("vector", fn, sem)

        def scop(fn, sem=None):
            return prog.emit("scalar", fn, sem)

        def vch(fn, sem=None):
            prog.wait("vector", s_v, prog.cur(s_v))
            return prog.emit("vector", fn, s_v)

        def gch(fn, sem=None):
            prog.wait("gpsimd", s_g, prog.cur(s_g))
            return prog.emit("gpsimd", fn, s_g)

        dma(masks, d_masks[:, 0:512], sM[0])
        cast_done = []
        for i in range(NT):
            par = i % 2
            buf = (ld_a, ld_b)[par]
            if i >= 2:
                prog.wait("sync", s_v, cast_done[i - 2])
            dma(buf, d_x[i * 128:(i + 1) * 128, :], sL[par])
            prog.wait("vector", sL[par], prog.cur(sL[par]))
            vop(lambda e, buf=buf, i=i: e.tensor_copy(
                out=xt2[:, :, i, :], in_=buf.rearrange("p (c d) -> p c d", d=128)), s_v)
            cast_done.append(prog.cur(s_v))
        prog.wait("sync", s_v, cast_done[-1])
        tp_xf = transpose(xf.rearrange("p c (i r) -> p (c i) r", r=128), s[2][:])

        if "xf" in dump:
            dd = dbg_out("xf", [D, T], F32)
            prog.wait("vector", *tp_xf)
            for i in range(2 * NDC):
                vop(lambda e, i=i: e.tensor_copy(
                    out=ld_a, in_=xf[:, i // 2, (i % 2) * 1024:(i % 2 + 1) * 1024]), s_v)
                prog.wait("sync", s_v, prog.cur(s_v))
                dma(dd[(i // 2) * 128:(i // 2 + 1) * 128,
                       (i % 2) * 1024:(i % 2 + 1) * 1024], ld_a, sS[0])
                prog.wait("vector", sS[0], prog.cur(sS[0]))
        if stage < 1:
            _finish(prog)
            return nc, dbg

        def load_weight_bf16(dsrc, prev_user_done=None):
            done = []
            entry_v = prog.cur(s_v)
            for dr in range(NDC):
                par = dr % 2
                buf = (ld_a, ld_b)[par]
                if dr < 2:
                    prog.wait("sync", s_v, entry_v)
                else:
                    prog.wait("sync", s_v, done[dr - 2])
                dma(buf, dsrc[dr * 128:(dr + 1) * 128, :], sL[par])
                prog.wait("vector", sL[par], prog.cur(sL[par]))
                if dr == 0 and prev_user_done is not None:
                    prog.wait("vector", s_t, prev_user_done)
                vop(lambda e, buf=buf, dr=dr: e.tensor_copy(out=wgt[:, dr, :], in_=buf), s_v)
                done.append(prog.cur(s_v))
            return done[-1]

        prog.wait("sync", s_v, cast_done[-1])
        wb_ld = dma(ld_a[:, 0:H * NDC].rearrange("p (c h) -> p c h", c=NDC),
                    d_wb.rearrange("(c p) h -> p c h", p=128), sM[1])
        prog.wait("vector", sM[1], wb_ld)
        vop(lambda e: e.tensor_copy(
            out=wgtb[:, :, :],
            in_=ld_a[:, 0:H * NDC].rearrange("p (c h) -> p c h", c=NDC)), s_v)
        vop(lambda e: e.memset(beta_fm, 0.0), s_v)
        wbc = prog.cur(s_v)
        prog.wait("tensor", s_v, wbc)
        prog.wait("tensor", *tp_xf)
        for t4 in range(4):
            for dr in range(NDC):
                mm(bank(t4)[0:H, :], wgtb[:, dr, :], xf[:, dr, t4 * 512:(t4 + 1) * 512],
                   start=(dr == 0), stop=(dr == NDC - 1),
                   sem=s_t if dr == NDC - 1 else None)
            prog.wait("scalar", s_t, prog.cur(s_t))
            scop(lambda e, t4=t4: e.activation(
                out=beta_fm[0:H, t4 * 512:(t4 + 1) * 512], in_=bank(t4)[0:H, :],
                func=AF.Sigmoid), s_sc)
        prog.wait("sync", s_sc, prog.cur(s_sc))
        tp_b = transpose(beta_bf.rearrange("p i h -> p (i h)").rearrange(
            "p (m r) -> p m r", r=16), beta_fm)
        prog.wait("vector", *tp_b)
        vop(lambda e: e.tensor_copy(
            out=beta_t[:, :, :].rearrange("p i h -> p (i h)"),
            in_=beta_bf[:, :, :].rearrange("p i h -> p (i h)")), s_v)
        beta_ready = prog.cur(s_v)

        wq_done = load_weight_bf16(d_wq, prev_user_done=prog.cur(s_t))
        prog.wait("tensor", s_v, wq_done)
        ev_tgt = {}
        st_par = {}
        for t4 in range(4):
            for h in range(H):
                bk = bank(4 + (h % 4))
                prev = (h - 4, t4) if h >= 4 else (h + 4, t4 - 1)
                if prev in ev_tgt:
                    prog.wait("tensor", s_v, ev_tgt[prev])
                for dr in range(NDC):
                    mm(bk, wgt[:, dr, h * 128:(h + 1) * 128],
                       xf[:, dr, t4 * 512:(t4 + 1) * 512],
                       start=(dr == 0), stop=(dr == NDC - 1),
                       sem=s_t if dr == NDC - 1 else None)
                prog.wait("vector", s_t, prog.cur(s_t))
                par = (t4 * H + h) % 2
                if st_par:
                    prog.wait("vector", s_stq, prog.cur(s_stq))
                vop(lambda e, bk=bk, par=par: e.tensor_copy(
                    out=qf_ch[:, par, 0:512], in_=bk), s_v)
                ev_tgt[(h, t4)] = prog.cur(s_v)
                prog.wait("sync", s_v, ev_tgt[(h, t4)])
                for j in range(4):
                    st_par[par] = dma(d_qf[4 * t4 + j, :, h * 128:(h + 1) * 128],
                                      qf_ch[:, par, j * 128:(j + 1) * 128], s_stq)

        wk_done = load_weight_bf16(d_wk, prev_user_done=prog.cur(s_t))
        prog.wait("tensor", s_v, wk_done)
        kv_ev = []
        for ti in range(NT):
            kb = ti % 2
            if ti >= 2:
                prog.wait("tensor", s_v, kv_ev[ti - 2])
            for half in range(2):
                bk = bank(kb * 2 + half)
                for dr in range(NDC):
                    mm(bk, xf[:, dr, ti * 128:(ti + 1) * 128],
                       wgt[:, dr, half * 512:(half + 1) * 512],
                       start=(dr == 0), stop=(dr == NDC - 1),
                       sem=s_t if dr == NDC - 1 else None)
                prog.wait("vector", s_t, prog.cur(s_t))
                vop(lambda e, kb=kb, half=half, bk=bk: e.tensor_copy(
                    out=kt_f32[:, kb, half * 512:(half + 1) * 512], in_=bk), s_v)
            vch(lambda e, kb=kb: e.scalar_tensor_tensor(
                out=v_f32[:, kb, :], in0=kt_f32[:, kb, :], scalar=1.0,
                in1=kt_f32[:, kb, :], op0=OP.mult, op1=OP.mult), None)
            vch(lambda e, kb=kb: e.tensor_reduce(
                out=stat[:, 0:8].rearrange("p (h o) -> p h o", o=1),
                in_=v_f32[:, kb, :].rearrange("p (h d) -> p h d", d=128),
                axis=AX.X, op=OP.add), s_v)
            prog.wait("scalar", s_v, prog.cur(s_v))
            scop(lambda e: e.activation(out=stat[:, 8:16], in_=stat[:, 0:8],
                                        func=AF.Sqrt), s_sc)
            prog.wait("vector", s_sc, prog.cur(s_sc))
            vch(lambda e: e.tensor_scalar(out=stat[:, 16:24], in0=stat[:, 8:16],
                                          scalar1=1e-12, scalar2=None, op0=OP.max), None)
            vch(lambda e: e.reciprocal(out=stat[:, 24:32], in_=stat[:, 16:24]), s_v)
            prog.wait("vector", s_v, prog.cur(s_v))
            prog.wait("vector", s_v, beta_ready)
            for h in range(H):
                vch(lambda e, kb=kb, ti=ti, h=h: e.tensor_scalar(
                    out=Ktn[:, h, ti, :], in0=kt_f32[:, kb, h * 128:(h + 1) * 128],
                    scalar1=stat[:, 24 + h:25 + h], scalar2=None, op0=OP.mult), s_v)
            kv_ev.append(prog.cur(s_v))

        wv_done = load_weight_bf16(d_wv, prev_user_done=prog.cur(s_t))
        prog.wait("tensor", s_v, wv_done)
        v_ev = []
        for ti in range(NT):
            kb = ti % 2
            if ti >= 2:
                prog.wait("tensor", s_v, v_ev[ti - 2])
            for half in range(2):
                bk = bank(kb * 2 + half)
                for dr in range(NDC):
                    mm(bk, xf[:, dr, ti * 128:(ti + 1) * 128],
                       wgt[:, dr, half * 512:(half + 1) * 512],
                       start=(dr == 0), stop=(dr == NDC - 1),
                       sem=s_t if dr == NDC - 1 else None)
                prog.wait("scalar", s_t, prog.cur(s_t))
                scop(lambda e, kb=kb, half=half, bk=bk: e.activation(
                    out=v_f32[:, kb, half * 512:(half + 1) * 512], in_=bk,
                    func=AF.Silu), s_sc)
            prog.wait("vector", s_sc, prog.cur(s_sc))
            prog.wait("vector", s_v, beta_ready)
            for h in range(H):
                vop(lambda e, kb=kb, ti=ti, h=h: e.tensor_scalar(
                    out=Vpp[:, h, ti, :], in0=v_f32[:, kb, h * 128:(h + 1) * 128],
                    scalar1=beta_t[:, ti, h:h + 1], scalar2=None, op0=OP.mult), s_v)
            v_ev.append(prog.cur(s_v))
        vpp_done = prog.cur(s_v)

        prog.wait("sync", s_v, kv_ev[-1])
        tp_kf = transpose(Kf.rearrange("p h (i r) -> p (h i) r", r=128), s[2][:])

        if "proj" in dump:
            dd_kf = dbg_out("kf", [H * 128, T], F32)
            dd_vpp = dbg_out("vpp", [T, D], F32)
            dd_ktn = dbg_out("ktn", [T, D], F32)
            dd_beta = dbg_out("beta", [128, NT * 16], F32)
            dd_qf = dbg_out("qf", [NT, 128, H * 128], F32)
            prog.wait("vector", *tp_kf)
            prog.wait("vector", s_v, vpp_done)
            for i in range(2 * H):
                vop(lambda e, i=i: e.tensor_copy(
                    out=ld_a, in_=Kf[:, i // 2, (i % 2) * 1024:(i % 2 + 1) * 1024]), s_v)
                prog.wait("sync", s_v, prog.cur(s_v))
                dma(dd_kf[(i // 2) * 128:(i // 2 + 1) * 128,
                          (i % 2) * 1024:(i % 2 + 1) * 1024], ld_a, sS[0])
                prog.wait("vector", sS[0], prog.cur(sS[0]))
            for ti in range(NT):
                vop(lambda e, ti=ti: e.tensor_copy(
                    out=ld_a.rearrange("p (h d) -> p h d", h=H), in_=Vpp[:, :, ti, :]), s_v)
                prog.wait("sync", s_v, prog.cur(s_v))
                dma(dd_vpp[ti * 128:(ti + 1) * 128, :], ld_a, sS[0])
                prog.wait("vector", sS[0], prog.cur(sS[0]))
                vop(lambda e, ti=ti: e.tensor_copy(
                    out=ld_a.rearrange("p (h d) -> p h d", h=H), in_=Ktn[:, :, ti, :]), s_v)
                prog.wait("sync", s_v, prog.cur(s_v))
                dma(dd_ktn[ti * 128:(ti + 1) * 128, :], ld_a, sS[0])
                prog.wait("vector", sS[0], prog.cur(sS[0]))
            vop(lambda e: e.tensor_copy(
                out=ld_a[:, 0:NT * 16],
                in_=beta_t[:, :, :].rearrange("p i h -> p (i h)")), s_v)
            prog.wait("sync", s_v, prog.cur(s_v))
            dma(dd_beta[:, :], ld_a[:, 0:NT * 16], sS[0])
            prog.wait("vector", sS[0], prog.cur(sS[0]))
            prog.wait("sync", s_stq, prog.cur(s_stq))
            for c in range(NT):
                dma(qf_ch[:, 0, :], d_qf[c, :, :], sQ[0])
                prog.wait("vector", sQ[0], prog.cur(sQ[0]))
                vop(lambda e: e.tensor_copy(out=ld_a, in_=qf_ch[:, 0, :]), s_v)
                prog.wait("sync", s_v, prog.cur(s_v))
                dma(dd_qf[c, :, :], ld_a, sS[0])
                prog.wait("vector", sS[0], prog.cur(sS[0]))

        if stage < 2:
            _finish(prog)
            return nc, dbg

        vop(lambda e: e.memset(S_f32[:, :, :].rearrange("p h v -> p (h v)"), 0.0), None)
        vop(lambda e: e.memset(S_bf[:, :, :].rearrange("p h v -> p (h v)"), 0.0), None)
        vop(lambda e: e.memset(onesb, 1.0), s_v)
        init_done = prog.cur(s_v)

        prog.wait("sync", s_v, prog.cur(s_v))
        ln_lda = dma(ld_a[0:1, 0:1024], d_lnw, sM[2])
        ln_ldb = dma(ld_b[0:1, 0:1024], d_lnb, sM[3])
        prog.wait("vector", sM[2], ln_lda)
        prog.wait("vector", sM[3], ln_ldb)
        vop(lambda e: e.tensor_copy(out=xq_st[0:1, 0:1024], in_=ld_a[0:1, 0:1024]), s_v)
        prog.wait("tensor", s_v, prog.cur(s_v))
        prog.wait("tensor", s_v, init_done)
        for q in range(2):
            mm(bank(6 + q), onesb[0:1, :], xq_st[0:1, q * 512:(q + 1) * 512], sem=s_t)
            prog.wait("vector", s_t, prog.cur(s_t))
            vop(lambda e, q=q: e.tensor_copy(
                out=lnwb[:, q * 512:(q + 1) * 512], in_=bank(6 + q)), s_v)
        prog.wait("vector", s_v, prog.cur(s_v))
        vop(lambda e: e.tensor_copy(out=xq_st[0:1, 0:1024], in_=ld_b[0:1, 0:1024]), s_v)
        prog.wait("tensor", s_v, prog.cur(s_v))
        for q in range(2):
            mm(bank(6 + q), onesb[0:1, :], xq_st[0:1, q * 512:(q + 1) * 512], sem=s_t)
            prog.wait("vector", s_t, prog.cur(s_t))
            vop(lambda e, q=q: e.tensor_copy(
                out=lnwb[:, 1024 + q * 512:1024 + (q + 1) * 512], in_=bank(6 + q)), s_v)
        lnwb_ready = prog.cur(s_v)

        prog.wait("gpsimd", s_t, prog.cur(s_t))
        t_after_p1 = prog.cur(s_t)
        wm_part = []
        for cc in range(NCC):
            par = cc % 2
            sl = wmld[:, par, 0:1024]
            absj = wmld[:, par, 5120:6144]
            if cc >= 2:
                prog.wait("gpsimd", s_sc, wm_part[cc - 2])
            gdma(sl, d_w1[cc * 128:(cc + 1) * 128, :], s_gl2[par])
            prog.wait("scalar", s_gl2[par], prog.cur(s_gl2[par]))
            scop(lambda e, sl=sl, absj=absj, cc=cc: e.activation(
                out=absj, in_=sl, func=AF.Abs,
                accum_out=wm_acc[:, cc:cc + 1]), s_sc)
            wm_part.append(prog.cur(s_sc))
        for mt in range(8):
            par = mt % 2
            sl = wmld[:, par, 1024:5120]
            absj = wmld[:, par, 5120:8192][:, 0:3072]
            prog.wait("gpsimd", s_sc, wm_part[NCC + mt - 2])
            gdma(sl, d_w2.rearrange("(a p) k -> p a k", p=128)[:, mt, :], s_gl2[par])
            prog.wait("scalar", s_gl2[par], prog.cur(s_gl2[par]))
            scop(lambda e, sl=sl, mt=mt, par=par: e.activation(
                out=wmld[:, par, 1024:5120], in_=sl, func=AF.Abs,
                accum_out=wm_acc[:, 32 + mt:33 + mt]), s_sc)
            wm_part.append(prog.cur(s_sc))
        wm_scal = prog.cur(s_sc)
        wm_done = None

        prog.wait("sync", s_t, t_after_p1)
        m4_tgt = dma(s[1][:, 14336:16384], d_masks[:, 512:2560], sM[6])
        prog.wait("vector", sM[6], m4_tgt)

        prog.wait("tensor", *tp_kf)
        prog.wait("tensor", s_v, lnwb_ready)
        prog.wait("tensor", s_sc, prog.cur(s_sc))
        prog.wait("vector", sM[0], 16)

        qld = [None, None]
        qfree = [None, None]
        xre_t = [None, None]
        xre_used = [None, None]
        y1st_prev = None
        xqst_prev = None
        ostats_prev = None
        par_free = [None, None]
        par_free_sc = [None, None]

        prog.wait("sync", s_stq, prog.cur(s_stq))
        qld[0] = dma(qf_ch[:, 0, :], d_qf[0, :, :], sQ[0])
        xre_t[0] = dma(x_re[:, 0, :], d_x[0:128, :], sX[0])

        NT_RUN = NT
        if 20 <= stage < 30:
            NT_RUN = {21: 0, 22: 1, 23: 2, 24: 4, 25: 8}.get(stage, 1)
        if 220 <= stage <= 234:
            NT_RUN = 1

        bbufG = [bbuf, s[1][:, 6144:6656].rearrange("p (q v) -> p q v", q=4)]
        kf2cG = [kf2c, s[1][:, 6656:7168].rearrange("p (q v) -> p q v", q=4)]
        echG = [ech, s[1][:, 7168:7680].rearrange("p (q v) -> p q v", q=4)]
        ptG = [nlp[:, :, 256:384],
               s[1][:, 7680:8192].rearrange("p (q v) -> p q v", q=4)]
        eT = {}

        def co(gens):
            gens = [gn for gn in gens if gn is not None]
            while gens:
                keep = []
                for gn in gens:
                    try:
                        next(gn)
                        keep.append(gn)
                    except StopIteration:
                        pass
                gens = keep

        m4U = s[1][:, 14336:14848].rearrange("p (q v) -> p q v", q=4)
        m4L = s[1][:, 14848:15360].rearrange("p (q v) -> p q v", q=4)
        m4P = s[1][:, 15360:15872].rearrange("p (q v) -> p q v", q=4)
        m4I = s[1][:, 15872:16384]

        def gen_ABC(j):
            c, g = divmod(j, 2)
            par = g
            cb = c % 2
            heads = list(range(4 * g, 4 * g + 4))
            bbf, kfc, ptv = bbufG[g], kf2cG[g], ptG[g]
            kfc_f = kfc.rearrange("p q v -> p (q v)")
            ec_f = echG[g].rearrange("p q v -> p (q v)")
            fch_f = fch.rearrange("p q v -> p (q v)")
            psa = psP[par][:, 0:1536].rearrange("p (q v) -> p q v", q=4)
            if par_free[par] is not None:
                prog.wait("tensor", s_v, par_free[par])
                prog.wait("tensor", s_sc, par_free_sc[par])
            prog.wait("tensor", sQ[cb], qld[cb])
            for hs, h in enumerate(heads):
                vch(lambda e, hs=hs, h=h, c=c, bbf=bbf: e.tensor_scalar(
                    out=bbf[:, hs, :], in0=masks[:, 384:512],
                    scalar1=beta_t[:, c, h:h + 1], scalar2=None, op0=OP.mult))
            diag_t = prog.cur(s_v)
            yield
            prog.wait("tensor", s_v, diag_t)
            for hs, h in enumerate(heads):
                mm(R(par, 1536 + hs * 128, 1664 + hs * 128),
                   Ktn[:, h, c, :], bbf[:, hs, :],
                   sem=s_t if hs == 3 else None)
            prog.wait("scalar", s_t, prog.cur(s_t))
            yield
            scop(lambda e, par=par, kfc_f=kfc_f: e.activation(
                out=kfc_f, in_=R(par, 1536, 2048), func=AF.Identity), s_sc)
            kf2_t = prog.cur(s_sc)
            prog.wait("tensor", s_sc, kf2_t)
            yield
            for hs, h in enumerate(heads):
                go = hs * 384
                mm(R(par, go, go + 128), Kf[:, h, c * 128:(c + 1) * 128],
                   kfc[:, hs, :])
                mm(R(par, go + 128, go + 256), kfc[:, hs, :],
                   Kf[:, h, c * 128:(c + 1) * 128])
                mm(R(par, go + 256, go + 384), Kf[:, h, c * 128:(c + 1) * 128],
                   qf_ch[:, cb, h * 128:(h + 1) * 128],
                   sem=s_t if hs == 3 else None)
            prog.wait("vector", s_t, prog.cur(s_t))
            yield
            vop(lambda e, par=par, psa=psa: e.tensor_tensor(
                out=nlp[:, :, 0:128], in0=psa[:, :, 0:128], in1=m4U,
                op=OP.mult), None)
            vop(lambda e, par=par, psa=psa: e.tensor_tensor(
                out=nlp[:, :, 128:256], in0=psa[:, :, 128:256], in1=m4L,
                op=OP.mult), None)
            vop(lambda e, par=par, psa=psa, ptv=ptv: e.tensor_tensor(
                out=ptv, in0=psa[:, :, 256:384], in1=m4P, op=OP.mult), s_v)
            mask_t = prog.cur(s_v)
            yield
            sq_ev = {}
            for l in range(5):
                lo = 1024 * (l % 2)
                if l == 0:
                    prog.wait("tensor", s_v, mask_t)
                else:
                    prog.wait("tensor", s_sc, sq_ev[l - 1])
                for hs in range(4):
                    if l == 0:
                        lsrc, nsrc = nlp[:, hs, 128:256], nlp[:, hs, 0:128]
                    else:
                        lsrc = pw[:, hs, l - 1, 0:128]
                        nsrc = pw[:, hs, l - 1, 128:256]
                    mm(R(par, lo + hs * 256, lo + hs * 256 + 128), nsrc, lsrc)
                    mm(R(par, lo + hs * 256 + 128, lo + hs * 256 + 256), lsrc, nsrc,
                       sem=s_t if hs == 3 else None)
                prog.wait("scalar", s_t, prog.cur(s_t))
                yield
                scop(lambda e, l=l, lo=lo, par=par: e.activation(
                    out=pw[:, :, l, :],
                    in_=R(par, lo, lo + 1024).rearrange("p (q v) -> p q v", q=4),
                    func=AF.Identity), s_sc)
                sq_ev[l] = prog.cur(s_sc)
            prog.wait("tensor", s_sc, sq_ev[4])
            yield
            for hs in range(4):
                mm(R(par, 1024 + hs * 128, 1152 + hs * 128), pw[:, hs, 4, 0:128],
                   pw[:, hs, 4, 128:256], sem=s_t if hs == 3 else None)
            prog.wait("vector", s_t, prog.cur(s_t))
            yield
            vop(lambda e, par=par, fch_f=fch_f: e.tensor_tensor(
                out=fch_f, in0=R(par, 1024, 1536), in1=m4I, op=OP.add), s_v)
            f_t = prog.cur(s_v)
            yield
            for step, l in enumerate([4, 3, 2, 1, 0]):
                lo = 512 * (step % 2)
                prog.wait("tensor", s_v, f_t)
                for hs in range(4):
                    mm(R(par, lo + hs * 128, lo + hs * 128 + 128), pw[:, hs, l, 0:128],
                       fch[:, hs, :], sem=s_t if hs == 3 else None)
                prog.wait("vector", s_t, prog.cur(s_t))
                yield
                vop(lambda e, lo=lo, par=par, fch_f=fch_f: e.tensor_tensor(
                    out=fch_f, in0=R(par, lo, lo + 512), in1=fch_f,
                    op=OP.add), s_v)
                f_t = prog.cur(s_v)
            prog.wait("tensor", s_v, f_t)
            yield
            for hs in range(4):
                mm(R(par, 1024 + hs * 128, 1152 + hs * 128), nlp[:, hs, 128:256],
                   fch[:, hs, :], sem=s_t if hs == 3 else None)
            prog.wait("vector", s_t, prog.cur(s_t))
            yield
            vop(lambda e, par=par, fch_f=fch_f, ec_f=ec_f: e.tensor_tensor(
                out=ec_f, in0=fch_f, in1=R(par, 1024, 1536),
                op=OP.subtract), s_v)
            eT[j] = prog.cur(s_v)

        def gen_D(j):
            c, g = divmod(j, 2)
            par = g
            cb = c % 2
            heads = list(range(4 * g, 4 * g + 4))
            bbf, kfc, ecv, ptv = bbufG[g], kf2cG[g], echG[g], ptG[g]
            psq = psP[par][:, 0:2048].rearrange("p (q v) -> p q v", q=4)
            Vg = Vpp[:, 4 * g:4 * g + 4, c, :]
            Sf = S_f32[:, 4 * g:4 * g + 4, :]
            Sb = S_bf[:, 4 * g:4 * g + 4, :]
            prog.wait("tensor", s_v, eT[j])
            if ostats_prev is not None:
                prog.wait("scalar", s_v, ostats_prev)
            for hs, h in enumerate(heads):
                b0 = hs * 512
                mm(R(par, b0, b0 + 128), kfc[:, hs, :], S_bf[:, h, :],
                   sem=s_t if hs == 3 else None)
            tB = prog.cur(s_t)
            yield
            prog.wait("vector", s_t, tB)
            vop(lambda e, par=par, psq=psq, bbf=bbf, Vg=Vg: e.tensor_tensor(
                out=bbf, in0=Vg, in1=psq[:, :, 0:128], op=OP.subtract), s_v)
            tv = prog.cur(s_v)
            yield
            prog.wait("tensor", s_v, tv)
            for hs, h in enumerate(heads):
                b0 = hs * 512
                mm(R(par, b0 + 128, b0 + 256), ecv[:, hs, :], bbf[:, hs, :],
                   sem=s_t if hs == 3 else None)
            tR = prog.cur(s_t)
            yield
            prog.wait("scalar", s_t, tR)
            scop(lambda e, par=par, psq=psq: e.activation(
                out=rbuf[:, :, :], in_=psq[:, :, 128:256],
                func=AF.Identity), s_sc)
            tsc = prog.cur(s_sc)
            yield
            prog.wait("tensor", s_sc, tsc)
            for hs, h in enumerate(heads):
                b0 = hs * 512
                mm(R(par, b0 + 384, b0 + 512), qf_ch[:, cb, h * 128:(h + 1) * 128],
                   S_bf[:, h, :], start=True, stop=False)
                mm(R(par, b0 + 384, b0 + 512), ptv[:, hs, :], rbuf[:, hs, :],
                   start=False, stop=True)
                mm(R(par, b0 + 256, b0 + 384), Ktn[:, h, c, :], rbuf[:, hs, :],
                   sem=s_t if hs == 3 else None)
            tD = prog.cur(s_t)
            yield
            prog.wait("vector", s_t, tD)
            vch(lambda e, par=par, psq=psq, Sf=Sf: e.tensor_tensor(
                out=Sf, in0=Sf, in1=psq[:, :, 256:384], op=OP.add))
            vch(lambda e, Sf=Sf, Sb=Sb: e.tensor_copy(out=Sb, in_=Sf))
            tS = prog.cur(s_v)
            yield
            prog.wait("scalar", s_t, tD)
            prog.wait("scalar", s_v, tS)
            scop(lambda e, par=par, psq=psq, g=g: e.activation(
                out=o_t[:, g * 512:(g + 1) * 512].rearrange(
                    "p (q v) -> p q v", q=4),
                in_=psq[:, :, 384:512], func=AF.Identity), s_sc)
            par_free[par] = prog.cur(s_v)
            par_free_sc[par] = prog.cur(s_sc)
            if g == 1:
                qfree[cb] = prog.cur(s_t)

        def emit_T(c):
            nonlocal ostats_prev, y1st_prev, xqst_prev
            cb = c % 2
            o_done = prog.cur(s_sc)
            prog.wait("vector", s_sc, o_done)
            vch(lambda e: e.tensor_reduce(
                out=stat[:, 40:41], in_=o_t, axis=AX.X, op=OP.add), None)
            vch(lambda e: e.scalar_tensor_tensor(
                out=q_sc[:, 0:1024], in0=o_t, scalar=1.0, in1=o_t,
                op0=OP.mult, op1=OP.mult), None)
            vch(lambda e: e.tensor_reduce(
                out=stat[:, 41:42], in_=q_sc[:, 0:1024], axis=AX.X, op=OP.add), s_v)
            ostats_prev = prog.cur(s_v)
            vch(lambda e: e.tensor_scalar(
                out=stat[:, 42:43], in0=stat[:, 40:41], scalar1=1.0 / D, scalar2=None,
                op0=OP.mult), None)
            vch(lambda e: e.scalar_tensor_tensor(
                out=stat[:, 43:44], in0=stat[:, 42:43], scalar=-1.0, in1=stat[:, 42:43],
                op0=OP.mult, op1=OP.mult), None)
            vch(lambda e: e.scalar_tensor_tensor(
                out=stat[:, 44:45], in0=stat[:, 41:42], scalar=1.0 / D, in1=stat[:, 43:44],
                op0=OP.mult, op1=OP.add), s_v)
            prog.wait("scalar", s_v, prog.cur(s_v))
            vch(lambda e: e.tensor_scalar(out=stat[:, 44:45], in0=stat[:, 44:45],
                                        scalar1=float(EPS), scalar2=None, op0=OP.add))
            prog.wait("scalar", s_v, prog.cur(s_v))
            scop(lambda e: e.activation(out=stat[:, 45:46], in_=stat[:, 44:45],
                                        func=AF.Sqrt), s_sc)
            prog.wait("vector", s_sc, prog.cur(s_sc))
            vch(lambda e: e.reciprocal(out=stat[:, 46:47], in_=stat[:, 45:46]), None)
            vch(lambda e: e.scalar_tensor_tensor(
                out=stat[:, 47:48], in0=stat[:, 42:43], scalar=-1.0, in1=stat[:, 46:47],
                op0=OP.mult, op1=OP.mult), s_v)
            ln1_t = prog.cur(s_v)
            prog.wait("scalar", s_v, ln1_t)
            scop(lambda e: e.activation(out=q_sc[:, 0:1024], in_=o_t, func=AF.Identity,
                                        bias=stat[:, 47:48], scale=stat[:, 46:47]), s_sc)
            an_t = prog.cur(s_sc)
            prog.wait("vector", s_v, lnwb_ready)
            prog.wait("vector", sX[cb], xre_t[cb])
            if y1st_prev is not None:
                prog.wait("vector", s_sty, y1st_prev)
            vch(lambda e, cb=cb: e.scalar_tensor_tensor(
                out=y1_t, in0=lnwb[:, 1024:2048], scalar=1.0, in1=x_re[:, cb, :],
                op0=OP.mult, op1=OP.add), s_v)
            xre_used[cb] = prog.cur(s_v)
            prog.wait("vector", s_sc, an_t)
            vch(lambda e: e.tensor_tensor(
                out=q_sc[:, 1024:2048], in0=q_sc[:, 0:1024], in1=lnwb[:, 0:1024],
                op=OP.mult), None)
            vch(lambda e: e.tensor_tensor(
                out=y1_t, in0=y1_t, in1=q_sc[:, 1024:2048], op=OP.add), None)
            vch(lambda e: e.tensor_reduce(
                out=stat[:, 48:49], in_=y1_t, axis=AX.X, op=OP.add), None)
            vch(lambda e: e.scalar_tensor_tensor(
                out=q_sc[:, 0:1024], in0=y1_t, scalar=1.0, in1=y1_t,
                op0=OP.mult, op1=OP.mult), None)
            vch(lambda e: e.tensor_reduce(
                out=stat[:, 49:50], in_=q_sc[:, 0:1024], axis=AX.X, op=OP.add), None)
            vch(lambda e: e.tensor_scalar(
                out=stat[:, 50:51], in0=stat[:, 48:49], scalar1=1.0 / D, scalar2=None,
                op0=OP.mult), None)
            vch(lambda e: e.scalar_tensor_tensor(
                out=stat[:, 51:52], in0=stat[:, 50:51], scalar=-1.0, in1=stat[:, 50:51],
                op0=OP.mult, op1=OP.mult), None)
            vch(lambda e: e.scalar_tensor_tensor(
                out=stat[:, 52:53], in0=stat[:, 49:50], scalar=1.0 / D, in1=stat[:, 51:52],
                op0=OP.mult, op1=OP.add), s_v)
            prog.wait("scalar", s_v, prog.cur(s_v))
            vch(lambda e: e.tensor_scalar(out=stat[:, 52:53], in0=stat[:, 52:53],
                                        scalar1=float(EPS), scalar2=None, op0=OP.add))
            prog.wait("scalar", s_v, prog.cur(s_v))
            scop(lambda e: e.activation(out=stat[:, 53:54], in_=stat[:, 52:53],
                                        func=AF.Sqrt), s_sc)
            prog.wait("vector", s_sc, prog.cur(s_sc))
            vch(lambda e: e.reciprocal(out=stat[:, 54:55], in_=stat[:, 53:54]), None)
            vch(lambda e: e.scalar_tensor_tensor(
                out=stat[:, 55:56], in0=stat[:, 50:51], scalar=-1.0, in1=stat[:, 54:55],
                op0=OP.mult, op1=OP.mult), s_v)
            prog.wait("scalar", s_v, prog.cur(s_v))
            scop(lambda e: e.activation(out=y1_t, in_=y1_t, func=AF.Identity,
                                        bias=stat[:, 55:56], scale=stat[:, 54:55]), s_sc)
            y1_done = prog.cur(s_sc)
            prog.wait("sync", s_sc, y1_done)
            y1st_prev = dma(d_y1[c * 128:(c + 1) * 128, :], y1_t, s_sty)
            prog.wait("vector", s_sc, y1_done)
            vch(lambda e: e.tensor_reduce(
                out=stat[:, 56:57], in_=y1_t, axis=AX.X, op=OP.max,
                apply_absolute_value=True), None)
            vch(lambda e: e.tensor_scalar(
                out=stat[:, 57:58], in0=stat[:, 56:57], scalar1=1e-5, scalar2=None,
                op0=OP.max), s_v)
            prog.wait("vector", s_v, prog.cur(s_v))
            vch(lambda e: e.reciprocal(out=stat[:, 58:59], in_=stat[:, 57:58]), None)
            vch(lambda e: e.tensor_scalar(
                out=stat[:, 59:60], in0=stat[:, 58:59], scalar1=127.0, scalar2=None,
                op0=OP.mult), None)
            vch(lambda e: e.tensor_scalar(
                out=stat[:, 60:61], in0=stat[:, 57:58], scalar1=1.0 / 127.0,
                scalar2=None, op0=OP.mult), s_v)
            prog.wait("vector", s_v, prog.cur(s_v))
            vch(lambda e: e.tensor_scalar(
                out=q_sc[:, 0:1024], in0=y1_t, scalar1=stat[:, 59:60], scalar2=None,
                op0=OP.mult), None)
            vch(lambda e: e.tensor_scalar(
                out=q_i32[:, 0:1024], in0=q_sc[:, 0:1024], scalar1=-0.5, scalar2=None,
                op0=OP.add), None)
            vch(lambda e: e.scalar_tensor_tensor(
                out=q_sc[:, 1024:2048], in0=q_i32[:, 0:1024], scalar=0.5,
                in1=q_sc[:, 0:1024], op0=OP.add, op1=OP.is_le), None)
            vch(lambda e: e.scalar_tensor_tensor(
                out=q_sc[:, 0:1024], in0=q_i32[:, 0:1024], scalar=1.5,
                in1=q_sc[:, 0:1024], op0=OP.add, op1=OP.is_le), None)
            vch(lambda e: e.tensor_tensor(
                out=q_sc[:, 0:1024], in0=q_sc[:, 0:1024], in1=q_sc[:, 1024:2048],
                op=OP.add), None)
            vch(lambda e: e.tensor_tensor(
                out=q_sc[:, 1024:2048], in0=q_sc[:, 0:1024], in1=q_i32[:, 0:1024],
                op=OP.add), None)
            if xqst_prev is not None:
                prog.wait("vector", s_stx, xqst_prev)
            vch(lambda e: e.tensor_scalar(
                out=xq_st, in0=q_sc[:, 1024:2048], scalar1=stat[:, 60:61],
                scalar2=None, op0=OP.mult), s_v)
            prog.wait("sync", s_v, prog.cur(s_v))
            xqst_prev = dma(d_xq[c, :, :], xq_st, s_stx)

        for j in range(2 * NT_RUN + 1):
            c2, g2 = divmod(j, 2)
            co([gen_ABC(j) if j < 2 * NT_RUN else None,
                gen_D(j - 1) if j >= 1 else None])
            if g2 == 0:
                if c2 >= 1:
                    emit_T(c2 - 1)
                if c2 + 1 < NT_RUN:
                    nb = (c2 + 1) % 2
                    if qfree[nb] is not None:
                        prog.wait("sync", s_t, qfree[nb])
                    qld[nb] = dma(qf_ch[:, nb, :], d_qf[c2 + 1, :, :], sQ[nb])
                    if xre_used[nb] is not None:
                        prog.wait("sync", s_v, xre_used[nb])
                    xre_t[nb] = dma(x_re[:, nb, :],
                                    d_x[(c2 + 1) * 128:(c2 + 2) * 128, :], sX[nb])

        if 20 <= stage <= 25 or 220 <= stage <= 234:
            _finish(prog)
            return nc, dbg
        if "p2" in dump:
            dd_y1 = dbg_out("y1d", [T, D], F32)
            dd_xq = dbg_out("xqd", [NT, 128, D], F32)
            prog.wait("sync", s_sty, prog.cur(s_sty))
            prog.wait("sync", s_stx, prog.cur(s_stx))
            for c2 in range(NT):
                dma(x_re[:, 0, :], d_y1[c2 * 128:(c2 + 1) * 128, :], sX[0])
                prog.wait("vector", sX[0], prog.cur(sX[0]))
                vop(lambda e: e.tensor_copy(out=ld_a, in_=x_re[:, 0, :]), s_v)
                prog.wait("sync", s_v, prog.cur(s_v))
                dma(dd_y1[c2 * 128:(c2 + 1) * 128, :], ld_a, sS[0])
                prog.wait("vector", sS[0], prog.cur(sS[0]))
                dma(qf_ch[:, 0, :], d_xq[c2, :, :], sQ[0])
                prog.wait("vector", sQ[0], prog.cur(sQ[0]))
                vop(lambda e: e.tensor_copy(out=ld_a, in_=qf_ch[:, 0, :]), s_v)
                prog.wait("sync", s_v, prog.cur(s_v))
                dma(dd_xq[c2, :, :], ld_a, sS[0])
                prog.wait("vector", sS[0], prog.cur(sS[0]))

        if stage < 3:
            _finish(prog)
            return nc, dbg

        prog.wait("vector", s_sc, wm_scal)
        vch(lambda e: e.tensor_reduce(
            out=wm_acc[:, 40:41], in_=wm_acc[:, 0:32], axis=AX.X, op=OP.add))
        vch(lambda e: e.tensor_reduce(
            out=wm_acc[:, 41:42], in_=wm_acc[:, 32:40], axis=AX.X, op=OP.add))
        vch(lambda e: e.memset(junk[:, 0:1], 1.0), s_v)
        prog.wait("tensor", s_v, prog.cur(s_v))
        mm(R(0, 0, 1)[0:1, :], wm_acc[:, 40:41], junk[:, 0:1], sem=s_t)
        mm(R(0, 512, 513)[0:1, :], wm_acc[:, 41:42], junk[:, 0:1], sem=s_t)
        prog.wait("vector", s_t, prog.cur(s_t))
        vch(lambda e: e.tensor_scalar(
            out=stat[0:1, 61:62], in0=R(0, 0, 1)[0:1, :], scalar1=1.0 / (DH * D),
            scalar2=None, op0=OP.mult), None)
        vch(lambda e: e.tensor_scalar(
            out=stat[0:1, 62:63], in0=R(0, 512, 513)[0:1, :], scalar1=1.0 / (DH * D),
            scalar2=None, op0=OP.mult), s_v)
        prog.wait("vector", s_v, prog.cur(s_v))
        vch(lambda e: e.tensor_copy(out=xq_st[0:1, 0:2], in_=stat[0:1, 61:63]), s_v)
        prog.wait("tensor", s_v, prog.cur(s_v))
        mm(R(0, 1024, 1026), onesb[0:1, :], xq_st[0:1, 0:2], sem=s_t)
        prog.wait("vector", s_t, prog.cur(s_t))
        vch(lambda e: e.tensor_copy(out=wthr[:, 2:3], in_=R(0, 1024, 1025)), None)
        vch(lambda e: e.tensor_copy(out=wthr[:, 5:6], in_=R(0, 1025, 1026)), None)
        vch(lambda e: e.tensor_scalar(out=wthr[:, 0:1], in0=R(0, 1024, 1025),
                                      scalar1=0.5, scalar2=None, op0=OP.mult), None)
        vch(lambda e: e.tensor_scalar(out=wthr[:, 1:2], in0=R(0, 1024, 1025),
                                      scalar1=-0.5, scalar2=None, op0=OP.mult), None)
        vch(lambda e: e.tensor_scalar(out=wthr[:, 3:4], in0=R(0, 1025, 1026),
                                      scalar1=0.5, scalar2=None, op0=OP.mult), None)
        vch(lambda e: e.tensor_scalar(out=wthr[:, 4:5], in0=R(0, 1025, 1026),
                                      scalar1=-0.5, scalar2=None, op0=OP.mult), s_v)
        prog.wait("vector", s_v, prog.cur(s_v))
        wthr_ready = prog.cur(s_v)

        cw_t = dma(ld_b[:, 0:96].rearrange("p (c j) -> p c j", j=3),
                   d_cw.rearrange("(c p) j -> p c j", p=128), sM[4], slow=True)
        cb_t = dma(cwb[:, 3, :], d_cb.rearrange("a (c p) -> p c a", p=128)[
            :, :, 0], sM[5], slow=True)
        prog.wait("vector", sM[4], cw_t)
        prog.wait("vector", sM[5], cb_t)
        vch(lambda e: e.tensor_copy(
            out=cwb[:, 0:3, :], in_=ld_b[:, 0:96].rearrange("p (c j) -> p j c", j=3)))

        prog.wait("sync", s_stx, prog.cur(s_stx))
        prog.wait("sync", s_sc, wm_scal)
        prog.wait("sync", s_gl2[0], prog.cur(s_gl2[0]))
        prog.wait("sync", s_gl2[1], prog.cur(s_gl2[1]))
        xq_tp = []
        for i in range(NT):
            par = i % 2
            if i >= 2:
                prog.wait("sync", *xq_tp[i - 2])
            ldt = dma(qf_ch[:, par, 0:1024], d_xq[i, :, :], sQ[par])
            prog.wait("sync", sQ[par], ldt)
            xq_tp.append(transpose(xqf[:, :, i * 128:(i + 1) * 128],
                                   qf_ch[:, par, 0:1024]))
        for smx in sT:
            prog.wait("tensor", smx, prog.cur(smx))

        vop(lambda e: e.memset(hfc[:, 0, 0:1], 0.0), None)
        vop(lambda e: e.memset(hfc[:, 0, 2049:2050], 0.0), None)
        vop(lambda e: e.memset(hfc[:, 1, 0:1], 0.0), None)
        vop(lambda e: e.memset(hfc[:, 1, 2049:2050], 0.0), s_v)
        prog.wait("tensor", s_v, prog.cur(s_v))

        w1q_tp = [None, None]
        w1q_used = [None, None]
        hfc_free = [None, None]
        h2st_free = [None, None]
        w2cast = []
        w2_tp = []
        conv_done_v = None
        mm_ev = {}

        def w2_mtile(mt):
            for j4 in range(4):
                par = (mt * 4 + j4) % 2
                buf = wld[:, par, :][:, 0:1024]
                idx = mt * 4 + j4
                if idx >= 2:
                    prog.wait("sync", s_v, w2cast[idx - 2])
                dma(buf, d_w2.rearrange("(a p) k -> p a k", p=128)[:, mt,
                                                                  j4 * 1024:(j4 + 1) * 1024], sL[par])
                prog.wait("vector", sL[par], prog.cur(sL[par]))
                vch(lambda e, buf=buf: e.tensor_scalar(
                    out=wqa, in0=buf, scalar1=wthr[:, 3:4], scalar2=wthr[:, 5:6],
                    op0=OP.is_ge, op1=OP.mult), None)
                vch(lambda e, buf=buf: e.tensor_scalar(
                    out=wqb, in0=buf, scalar1=wthr[:, 4:5], scalar2=wthr[:, 5:6],
                    op0=OP.is_le, op1=OP.mult), None)
                vch(lambda e, j4=j4: e.tensor_tensor(
                    out=w2q_m[:, j4 * 1024:(j4 + 1) * 1024], in0=wqa, in1=wqb,
                    op=OP.subtract), s_v)
                w2cast.append(prog.cur(s_v))
            prog.wait("sync", s_v, w2cast[-1])
            if len(w2_tp) >= 2:
                prog.wait("sync", *w2_tp[-2])
            half, r0_ = mt // 4, (mt % 4) * 128
            w2_tp.append(transpose(w2qt[half][:, :, r0_:r0_ + 128], w2q_m[:]))

        prog.wait("vector", s_sty, prog.cur(s_sty))
        prog.wait("vector", s_stx, prog.cur(s_stx))
        prog.wait("vector", s_sc, prog.cur(s_sc))
        for cc in range(NCC):
            par = cc % 2
            buf = wld[:, par, :][:, 0:1024]
            if cc >= 2:
                prog.wait("sync", s_v, mm_ev.get(("w1c", cc - 2)))
            dma(buf, d_w1[cc * 128:(cc + 1) * 128, :], sL[par])
            prog.wait("vector", sL[par], prog.cur(sL[par]))
            if cc == 0:
                prog.wait("vector", s_v, wthr_ready)
            vch(lambda e, buf=buf: e.tensor_scalar(
                out=wqa, in0=buf, scalar1=wthr[:, 0:1], scalar2=wthr[:, 2:3],
                op0=OP.is_ge, op1=OP.mult), None)
            vch(lambda e, buf=buf: e.tensor_scalar(
                out=wqb, in0=buf, scalar1=wthr[:, 1:2], scalar2=wthr[:, 2:3],
                op0=OP.is_le, op1=OP.mult), None)
            if w1q_tp[par] is not None:
                prog.wait("vector", *w1q_tp[par])
            vch(lambda e, par=par: e.tensor_tensor(
                out=w1q_cc[:, par, :], in0=wqa, in1=wqb, op=OP.subtract), s_v)
            mm_ev[("w1c", cc)] = prog.cur(s_v)
            prog.wait("sync", s_v, mm_ev[("w1c", cc)])
            if w1q_used[par] is not None:
                prog.wait("sync", s_t, w1q_used[par])
            w1q_tp[par] = transpose(
                w1t_cc[:, par, :, :].rearrange("p c d -> p (c d)").rearrange(
                    "p (m r) -> p m r", r=128), w1q_cc[:, par, :])
            prog.wait("tensor", *w1q_tp[par])
            if hfc_free[par] is not None:
                for sm, tg in hfc_free[par]:
                    prog.wait("scalar", sm, tg)
            for t4 in range(4):
                bk = bank(par * 2 + t4 % 2)
                if t4 >= 2:
                    prog.wait("tensor", s_sc, mm_ev[("hfe", cc, t4 - 2)])
                for dr in range(NDC):
                    mm(bk, w1t_cc[:, par, dr, :], xqf[:, dr, t4 * 512:(t4 + 1) * 512],
                       start=(dr == 0), stop=(dr == NDC - 1),
                       sem=s_t if dr == NDC - 1 else None)
                prog.wait("scalar", s_t, prog.cur(s_t))
                scop(lambda e, par=par, t4=t4, bk=bk: e.copy(
                    out=hfc[:, par, 1 + t4 * 512:1 + (t4 + 1) * 512], in_=bk), s_sc)
                mm_ev[("hfe", cc, t4)] = prog.cur(s_sc)
            w1q_used[par] = prog.cur(s_t)
            prog.wait("vector", s_sc, mm_ev[("hfe", cc, 3)])
            if cc == 0:
                prog.wait("vector", sM[4], cw_t)
                prog.wait("vector", sM[5], cb_t)
            if conv_done_v is not None:
                for smx, tgx in conv_done_v:
                    prog.wait("vector", smx, tgx)
            vch(lambda e, par=par, cc=cc: e.tensor_scalar(
                out=conv_y, in0=hfc[:, par, 1:2049], scalar1=cwb[:, 1, cc:cc + 1],
                scalar2=cwb[:, 3, cc:cc + 1], op0=OP.mult, op1=OP.add), None)
            vch(lambda e, par=par, cc=cc: e.scalar_tensor_tensor(
                out=conv_y, in0=hfc[:, par, 0:2048], scalar=cwb[:, 0, cc:cc + 1],
                in1=conv_y, op0=OP.mult, op1=OP.add), None)
            vch(lambda e, par=par, cc=cc: e.scalar_tensor_tensor(
                out=conv_y, in0=hfc[:, par, 2:2050], scalar=cwb[:, 2, cc:cc + 1],
                in1=conv_y, op0=OP.mult, op1=OP.add), s_v)
            conv_t = prog.cur(s_v)
            prog.wait("scalar", s_v, conv_t)
            scop(lambda e: e.activation(
                out=conv_y, in_=conv_y, func=AF.Silu), s_sc)
            silu_t = prog.cur(s_sc)
            prog.wait("sync", s_sc, silu_t)
            tps = []
            tps.append(transpose(hq[:, :, cc * 128:(cc + 1) * 128],
                                 conv_y[:, 0:512]))
            for q in range(1, NQ):
                hb = q % 2
                if h2st_free[hb] is not None:
                    prog.wait("sync", s_sth, prog.cur(s_sth))
                tq = transpose(h2st[:, hb, 0:512].rearrange("p (m r) -> p m r", r=128),
                               conv_y[:, q * 512:(q + 1) * 512])
                prog.wait("sync", *tq)
                h2st_free[hb] = dma(
                    d_h2[q - 1, :, :, cc * 128:(cc + 1) * 128].rearrange(
                        "i p r -> p i r"),
                    h2st[:, hb, 0:512].rearrange("p (i r) -> p i r", r=128), s_sth)
                tps.append(tq)
            hfc_free[par] = tps
            conv_done_v = tps
            if cc % 4 == 3:
                w2_mtile(cc // 4)

        w2qt_done = w2_tp[-1]
        hq0_done = hfc_free

        if "p3" in dump:
            dd_hf = dbg_out("hfd", [QT * 128, DH], F32)
            for sm_tg in hfc_free:
                if sm_tg:
                    for sm, tg in sm_tg:
                        prog.wait("vector", sm, tg)
            for i in range(QT):
                for hhalf in range(4):
                    vop(lambda e, i=i, hhalf=hhalf: e.tensor_copy(
                        out=ld_a, in_=hq[:, i, hhalf * 1024:(hhalf + 1) * 1024]), s_v)
                    prog.wait("sync", s_v, prog.cur(s_v))
                    dma(dd_hf[i * 128:(i + 1) * 128, hhalf * 1024:(hhalf + 1) * 1024],
                        ld_a, sS[0])
                    prog.wait("vector", sS[0], prog.cur(sS[0]))

        if stage < 4:
            _finish(prog)
            return nc, dbg

        prog.wait("tensor", *w2qt_done)
        prog.wait("vector", *w2qt_done)
        prog.wait("sync", s_sth, prog.cur(s_sth))
        hload = [None, None, None, None]
        y1ld = [None, None]
        sif_tp = [None, None]
        sif_used = [None, None]
        out_prev = None

        for q in range(NQ):
            if q > 0:
                for i in range(QT):
                    prog.wait("sync", s_v, prog.cur(s_v))
                    hload[i] = dma(hq[:, i, :], d_h2[q - 1, i, :, :], sX[i])
            for i in range(QT):
                ti = q * QT + i
                prog.wait("vector", s_sc, prog.cur(s_sc))
                prog.wait("vector", s_sto, prog.cur(s_sto))
                if q > 0:
                    prog.wait("vector", sX[i], hload[i])
                else:
                    for sm_tg in hfc_free:
                        if sm_tg:
                            for sm, tg in sm_tg:
                                prog.wait("vector", sm, tg)
                vch(lambda e, i=i: e.tensor_reduce(
                    out=stat[:, 63:64], in_=hq[:, i, :], axis=AX.X, op=OP.max,
                    apply_absolute_value=True), None)
                vch(lambda e: e.tensor_scalar(
                    out=stat[:, 64:65], in0=stat[:, 63:64], scalar1=1e-5, scalar2=None,
                    op0=OP.max), s_v)
                prog.wait("vector", s_v, prog.cur(s_v))
                vch(lambda e: e.reciprocal(out=stat[:, 65:66], in_=stat[:, 64:65]), None)
                vch(lambda e: e.tensor_scalar(
                    out=stat[:, 66:67], in0=stat[:, 65:66], scalar1=127.0, scalar2=None,
                    op0=OP.mult), None)
                vch(lambda e: e.tensor_scalar(
                    out=stat[:, 67:68], in0=stat[:, 64:65], scalar1=1.0 / 127.0,
                    scalar2=None, op0=OP.mult), s_v)
                prog.wait("vector", s_v, prog.cur(s_v))
                for hf2 in range(2):
                    sl = hq[:, i, hf2 * 2048:(hf2 + 1) * 2048]
                    vch(lambda e, sl=sl: e.tensor_scalar(
                        out=ysc, in0=sl, scalar1=stat[:, 66:67], scalar2=None,
                        op0=OP.mult), None)
                    vch(lambda e: e.tensor_scalar(
                        out=q_i32[:, 0:2048], in0=ysc, scalar1=-0.5, scalar2=None,
                        op0=OP.add), None)
                    vch(lambda e: e.scalar_tensor_tensor(
                        out=usc, in0=q_i32[:, 0:2048], scalar=0.5, in1=ysc,
                        op0=OP.add, op1=OP.is_le), None)
                    vch(lambda e: e.scalar_tensor_tensor(
                        out=u2s, in0=q_i32[:, 0:2048], scalar=1.5, in1=ysc,
                        op0=OP.add, op1=OP.is_le), None)
                    vch(lambda e: e.tensor_tensor(
                        out=usc, in0=usc, in1=u2s, op=OP.add), None)
                    vch(lambda e: e.tensor_tensor(
                        out=usc, in0=usc, in1=q_i32[:, 0:2048], op=OP.add), None)
                    vch(lambda e, sl=sl: e.tensor_scalar(
                        out=sl, in0=usc, scalar1=stat[:, 67:68], scalar2=None,
                        op0=OP.mult), s_v)
                si_done = prog.cur(s_v)
                spar = ti % 2
                if sif_used[spar] is not None:
                    prog.wait("sync", s_t, sif_used[spar])
                prog.wait("sync", s_v, si_done)
                sif_tp[spar] = transpose(
                    sif[:, spar, :, :].rearrange("p c d -> p (c d)").rearrange(
                        "p (m r) -> p m r", r=128), hq[:, i, :])
                ypar = ti % 2
                if ti >= 2:
                    prog.wait("sync", s_v, mm_ev.get(("y3", ti - 2)))
                y1ld[ypar] = dma(y1in[:, ypar, :], d_y1[ti * 128:(ti + 1) * 128, :],
                                 sQ[ypar])
                prog.wait("tensor", *sif_tp[spar])
                for mh in range(2):
                    bk = bank(4 * (ti % 2) + mh)
                    if ("f2e", ti - 1, mh) in mm_ev:
                        pass
                    if ("f2e", ti - 2, mh) in mm_ev:
                        prog.wait("tensor", s_sc, mm_ev[("f2e", ti - 2, mh)])
                    for c2 in range(NCC):
                        mm(bk, sif[:, spar, c2, :], w2qt[mh][:, c2, :],
                           start=(c2 == 0), stop=(c2 == NCC - 1),
                           sem=s_t if c2 == NCC - 1 else None)
                    prog.wait("scalar", s_t, prog.cur(s_t))
                    scop(lambda e, mh=mh, bk=bk: e.copy(
                        out=y2_t[:, mh * 512:(mh + 1) * 512], in_=bk), s_sc)
                    mm_ev[("f2e", ti, mh)] = prog.cur(s_sc)
                sif_used[spar] = prog.cur(s_t)
                prog.wait("vector", s_sc, mm_ev[("f2e", ti, 1)])
                prog.wait("vector", sQ[ypar], y1ld[ypar])
                vch(lambda e, ypar=ypar: e.tensor_tensor(
                    out=ysc[:, 0:1024], in0=y1in[:, ypar, :], in1=y2_t, op=OP.add), s_v)
                mm_ev[("y3", ti)] = prog.cur(s_v)
                vch(lambda e: e.tensor_reduce(
                    out=stat[:, 70:71], in_=ysc[:, 0:1024], axis=AX.X, op=OP.add), None)
                vch(lambda e: e.scalar_tensor_tensor(
                    out=usc[:, 0:1024], in0=ysc[:, 0:1024], scalar=1.0,
                    in1=ysc[:, 0:1024], op0=OP.mult, op1=OP.mult), None)
                vch(lambda e: e.tensor_reduce(
                    out=stat[:, 71:72], in_=usc[:, 0:1024], axis=AX.X, op=OP.add), s_v)
                vch(lambda e: e.tensor_scalar(
                    out=stat[:, 72:73], in0=stat[:, 70:71], scalar1=1.0 / D,
                    scalar2=None, op0=OP.mult), None)
                vch(lambda e: e.scalar_tensor_tensor(
                    out=stat[:, 73:74], in0=stat[:, 72:73], scalar=-1.0,
                    in1=stat[:, 72:73], op0=OP.mult, op1=OP.mult), None)
                vch(lambda e: e.scalar_tensor_tensor(
                    out=stat[:, 74:75], in0=stat[:, 71:72], scalar=1.0 / D,
                    in1=stat[:, 73:74], op0=OP.mult, op1=OP.add), s_v)
                prog.wait("scalar", s_v, prog.cur(s_v))
                vch(lambda e: e.tensor_scalar(out=stat[:, 74:75], in0=stat[:, 74:75],
                                            scalar1=float(EPS), scalar2=None, op0=OP.add))
                prog.wait("scalar", s_v, prog.cur(s_v))
                scop(lambda e: e.activation(out=stat[:, 75:76], in_=stat[:, 74:75],
                                            func=AF.Sqrt), s_sc)
                prog.wait("vector", s_sc, prog.cur(s_sc))
                vch(lambda e: e.reciprocal(out=stat[:, 76:77], in_=stat[:, 75:76]), None)
                vch(lambda e: e.scalar_tensor_tensor(
                    out=stat[:, 77:78], in0=stat[:, 72:73], scalar=-1.0,
                    in1=stat[:, 76:77], op0=OP.mult, op1=OP.mult), s_v)
                prog.wait("scalar", s_v, prog.cur(s_v))
                if out_prev is not None:
                    prog.wait("scalar", s_sto, out_prev)
                scop(lambda e: e.activation(
                    out=usc[:, 0:1024], in_=ysc[:, 0:1024], func=AF.Identity,
                    bias=stat[:, 77:78], scale=stat[:, 76:77]), s_sc)
                prog.wait("sync", s_sc, prog.cur(s_sc))
                out_prev = dma(d_out[ti * 128:(ti + 1) * 128, :], usc[:, 0:1024], s_sto)

        _finish(prog)
        return nc, dbg


def _finish(prog):
    nc = prog.nc
    for nm, h in prog.sems.items():
        if nm in ("s_gld", "s_gl0", "s_gl1"):
            prog.wait("gpsimd", h, prog.cur(h))
        elif nm[0] == "s" and nm[1] in "LQXTMS" or nm.startswith("s_st"):
            prog.wait("sync", h, prog.cur(h))
    with nc.Block() as block0:
        @block0.gpsimd
        def _(g):
            for nm in list(prog.count):
                g.sem_clear(prog.sems[nm])
    with nc.Block() as block:
        @block.sync
        def _(eng):
            for f in prog.ops["sync"]:
                f(eng)

        @block.scalar
        def _(eng):
            for f in prog.ops["scalar"]:
                f(eng)

        @block.vector
        def _(eng):
            for f in prog.ops["vector"]:
                f(eng)

        @block.tensor
        def _(eng):
            for f in prog.ops["tensor"]:
                f(eng)

        @block.gpsimd
        def _(eng):
            for f in prog.ops["gpsimd"]:
                f(eng)


def make_consts():
    import ml_dtypes
    iu = np.triu(np.ones((128, 128), np.float32), 1)
    il = np.tril(np.ones((128, 128), np.float32), -1)
    ui = np.triu(np.ones((128, 128), np.float32), 0)
    ident = np.eye(128, dtype=np.float32)
    base = np.concatenate([iu, il, ui, ident], 1)
    rep = np.concatenate([np.tile(m, (1, 4)) for m in (iu, il, ui, ident)], 1)
    return np.concatenate([base, rep], 1).astype(ml_dtypes.bfloat16)


def in_map_for(inputs, b):
    return {
        "x": np.ascontiguousarray(np.asarray(inputs["L0_attn_x"])[b]),
        "wq": np.asarray(inputs["L0_attn_Wq"]), "wk": np.asarray(inputs["L0_attn_Wk"]),
        "wv": np.asarray(inputs["L0_attn_Wv"]), "wb": np.asarray(inputs["L0_attn_Wb"]),
        "lnw": np.asarray(inputs["L0_attn_ln_w"]).reshape(1, -1),
        "lnb": np.asarray(inputs["L0_attn_ln_b"]).reshape(1, -1),
        "w1": np.asarray(inputs["L0_ffn_w1"]),
        "cw": np.asarray(inputs["L0_ffn_conv_w"]).reshape(DH, 3),
        "cb": np.asarray(inputs["L0_ffn_conv_b"]).reshape(1, DH),
        "w2": np.asarray(inputs["L0_ffn_w2"]),
        "masks": make_consts(),
    }


def kernel(**inputs):
    from concourse.bass_utils import run_bass_kernel_spmd
    nc, _ = build(stage=99)
    in_maps = [in_map_for(inputs, b) for b in range(8)]
    res = run_bass_kernel_spmd(nc, in_maps, core_ids=list(range(8)))
    out = np.stack([np.asarray(res.results[b]["out"]) for b in range(8)], 0)
    return out.astype(np.float32, copy=False)


# version 03 (final)
# speedup vs baseline: 1.1559x; 1.0638x over previous
import numpy as np
from contextlib import ExitStack

import concourse.bass as bass
import concourse.mybir as mybir

F32 = mybir.dt.float32
BF16 = mybir.dt.bfloat16
I32 = mybir.dt.int32
AF = mybir.ActivationFunctionType
OP = mybir.AluOpType
AX = mybir.AxisListType

T, D, H, HD, DH = 2048, 1024, 8, 128, 4096
NT = T // 128
NDC = D // 128
NCC = DH // 128
NQ = 4
QT = NT // NQ
EPS = 1e-5

ENGINES = ("sync", "scalar", "vector", "tensor", "gpsimd")


class Prog:
    def __init__(self, nc):
        self.nc = nc
        self.ops = {e: [] for e in ENGINES}
        self.count = {}
        self.sems = {}

    def sem(self, stack, name):
        h = stack.enter_context(self.nc.semaphore(name))
        self.count[name] = 0
        self.sems[name] = h
        return h

    def emit(self, eng, fn, sem=None, n=1):
        if sem is None:
            self.ops[eng].append(fn)
            return None
        self.count[sem.name] += n
        tgt = self.count[sem.name]
        self.ops[eng].append(lambda e, fn=fn, sem=sem, n=n: fn(e).then_inc(sem, n))
        return tgt

    def wait(self, eng, sem, val):
        if val is None or val <= 0:
            return
        self.ops[eng].append(lambda e, sem=sem, val=val: e.wait_ge(sem, val))

    def cur(self, sem):
        return self.count[sem.name]


def build(stage=99, dump=()):
    nc = bass.Bass("TRN2", target_bir_lowering=False, debug=False)
    dump = set(dump)

    d_x = nc.dram_tensor("x", [T, D], F32, kind="ExternalInput").ap()
    d_wq = nc.dram_tensor("wq", [D, D], F32, kind="ExternalInput").ap()
    d_wk = nc.dram_tensor("wk", [D, D], F32, kind="ExternalInput").ap()
    d_wv = nc.dram_tensor("wv", [D, D], F32, kind="ExternalInput").ap()
    d_wb = nc.dram_tensor("wb", [D, H], F32, kind="ExternalInput").ap()
    d_lnw = nc.dram_tensor("lnw", [1, D], F32, kind="ExternalInput").ap()
    d_lnb = nc.dram_tensor("lnb", [1, D], F32, kind="ExternalInput").ap()
    d_w1 = nc.dram_tensor("w1", [DH, D], F32, kind="ExternalInput").ap()
    d_cw = nc.dram_tensor("cw", [DH, 3], F32, kind="ExternalInput").ap()
    d_cb = nc.dram_tensor("cb", [1, DH], F32, kind="ExternalInput").ap()
    d_w2 = nc.dram_tensor("w2", [D, DH], F32, kind="ExternalInput").ap()
    d_masks = nc.dram_tensor("masks", [128, 2560], BF16, kind="ExternalInput").ap()
    d_out = nc.dram_tensor("out", [T, D], F32, kind="ExternalOutput").ap()
    d_qf = nc.dram_tensor("qf_s", [NT, 128, H * 128], BF16).ap()
    d_xq = nc.dram_tensor("xq_s", [NT, 128, D], BF16).ap()
    d_y1 = nc.dram_tensor("y1_s", [T, D], F32).ap()
    d_h2 = nc.dram_tensor("h2_s", [NQ - 1, QT, 128, DH], BF16).ap()

    dbg = {}

    def dbg_out(name, shape, dtype=F32):
        dbg[name] = nc.dram_tensor("dbg_" + name, list(shape), dtype,
                                   kind="ExternalOutput").ap()
        return dbg[name]

    prog = Prog(nc)
    stk = ExitStack()
    with stk:
        e = stk.enter_context

        s = [e(nc.sbuf_tensor(f"s{i}", [128, 16384], BF16)) for i in range(5)]
        fsl = e(nc.sbuf_tensor("fsl", [128, 8192], F32))
        q_i32 = e(nc.sbuf_tensor("q_i32", [128, 2048], I32))

        xt2 = s[2][:].rearrange("p (c i d) -> p c i d", c=NDC, i=NT)
        xf = s[1][:].rearrange("p (c t) -> p c t", c=NDC)
        Ktn = s[2][:].rearrange("p (h i d) -> p h i d", h=H, i=NT)
        Vpp = s[4][:].rearrange("p (h i d) -> p h i d", h=H, i=NT)
        Kf = s[0][:].rearrange("p (h t) -> p h t", h=H)
        wgt = s[3][:, 0:8192].rearrange("p (c d) -> p c d", c=NDC)
        wgtb = s[3][:, 8192:8256].rearrange("p (c h) -> p c h", c=NDC)
        beta_fm = s[3][0:16, 9280:11328]
        beta_bf = s[3][:, 11328:11584].rearrange("p (i h) -> p i h", i=NT)
        pw = s[3][:, 0:5120].rearrange("p (q l v) -> p q l v", q=4, l=5)
        onesb = s[3][:, 5120:5248]
        nlp = s[3][:, 5632:7168].rearrange("p (q v) -> p q v", q=4)
        fch = s[3][:, 7168:7680].rearrange("p (q v) -> p q v", q=4)
        ech = s[3][:, 7680:8192].rearrange("p (q v) -> p q v", q=4)
        bbuf = s[3][:, 8192:8704].rearrange("p (q v) -> p q v", q=4)
        rbuf = s[3][:, 8704:9216].rearrange("p (q v) -> p q v", q=4)
        kf2c = s[3][:, 9216:9728].rearrange("p (q v) -> p q v", q=4)
        S_bf = s[3][:, 9728:10752].rearrange("p (h v) -> p h v", h=H)
        qf_ch = s[3][:, 10752:12800].rearrange("p (b v) -> p b v", b=2)
        masks = s[3][:, 12800:13312]
        lnwb = s[3][:, 13312:15360]
        xq_st = s[3][:, 15360:16384]
        wmld = s[1][:].rearrange("p (b v) -> p b v", b=2)
        conv_y = s[3][:, 0:2048]
        w1t_cc = s[3][:, 2048:4096].rearrange("p (b c d) -> p b c d", b=2, c=NDC)
        hfc = s[3][:, 4096:8296].rearrange("p (b v) -> p b v", b=2)
        h2st = s[3][:, 8320:9856].rearrange("p (b v) -> p b v", b=3)
        w2q_m = s[3][:, 9856:13952]
        w1q_cc = s[3][:, 13952:16000].rearrange("p (b v) -> p b v", b=2)
        xq_ld = s[3][:, 15360:16384]
        sif = s[3][:, 0:8192].rearrange("p (b c d) -> p b c d", b=2, c=NCC)
        xqf = s[1][:].rearrange("p (c t) -> p c t", c=NDC)
        hq = s[4][:].rearrange("p (i v) -> p i v", i=QT)
        w2qt = [s[0][:].rearrange("p (c v) -> p c v", c=NCC),
                s[2][:].rearrange("p (c v) -> p c v", c=NCC)]
        ld_a = fsl[:, 0:1024]
        ld_b = fsl[:, 1024:2048]
        kt_f32 = fsl[:, 2048:4096].rearrange("p (b v) -> p b v", b=2)
        v_f32 = fsl[:, 4096:6144].rearrange("p (b v) -> p b v", b=2)
        S_f32 = fsl[:, 6144:7168].rearrange("p (h v) -> p h v", h=8)
        stat = fsl[:, 7168:7296]
        beta_t = fsl[:, 7296:7552].rearrange("p (i h) -> p i h", i=NT)
        junk = fsl[:, 7568:7696]
        wm_acc = fsl[:, 7696:7760]
        cwb = fsl[:, 7760:7888].rearrange("p (j c) -> p j c", c=32)
        wthr = fsl[:, 7888:7904]
        q_sc = fsl[:, 0:2048]
        o_t = fsl[:, 2048:3072]
        y1_t = fsl[:, 3072:4096]
        x_re = fsl[:, 4096:6144].rearrange("p (b v) -> p b v", b=2)
        wld = fsl[:, 0:2048].rearrange("p (b v) -> p b v", b=2)
        wqa = fsl[:, 2048:3072]
        wqb = fsl[:, 3072:4096]
        ysc = fsl[:, 0:2048]
        usc = fsl[:, 2048:4096]
        u2s = fsl[:, 4096:6144]
        y1in = fsl[:, 4096:6144].rearrange("p (b v) -> p b v", b=2)
        y2_t = fsl[:, 6144:7168]

        psP = [e(nc.psum_tensor("psP0", [128, 2048], F32)),
               e(nc.psum_tensor("psP1", [128, 2048], F32))]

        def bank(i):
            return psP[i // 4][:, (i % 4) * 512:(i % 4 + 1) * 512]

        def R(par, lo, hi):
            return psP[par][:, lo:hi]

        s_v = prog.sem(stk, "s_v")
        s_sc = prog.sem(stk, "s_sc")
        s_t = prog.sem(stk, "s_t")
        s_g = prog.sem(stk, "s_g")
        sL = [prog.sem(stk, f"sL{i}") for i in range(2)]
        sQ = [prog.sem(stk, f"sQ{i}") for i in range(2)]
        sX = [prog.sem(stk, f"sX{i}") for i in range(4)]
        sT = [prog.sem(stk, f"sT{i}") for i in range(4)]
        sM = [prog.sem(stk, f"sM{i}") for i in range(7)]
        sS = [prog.sem(stk, f"sS{i}") for i in range(2)]
        s_sty = prog.sem(stk, "s_sty")
        s_stx = prog.sem(stk, "s_stx")
        s_stq = prog.sem(stk, "s_stq")
        s_sth = prog.sem(stk, "s_sth")
        s_sto = prog.sem(stk, "s_sto")
        s_gld = prog.sem(stk, "s_gld")
        s_gl2 = [prog.sem(stk, "s_gl0"), prog.sem(stk, "s_gl1")]

        tpi = [0]

        def dma(dst, src, sem, slow=False):
            return prog.emit("sync", lambda e, dst=dst, src=src, sl=slow: e.dma_start(
                out=dst, in_=src, allow_slow_non_contiguous=sl), sem, 16)

        def gdma(dst, src, sem):
            prog.wait("gpsimd", sem, prog.cur(sem))
            return prog.emit("gpsimd", lambda e, dst=dst, src=src: e.dma_start(
                out=dst, in_=src), sem, 16)

        def transpose(dst, src):
            sm = sT[tpi[0] % 4]
            tpi[0] += 1
            prog.wait("sync", sm, prog.cur(sm))
            tgt = prog.emit("sync", lambda e, dst=dst, src=src: e.dma_start_transpose(
                out=dst, in_=src), sm, 16)
            return (sm, tgt)

        def mm(out_ap, lhsT, rhs, start=True, stop=True, sem=None):
            return prog.emit("tensor", lambda e, o=out_ap, l=lhsT, r=rhs,
                             st=start, sp=stop: e.matmul(out=o, lhsT=l, rhs=r,
                                                         start=st, stop=sp), sem)

        def vop(fn, sem=None):
            return prog.emit("vector", fn, sem)

        def scop(fn, sem=None):
            return prog.emit("scalar", fn, sem)

        def vch(fn, sem=None):
            prog.wait("vector", s_v, prog.cur(s_v))
            return prog.emit("vector", fn, s_v)

        def gch(fn, sem=None):
            prog.wait("gpsimd", s_g, prog.cur(s_g))
            return prog.emit("gpsimd", fn, s_g)

        dma(masks, d_masks[:, 0:512], sM[0])
        cast_done = []
        for i in range(NT):
            par = i % 2
            buf = (ld_a, ld_b)[par]
            if i >= 2:
                prog.wait("sync", s_v, cast_done[i - 2])
            dma(buf, d_x[i * 128:(i + 1) * 128, :], sL[par])
            prog.wait("vector", sL[par], prog.cur(sL[par]))
            vop(lambda e, buf=buf, i=i: e.tensor_copy(
                out=xt2[:, :, i, :], in_=buf.rearrange("p (c d) -> p c d", d=128)), s_v)
            cast_done.append(prog.cur(s_v))
        prog.wait("sync", s_v, cast_done[-1])
        tp_xf = transpose(xf.rearrange("p c (i r) -> p (c i) r", r=128), s[2][:])

        if "xf" in dump:
            dd = dbg_out("xf", [D, T], F32)
            prog.wait("vector", *tp_xf)
            for i in range(2 * NDC):
                vop(lambda e, i=i: e.tensor_copy(
                    out=ld_a, in_=xf[:, i // 2, (i % 2) * 1024:(i % 2 + 1) * 1024]), s_v)
                prog.wait("sync", s_v, prog.cur(s_v))
                dma(dd[(i // 2) * 128:(i // 2 + 1) * 128,
                       (i % 2) * 1024:(i % 2 + 1) * 1024], ld_a, sS[0])
                prog.wait("vector", sS[0], prog.cur(sS[0]))
        if stage < 1:
            _finish(prog)
            return nc, dbg

        def load_weight_bf16(dsrc, prev_user_done=None):
            done = []
            entry_v = prog.cur(s_v)
            for dr in range(NDC):
                par = dr % 2
                buf = (ld_a, ld_b)[par]
                if dr < 2:
                    prog.wait("sync", s_v, entry_v)
                else:
                    prog.wait("sync", s_v, done[dr - 2])
                dma(buf, dsrc[dr * 128:(dr + 1) * 128, :], sL[par])
                prog.wait("vector", sL[par], prog.cur(sL[par]))
                if dr == 0 and prev_user_done is not None:
                    prog.wait("vector", s_t, prev_user_done)
                vop(lambda e, buf=buf, dr=dr: e.tensor_copy(out=wgt[:, dr, :], in_=buf), s_v)
                done.append(prog.cur(s_v))
            return done[-1]

        prog.wait("sync", s_v, cast_done[-1])
        wb_ld = dma(ld_a[:, 0:H * NDC].rearrange("p (c h) -> p c h", c=NDC),
                    d_wb.rearrange("(c p) h -> p c h", p=128), sM[1])
        prog.wait("vector", sM[1], wb_ld)
        vop(lambda e: e.tensor_copy(
            out=wgtb[:, :, :],
            in_=ld_a[:, 0:H * NDC].rearrange("p (c h) -> p c h", c=NDC)), s_v)
        vop(lambda e: e.memset(beta_fm, 0.0), s_v)
        wbc = prog.cur(s_v)
        prog.wait("tensor", s_v, wbc)
        prog.wait("tensor", *tp_xf)
        for t4 in range(4):
            for dr in range(NDC):
                mm(bank(t4)[0:H, :], wgtb[:, dr, :], xf[:, dr, t4 * 512:(t4 + 1) * 512],
                   start=(dr == 0), stop=(dr == NDC - 1),
                   sem=s_t if dr == NDC - 1 else None)
            prog.wait("scalar", s_t, prog.cur(s_t))
            scop(lambda e, t4=t4: e.activation(
                out=beta_fm[0:H, t4 * 512:(t4 + 1) * 512], in_=bank(t4)[0:H, :],
                func=AF.Sigmoid), s_sc)
        prog.wait("sync", s_sc, prog.cur(s_sc))
        tp_b = transpose(beta_bf.rearrange("p i h -> p (i h)").rearrange(
            "p (m r) -> p m r", r=16), beta_fm)
        prog.wait("vector", *tp_b)
        vop(lambda e: e.tensor_copy(
            out=beta_t[:, :, :].rearrange("p i h -> p (i h)"),
            in_=beta_bf[:, :, :].rearrange("p i h -> p (i h)")), s_v)
        beta_ready = prog.cur(s_v)

        wq_done = load_weight_bf16(d_wq, prev_user_done=prog.cur(s_t))
        prog.wait("tensor", s_v, wq_done)
        ev_tgt = {}
        st_par = {}
        for t4 in range(4):
            for h in range(H):
                bk = bank(4 + (h % 4))
                prev = (h - 4, t4) if h >= 4 else (h + 4, t4 - 1)
                if prev in ev_tgt:
                    prog.wait("tensor", s_v, ev_tgt[prev])
                for dr in range(NDC):
                    mm(bk, wgt[:, dr, h * 128:(h + 1) * 128],
                       xf[:, dr, t4 * 512:(t4 + 1) * 512],
                       start=(dr == 0), stop=(dr == NDC - 1),
                       sem=s_t if dr == NDC - 1 else None)
                prog.wait("vector", s_t, prog.cur(s_t))
                par = (t4 * H + h) % 2
                if st_par:
                    prog.wait("vector", s_stq, prog.cur(s_stq))
                vop(lambda e, bk=bk, par=par: e.tensor_copy(
                    out=qf_ch[:, par, 0:512], in_=bk), s_v)
                ev_tgt[(h, t4)] = prog.cur(s_v)
                prog.wait("sync", s_v, ev_tgt[(h, t4)])
                for j in range(4):
                    st_par[par] = dma(d_qf[4 * t4 + j, :, h * 128:(h + 1) * 128],
                                      qf_ch[:, par, j * 128:(j + 1) * 128], s_stq)

        wk_done = load_weight_bf16(d_wk, prev_user_done=prog.cur(s_t))
        prog.wait("tensor", s_v, wk_done)
        kv_ev = []
        for ti in range(NT):
            kb = ti % 2
            if ti >= 2:
                prog.wait("tensor", s_v, kv_ev[ti - 2])
            for half in range(2):
                bk = bank(kb * 2 + half)
                for dr in range(NDC):
                    mm(bk, xf[:, dr, ti * 128:(ti + 1) * 128],
                       wgt[:, dr, half * 512:(half + 1) * 512],
                       start=(dr == 0), stop=(dr == NDC - 1),
                       sem=s_t if dr == NDC - 1 else None)
                prog.wait("vector", s_t, prog.cur(s_t))
                vop(lambda e, kb=kb, half=half, bk=bk: e.tensor_copy(
                    out=kt_f32[:, kb, half * 512:(half + 1) * 512], in_=bk), s_v)
            vch(lambda e, kb=kb: e.scalar_tensor_tensor(
                out=v_f32[:, kb, :], in0=kt_f32[:, kb, :], scalar=1.0,
                in1=kt_f32[:, kb, :], op0=OP.mult, op1=OP.mult), None)
            vch(lambda e, kb=kb: e.tensor_reduce(
                out=stat[:, 0:8].rearrange("p (h o) -> p h o", o=1),
                in_=v_f32[:, kb, :].rearrange("p (h d) -> p h d", d=128),
                axis=AX.X, op=OP.add), s_v)
            prog.wait("scalar", s_v, prog.cur(s_v))
            scop(lambda e: e.activation(out=stat[:, 8:16], in_=stat[:, 0:8],
                                        func=AF.Sqrt), s_sc)
            prog.wait("vector", s_sc, prog.cur(s_sc))
            vch(lambda e: e.tensor_scalar(out=stat[:, 16:24], in0=stat[:, 8:16],
                                          scalar1=1e-12, scalar2=None, op0=OP.max), None)
            vch(lambda e: e.reciprocal(out=stat[:, 24:32], in_=stat[:, 16:24]), s_v)
            prog.wait("vector", s_v, prog.cur(s_v))
            prog.wait("vector", s_v, beta_ready)
            for h in range(H):
                vch(lambda e, kb=kb, ti=ti, h=h: e.tensor_scalar(
                    out=Ktn[:, h, ti, :], in0=kt_f32[:, kb, h * 128:(h + 1) * 128],
                    scalar1=stat[:, 24 + h:25 + h], scalar2=None, op0=OP.mult), s_v)
            kv_ev.append(prog.cur(s_v))

        wv_done = load_weight_bf16(d_wv, prev_user_done=prog.cur(s_t))
        prog.wait("tensor", s_v, wv_done)
        v_ev = []
        for ti in range(NT):
            kb = ti % 2
            if ti >= 2:
                prog.wait("tensor", s_v, v_ev[ti - 2])
            for half in range(2):
                bk = bank(kb * 2 + half)
                for dr in range(NDC):
                    mm(bk, xf[:, dr, ti * 128:(ti + 1) * 128],
                       wgt[:, dr, half * 512:(half + 1) * 512],
                       start=(dr == 0), stop=(dr == NDC - 1),
                       sem=s_t if dr == NDC - 1 else None)
                prog.wait("scalar", s_t, prog.cur(s_t))
                scop(lambda e, kb=kb, half=half, bk=bk: e.activation(
                    out=v_f32[:, kb, half * 512:(half + 1) * 512], in_=bk,
                    func=AF.Silu), s_sc)
            prog.wait("vector", s_sc, prog.cur(s_sc))
            prog.wait("vector", s_v, beta_ready)
            for h in range(H):
                vop(lambda e, kb=kb, ti=ti, h=h: e.tensor_scalar(
                    out=Vpp[:, h, ti, :], in0=v_f32[:, kb, h * 128:(h + 1) * 128],
                    scalar1=beta_t[:, ti, h:h + 1], scalar2=None, op0=OP.mult), s_v)
            v_ev.append(prog.cur(s_v))
        vpp_done = prog.cur(s_v)

        prog.wait("sync", s_v, kv_ev[-1])
        tp_kf = transpose(Kf.rearrange("p h (i r) -> p (h i) r", r=128), s[2][:])

        if "proj" in dump:
            dd_kf = dbg_out("kf", [H * 128, T], F32)
            dd_vpp = dbg_out("vpp", [T, D], F32)
            dd_ktn = dbg_out("ktn", [T, D], F32)
            dd_beta = dbg_out("beta", [128, NT * 16], F32)
            dd_qf = dbg_out("qf", [NT, 128, H * 128], F32)
            prog.wait("vector", *tp_kf)
            prog.wait("vector", s_v, vpp_done)
            for i in range(2 * H):
                vop(lambda e, i=i: e.tensor_copy(
                    out=ld_a, in_=Kf[:, i // 2, (i % 2) * 1024:(i % 2 + 1) * 1024]), s_v)
                prog.wait("sync", s_v, prog.cur(s_v))
                dma(dd_kf[(i // 2) * 128:(i // 2 + 1) * 128,
                          (i % 2) * 1024:(i % 2 + 1) * 1024], ld_a, sS[0])
                prog.wait("vector", sS[0], prog.cur(sS[0]))
            for ti in range(NT):
                vop(lambda e, ti=ti: e.tensor_copy(
                    out=ld_a.rearrange("p (h d) -> p h d", h=H), in_=Vpp[:, :, ti, :]), s_v)
                prog.wait("sync", s_v, prog.cur(s_v))
                dma(dd_vpp[ti * 128:(ti + 1) * 128, :], ld_a, sS[0])
                prog.wait("vector", sS[0], prog.cur(sS[0]))
                vop(lambda e, ti=ti: e.tensor_copy(
                    out=ld_a.rearrange("p (h d) -> p h d", h=H), in_=Ktn[:, :, ti, :]), s_v)
                prog.wait("sync", s_v, prog.cur(s_v))
                dma(dd_ktn[ti * 128:(ti + 1) * 128, :], ld_a, sS[0])
                prog.wait("vector", sS[0], prog.cur(sS[0]))
            vop(lambda e: e.tensor_copy(
                out=ld_a[:, 0:NT * 16],
                in_=beta_t[:, :, :].rearrange("p i h -> p (i h)")), s_v)
            prog.wait("sync", s_v, prog.cur(s_v))
            dma(dd_beta[:, :], ld_a[:, 0:NT * 16], sS[0])
            prog.wait("vector", sS[0], prog.cur(sS[0]))
            prog.wait("sync", s_stq, prog.cur(s_stq))
            for c in range(NT):
                dma(qf_ch[:, 0, :], d_qf[c, :, :], sQ[0])
                prog.wait("vector", sQ[0], prog.cur(sQ[0]))
                vop(lambda e: e.tensor_copy(out=ld_a, in_=qf_ch[:, 0, :]), s_v)
                prog.wait("sync", s_v, prog.cur(s_v))
                dma(dd_qf[c, :, :], ld_a, sS[0])
                prog.wait("vector", sS[0], prog.cur(sS[0]))

        if stage < 2:
            _finish(prog)
            return nc, dbg

        vop(lambda e: e.memset(S_f32[:, :, :].rearrange("p h v -> p (h v)"), 0.0), None)
        vop(lambda e: e.memset(S_bf[:, :, :].rearrange("p h v -> p (h v)"), 0.0), None)
        vop(lambda e: e.memset(onesb, 1.0), s_v)
        init_done = prog.cur(s_v)

        prog.wait("sync", s_v, prog.cur(s_v))
        ln_lda = dma(ld_a[0:1, 0:1024], d_lnw, sM[2])
        ln_ldb = dma(ld_b[0:1, 0:1024], d_lnb, sM[3])
        prog.wait("vector", sM[2], ln_lda)
        prog.wait("vector", sM[3], ln_ldb)
        vop(lambda e: e.tensor_copy(out=xq_st[0:1, 0:1024], in_=ld_a[0:1, 0:1024]), s_v)
        prog.wait("tensor", s_v, prog.cur(s_v))
        prog.wait("tensor", s_v, init_done)
        for q in range(2):
            mm(bank(6 + q), onesb[0:1, :], xq_st[0:1, q * 512:(q + 1) * 512], sem=s_t)
            prog.wait("vector", s_t, prog.cur(s_t))
            vop(lambda e, q=q: e.tensor_copy(
                out=lnwb[:, q * 512:(q + 1) * 512], in_=bank(6 + q)), s_v)
        prog.wait("vector", s_v, prog.cur(s_v))
        vop(lambda e: e.tensor_copy(out=xq_st[0:1, 0:1024], in_=ld_b[0:1, 0:1024]), s_v)
        prog.wait("tensor", s_v, prog.cur(s_v))
        for q in range(2):
            mm(bank(6 + q), onesb[0:1, :], xq_st[0:1, q * 512:(q + 1) * 512], sem=s_t)
            prog.wait("vector", s_t, prog.cur(s_t))
            vop(lambda e, q=q: e.tensor_copy(
                out=lnwb[:, 1024 + q * 512:1024 + (q + 1) * 512], in_=bank(6 + q)), s_v)
        lnwb_ready = prog.cur(s_v)

        prog.wait("gpsimd", s_t, prog.cur(s_t))
        t_after_p1 = prog.cur(s_t)
        wm_part = []
        for cc in range(NCC):
            par = cc % 2
            sl = wmld[:, par, 0:1024]
            absj = wmld[:, par, 5120:6144]
            if cc >= 2:
                prog.wait("gpsimd", s_sc, wm_part[cc - 2])
            gdma(sl, d_w1[cc * 128:(cc + 1) * 128, :], s_gl2[par])
            prog.wait("scalar", s_gl2[par], prog.cur(s_gl2[par]))
            scop(lambda e, sl=sl, absj=absj, cc=cc: e.activation(
                out=absj, in_=sl, func=AF.Abs,
                accum_out=wm_acc[:, cc:cc + 1]), s_sc)
            wm_part.append(prog.cur(s_sc))
        for mt in range(8):
            par = mt % 2
            sl = wmld[:, par, 1024:5120]
            absj = wmld[:, par, 5120:8192][:, 0:3072]
            prog.wait("gpsimd", s_sc, wm_part[NCC + mt - 2])
            gdma(sl, d_w2.rearrange("(a p) k -> p a k", p=128)[:, mt, :], s_gl2[par])
            prog.wait("scalar", s_gl2[par], prog.cur(s_gl2[par]))
            scop(lambda e, sl=sl, mt=mt, par=par: e.activation(
                out=wmld[:, par, 1024:5120], in_=sl, func=AF.Abs,
                accum_out=wm_acc[:, 32 + mt:33 + mt]), s_sc)
            wm_part.append(prog.cur(s_sc))
        wm_scal = prog.cur(s_sc)
        wm_done = None

        prog.wait("sync", s_t, t_after_p1)
        m4_tgt = dma(s[1][:, 14336:16384], d_masks[:, 512:2560], sM[6])
        prog.wait("vector", sM[6], m4_tgt)

        prog.wait("tensor", *tp_kf)
        prog.wait("tensor", s_v, lnwb_ready)
        prog.wait("tensor", s_sc, prog.cur(s_sc))
        prog.wait("vector", sM[0], 16)

        qld = [None, None]
        qfree = [None, None]
        xre_t = [None, None]
        xre_used = [None, None]
        y1st_prev = None
        xqst_prev = None
        ostats_prev = None
        par_free = [None, None]
        par_free_sc = [None, None]

        prog.wait("sync", s_stq, prog.cur(s_stq))
        qld[0] = dma(qf_ch[:, 0, :], d_qf[0, :, :], sQ[0])
        xre_t[0] = dma(x_re[:, 0, :], d_x[0:128, :], sX[0])

        NT_RUN = NT
        if 20 <= stage < 30:
            NT_RUN = {21: 0, 22: 1, 23: 2, 24: 4, 25: 8}.get(stage, 1)
        if 220 <= stage <= 234:
            NT_RUN = 1

        bbufG = [bbuf, s[1][:, 6144:6656].rearrange("p (q v) -> p q v", q=4)]
        kf2cG = [kf2c, s[1][:, 6656:7168].rearrange("p (q v) -> p q v", q=4)]
        echG = [ech, s[1][:, 7168:7680].rearrange("p (q v) -> p q v", q=4)]
        ptG = [nlp[:, :, 256:384],
               s[1][:, 7680:8192].rearrange("p (q v) -> p q v", q=4)]
        eT = {}

        def co(gens):
            gens = [gn for gn in gens if gn is not None]
            while gens:
                keep = []
                for gn in gens:
                    try:
                        next(gn)
                        keep.append(gn)
                    except StopIteration:
                        pass
                gens = keep

        m4U = s[1][:, 14336:14848].rearrange("p (q v) -> p q v", q=4)
        m4L = s[1][:, 14848:15360].rearrange("p (q v) -> p q v", q=4)
        m4P = s[1][:, 15360:15872].rearrange("p (q v) -> p q v", q=4)
        m4I = s[1][:, 15872:16384]

        def gen_ABC(j):
            c, g = divmod(j, 2)
            par = g
            cb = c % 2
            heads = list(range(4 * g, 4 * g + 4))
            bbf, kfc, ptv = bbufG[g], kf2cG[g], ptG[g]
            kfc_f = kfc.rearrange("p q v -> p (q v)")
            ec_f = echG[g].rearrange("p q v -> p (q v)")
            fch_f = fch.rearrange("p q v -> p (q v)")
            psa = psP[par][:, 0:1536].rearrange("p (q v) -> p q v", q=4)
            if par_free[par] is not None:
                prog.wait("tensor", s_v, par_free[par])
                prog.wait("tensor", s_sc, par_free_sc[par])
            prog.wait("tensor", sQ[cb], qld[cb])
            for hs, h in enumerate(heads):
                vch(lambda e, hs=hs, h=h, c=c, bbf=bbf: e.tensor_scalar(
                    out=bbf[:, hs, :], in0=masks[:, 384:512],
                    scalar1=beta_t[:, c, h:h + 1], scalar2=None, op0=OP.mult))
            diag_t = prog.cur(s_v)
            yield
            prog.wait("tensor", s_v, diag_t)
            for hs, h in enumerate(heads):
                mm(R(par, 1536 + hs * 128, 1664 + hs * 128),
                   Ktn[:, h, c, :], bbf[:, hs, :],
                   sem=s_t if hs == 3 else None)
            prog.wait("scalar", s_t, prog.cur(s_t))
            yield
            scop(lambda e, par=par, kfc_f=kfc_f: e.activation(
                out=kfc_f, in_=R(par, 1536, 2048), func=AF.Identity), s_sc)
            kf2_t = prog.cur(s_sc)
            prog.wait("tensor", s_sc, kf2_t)
            yield
            for hs, h in enumerate(heads):
                go = hs * 384
                mm(R(par, go, go + 128), Kf[:, h, c * 128:(c + 1) * 128],
                   kfc[:, hs, :])
                mm(R(par, go + 128, go + 256), kfc[:, hs, :],
                   Kf[:, h, c * 128:(c + 1) * 128])
                mm(R(par, go + 256, go + 384), Kf[:, h, c * 128:(c + 1) * 128],
                   qf_ch[:, cb, h * 128:(h + 1) * 128],
                   sem=s_t if hs == 3 else None)
            prog.wait("vector", s_t, prog.cur(s_t))
            yield
            vop(lambda e, par=par, psa=psa: e.tensor_tensor(
                out=nlp[:, :, 0:128], in0=psa[:, :, 0:128], in1=m4U,
                op=OP.mult), None)
            vop(lambda e, par=par, psa=psa: e.tensor_tensor(
                out=nlp[:, :, 128:256], in0=psa[:, :, 128:256], in1=m4L,
                op=OP.mult), None)
            vop(lambda e, par=par, psa=psa, ptv=ptv: e.tensor_tensor(
                out=ptv, in0=psa[:, :, 256:384], in1=m4P, op=OP.mult), s_v)
            mask_t = prog.cur(s_v)
            yield
            sq_ev = {}
            for l in range(5):
                lo = 1024 * (l % 2)
                if l == 0:
                    prog.wait("tensor", s_v, mask_t)
                else:
                    prog.wait("tensor", s_sc, sq_ev[l - 1][0])
                    prog.wait("tensor", s_v, sq_ev[l - 1][1])
                for hs in range(4):
                    if l == 0:
                        lsrc, nsrc = nlp[:, hs, 128:256], nlp[:, hs, 0:128]
                    else:
                        lsrc = pw[:, hs, l - 1, 0:128]
                        nsrc = pw[:, hs, l - 1, 128:256]
                    mm(R(par, lo + hs * 256, lo + hs * 256 + 128), nsrc, lsrc)
                    mm(R(par, lo + hs * 256 + 128, lo + hs * 256 + 256), lsrc, nsrc,
                       sem=s_t if hs == 3 else None)
                prog.wait("scalar", s_t, prog.cur(s_t))
                prog.wait("vector", s_t, prog.cur(s_t))
                yield
                scop(lambda e, l=l, lo=lo, par=par: e.activation(
                    out=pw[:, 0:2, l, :],
                    in_=R(par, lo, lo + 512).rearrange("p (q v) -> p q v", q=2),
                    func=AF.Identity), s_sc)
                vop(lambda e, l=l, lo=lo, par=par: e.tensor_copy(
                    out=pw[:, 2:4, l, :],
                    in_=R(par, lo + 512, lo + 1024).rearrange(
                        "p (q v) -> p q v", q=2)), s_v)
                sq_ev[l] = (prog.cur(s_sc), prog.cur(s_v))
            prog.wait("tensor", s_sc, sq_ev[4][0])
            prog.wait("tensor", s_v, sq_ev[4][1])
            yield
            for hs in range(4):
                mm(R(par, 1024 + hs * 128, 1152 + hs * 128), pw[:, hs, 4, 0:128],
                   pw[:, hs, 4, 128:256], sem=s_t if hs == 3 else None)
            prog.wait("vector", s_t, prog.cur(s_t))
            yield
            vop(lambda e, par=par, fch_f=fch_f: e.tensor_tensor(
                out=fch_f, in0=R(par, 1024, 1536), in1=m4I, op=OP.add), s_v)
            f_t = prog.cur(s_v)
            yield
            for step, l in enumerate([4, 3, 2, 1, 0]):
                lo = 512 * (step % 2)
                prog.wait("tensor", s_v, f_t)
                for hs in range(4):
                    mm(R(par, lo + hs * 128, lo + hs * 128 + 128), pw[:, hs, l, 0:128],
                       fch[:, hs, :], sem=s_t if hs == 3 else None)
                prog.wait("vector", s_t, prog.cur(s_t))
                yield
                vop(lambda e, lo=lo, par=par, fch_f=fch_f: e.tensor_tensor(
                    out=fch_f, in0=R(par, lo, lo + 512), in1=fch_f,
                    op=OP.add), s_v)
                f_t = prog.cur(s_v)
            prog.wait("tensor", s_v, f_t)
            yield
            for hs in range(4):
                mm(R(par, 1024 + hs * 128, 1152 + hs * 128), nlp[:, hs, 128:256],
                   fch[:, hs, :], sem=s_t if hs == 3 else None)
            prog.wait("vector", s_t, prog.cur(s_t))
            yield
            vop(lambda e, par=par, fch_f=fch_f, ec_f=ec_f: e.tensor_tensor(
                out=ec_f, in0=fch_f, in1=R(par, 1024, 1536),
                op=OP.subtract), s_v)
            eT[j] = prog.cur(s_v)

        def gen_D(j):
            c, g = divmod(j, 2)
            par = g
            cb = c % 2
            heads = list(range(4 * g, 4 * g + 4))
            bbf, kfc, ecv, ptv = bbufG[g], kf2cG[g], echG[g], ptG[g]
            psq = psP[par][:, 0:2048].rearrange("p (q v) -> p q v", q=4)
            Vg = Vpp[:, 4 * g:4 * g + 4, c, :]
            Sf = S_f32[:, 4 * g:4 * g + 4, :]
            Sb = S_bf[:, 4 * g:4 * g + 4, :]
            prog.wait("tensor", s_v, eT[j])
            if ostats_prev is not None:
                prog.wait("scalar", s_v, ostats_prev)
            for hs, h in enumerate(heads):
                b0 = hs * 512
                mm(R(par, b0, b0 + 128), kfc[:, hs, :], S_bf[:, h, :],
                   sem=s_t if hs == 3 else None)
            tB = prog.cur(s_t)
            yield
            prog.wait("vector", s_t, tB)
            vop(lambda e, par=par, psq=psq, bbf=bbf, Vg=Vg: e.tensor_tensor(
                out=bbf, in0=Vg, in1=psq[:, :, 0:128], op=OP.subtract), s_v)
            tv = prog.cur(s_v)
            yield
            prog.wait("tensor", s_v, tv)
            for hs, h in enumerate(heads):
                b0 = hs * 512
                mm(R(par, b0 + 128, b0 + 256), ecv[:, hs, :], bbf[:, hs, :],
                   sem=s_t if hs == 3 else None)
            tR = prog.cur(s_t)
            yield
            prog.wait("scalar", s_t, tR)
            scop(lambda e, par=par, psq=psq: e.activation(
                out=rbuf[:, :, :], in_=psq[:, :, 128:256],
                func=AF.Identity), s_sc)
            tsc = prog.cur(s_sc)
            yield
            prog.wait("tensor", s_sc, tsc)
            for hs, h in enumerate(heads):
                b0 = hs * 512
                mm(R(par, b0 + 384, b0 + 512), qf_ch[:, cb, h * 128:(h + 1) * 128],
                   S_bf[:, h, :], start=True, stop=False)
                mm(R(par, b0 + 384, b0 + 512), ptv[:, hs, :], rbuf[:, hs, :],
                   start=False, stop=True)
                mm(R(par, b0 + 256, b0 + 384), Ktn[:, h, c, :], rbuf[:, hs, :],
                   sem=s_t if hs == 3 else None)
            tD = prog.cur(s_t)
            yield
            prog.wait("vector", s_t, tD)
            vch(lambda e, par=par, psq=psq, Sf=Sf: e.tensor_tensor(
                out=Sf, in0=Sf, in1=psq[:, :, 256:384], op=OP.add))
            vch(lambda e, Sf=Sf, Sb=Sb: e.tensor_copy(out=Sb, in_=Sf))
            tS = prog.cur(s_v)
            yield
            prog.wait("scalar", s_t, tD)
            prog.wait("scalar", s_v, tS)
            scop(lambda e, par=par, psq=psq, g=g: e.activation(
                out=o_t[:, g * 512:(g + 1) * 512].rearrange(
                    "p (q v) -> p q v", q=4),
                in_=psq[:, :, 384:512], func=AF.Identity), s_sc)
            par_free[par] = prog.cur(s_v)
            par_free_sc[par] = prog.cur(s_sc)
            if g == 1:
                qfree[cb] = prog.cur(s_t)

        def emit_T(c):
            nonlocal ostats_prev, y1st_prev, xqst_prev
            cb = c % 2
            o_done = prog.cur(s_sc)
            prog.wait("vector", s_sc, o_done)
            vch(lambda e: e.tensor_reduce(
                out=stat[:, 40:41], in_=o_t, axis=AX.X, op=OP.add), None)
            vch(lambda e: e.scalar_tensor_tensor(
                out=q_sc[:, 0:1024], in0=o_t, scalar=1.0, in1=o_t,
                op0=OP.mult, op1=OP.mult), None)
            vch(lambda e: e.tensor_reduce(
                out=stat[:, 41:42], in_=q_sc[:, 0:1024], axis=AX.X, op=OP.add), s_v)
            ostats_prev = prog.cur(s_v)
            vch(lambda e: e.tensor_scalar(
                out=stat[:, 42:43], in0=stat[:, 40:41], scalar1=1.0 / D, scalar2=None,
                op0=OP.mult), None)
            vch(lambda e: e.scalar_tensor_tensor(
                out=stat[:, 43:44], in0=stat[:, 42:43], scalar=-1.0, in1=stat[:, 42:43],
                op0=OP.mult, op1=OP.mult), None)
            vch(lambda e: e.scalar_tensor_tensor(
                out=stat[:, 44:45], in0=stat[:, 41:42], scalar=1.0 / D, in1=stat[:, 43:44],
                op0=OP.mult, op1=OP.add), s_v)
            prog.wait("scalar", s_v, prog.cur(s_v))
            vch(lambda e: e.tensor_scalar(out=stat[:, 44:45], in0=stat[:, 44:45],
                                        scalar1=float(EPS), scalar2=None, op0=OP.add))
            prog.wait("scalar", s_v, prog.cur(s_v))
            scop(lambda e: e.activation(out=stat[:, 45:46], in_=stat[:, 44:45],
                                        func=AF.Sqrt), s_sc)
            prog.wait("vector", s_sc, prog.cur(s_sc))
            vch(lambda e: e.reciprocal(out=stat[:, 46:47], in_=stat[:, 45:46]), None)
            vch(lambda e: e.scalar_tensor_tensor(
                out=stat[:, 47:48], in0=stat[:, 42:43], scalar=-1.0, in1=stat[:, 46:47],
                op0=OP.mult, op1=OP.mult), s_v)
            ln1_t = prog.cur(s_v)
            prog.wait("scalar", s_v, ln1_t)
            scop(lambda e: e.activation(out=q_sc[:, 0:1024], in_=o_t, func=AF.Identity,
                                        bias=stat[:, 47:48], scale=stat[:, 46:47]), s_sc)
            an_t = prog.cur(s_sc)
            prog.wait("vector", s_v, lnwb_ready)
            prog.wait("vector", sX[cb], xre_t[cb])
            if y1st_prev is not None:
                prog.wait("vector", s_sty, y1st_prev)
            vch(lambda e, cb=cb: e.scalar_tensor_tensor(
                out=y1_t, in0=lnwb[:, 1024:2048], scalar=1.0, in1=x_re[:, cb, :],
                op0=OP.mult, op1=OP.add), s_v)
            xre_used[cb] = prog.cur(s_v)
            prog.wait("vector", s_sc, an_t)
            vch(lambda e: e.tensor_tensor(
                out=q_sc[:, 1024:2048], in0=q_sc[:, 0:1024], in1=lnwb[:, 0:1024],
                op=OP.mult), None)
            vch(lambda e: e.tensor_tensor(
                out=y1_t, in0=y1_t, in1=q_sc[:, 1024:2048], op=OP.add), None)
            vch(lambda e: e.tensor_reduce(
                out=stat[:, 48:49], in_=y1_t, axis=AX.X, op=OP.add), None)
            vch(lambda e: e.scalar_tensor_tensor(
                out=q_sc[:, 0:1024], in0=y1_t, scalar=1.0, in1=y1_t,
                op0=OP.mult, op1=OP.mult), None)
            vch(lambda e: e.tensor_reduce(
                out=stat[:, 49:50], in_=q_sc[:, 0:1024], axis=AX.X, op=OP.add), None)
            vch(lambda e: e.tensor_scalar(
                out=stat[:, 50:51], in0=stat[:, 48:49], scalar1=1.0 / D, scalar2=None,
                op0=OP.mult), None)
            vch(lambda e: e.scalar_tensor_tensor(
                out=stat[:, 51:52], in0=stat[:, 50:51], scalar=-1.0, in1=stat[:, 50:51],
                op0=OP.mult, op1=OP.mult), None)
            vch(lambda e: e.scalar_tensor_tensor(
                out=stat[:, 52:53], in0=stat[:, 49:50], scalar=1.0 / D, in1=stat[:, 51:52],
                op0=OP.mult, op1=OP.add), s_v)
            prog.wait("scalar", s_v, prog.cur(s_v))
            vch(lambda e: e.tensor_scalar(out=stat[:, 52:53], in0=stat[:, 52:53],
                                        scalar1=float(EPS), scalar2=None, op0=OP.add))
            prog.wait("scalar", s_v, prog.cur(s_v))
            scop(lambda e: e.activation(out=stat[:, 53:54], in_=stat[:, 52:53],
                                        func=AF.Sqrt), s_sc)
            prog.wait("vector", s_sc, prog.cur(s_sc))
            vch(lambda e: e.reciprocal(out=stat[:, 54:55], in_=stat[:, 53:54]), None)
            vch(lambda e: e.scalar_tensor_tensor(
                out=stat[:, 55:56], in0=stat[:, 50:51], scalar=-1.0, in1=stat[:, 54:55],
                op0=OP.mult, op1=OP.mult), s_v)
            prog.wait("scalar", s_v, prog.cur(s_v))
            scop(lambda e: e.activation(out=y1_t, in_=y1_t, func=AF.Identity,
                                        bias=stat[:, 55:56], scale=stat[:, 54:55]), s_sc)
            y1_done = prog.cur(s_sc)
            prog.wait("sync", s_sc, y1_done)
            y1st_prev = dma(d_y1[c * 128:(c + 1) * 128, :], y1_t, s_sty)
            prog.wait("vector", s_sc, y1_done)
            vch(lambda e: e.tensor_reduce(
                out=stat[:, 56:57], in_=y1_t, axis=AX.X, op=OP.max,
                apply_absolute_value=True), None)
            vch(lambda e: e.tensor_scalar(
                out=stat[:, 57:58], in0=stat[:, 56:57], scalar1=1e-5, scalar2=None,
                op0=OP.max), s_v)
            prog.wait("vector", s_v, prog.cur(s_v))
            vch(lambda e: e.reciprocal(out=stat[:, 58:59], in_=stat[:, 57:58]), None)
            vch(lambda e: e.tensor_scalar(
                out=stat[:, 59:60], in0=stat[:, 58:59], scalar1=127.0, scalar2=None,
                op0=OP.mult), None)
            vch(lambda e: e.tensor_scalar(
                out=stat[:, 60:61], in0=stat[:, 57:58], scalar1=1.0 / 127.0,
                scalar2=None, op0=OP.mult), s_v)
            prog.wait("vector", s_v, prog.cur(s_v))
            prog.wait("scalar", s_v, prog.cur(s_v))
            scop(lambda e: e.activation(
                out=q_sc[:, 0:1024], in_=y1_t, func=AF.Identity,
                scale=stat[:, 59:60]), s_sc)
            prog.wait("vector", s_sc, prog.cur(s_sc))
            vch(lambda e: e.tensor_scalar(
                out=q_i32[:, 0:1024], in0=q_sc[:, 0:1024], scalar1=-0.5, scalar2=None,
                op0=OP.add), None)
            vch(lambda e: e.scalar_tensor_tensor(
                out=q_sc[:, 1024:2048], in0=q_i32[:, 0:1024], scalar=0.5,
                in1=q_sc[:, 0:1024], op0=OP.add, op1=OP.is_le), None)
            vch(lambda e: e.scalar_tensor_tensor(
                out=q_sc[:, 0:1024], in0=q_i32[:, 0:1024], scalar=1.5,
                in1=q_sc[:, 0:1024], op0=OP.add, op1=OP.is_le), None)
            vch(lambda e: e.tensor_tensor(
                out=q_sc[:, 0:1024], in0=q_sc[:, 0:1024], in1=q_sc[:, 1024:2048],
                op=OP.add), None)
            vch(lambda e: e.tensor_tensor(
                out=q_sc[:, 1024:2048], in0=q_sc[:, 0:1024], in1=q_i32[:, 0:1024],
                op=OP.add), None)
            if xqst_prev is not None:
                prog.wait("scalar", s_stx, xqst_prev)
            prog.wait("scalar", s_v, prog.cur(s_v))
            scop(lambda e: e.activation(
                out=xq_st, in_=q_sc[:, 1024:2048], func=AF.Identity,
                scale=stat[:, 60:61]), s_sc)
            prog.wait("sync", s_sc, prog.cur(s_sc))
            xqst_prev = dma(d_xq[c, :, :], xq_st, s_stx)

        for j in range(2 * NT_RUN + 1):
            c2, g2 = divmod(j, 2)
            co([gen_ABC(j) if j < 2 * NT_RUN else None,
                gen_D(j - 1) if j >= 1 else None])
            if g2 == 0:
                if c2 >= 1:
                    emit_T(c2 - 1)
                if c2 + 1 < NT_RUN:
                    nb = (c2 + 1) % 2
                    if qfree[nb] is not None:
                        prog.wait("sync", s_t, qfree[nb])
                    qld[nb] = dma(qf_ch[:, nb, :], d_qf[c2 + 1, :, :], sQ[nb])
                    if xre_used[nb] is not None:
                        prog.wait("sync", s_v, xre_used[nb])
                    xre_t[nb] = dma(x_re[:, nb, :],
                                    d_x[(c2 + 1) * 128:(c2 + 2) * 128, :], sX[nb])

        if 20 <= stage <= 25 or 220 <= stage <= 234:
            _finish(prog)
            return nc, dbg
        if "p2" in dump:
            dd_y1 = dbg_out("y1d", [T, D], F32)
            dd_xq = dbg_out("xqd", [NT, 128, D], F32)
            prog.wait("sync", s_sty, prog.cur(s_sty))
            prog.wait("sync", s_stx, prog.cur(s_stx))
            for c2 in range(NT):
                dma(x_re[:, 0, :], d_y1[c2 * 128:(c2 + 1) * 128, :], sX[0])
                prog.wait("vector", sX[0], prog.cur(sX[0]))
                vop(lambda e: e.tensor_copy(out=ld_a, in_=x_re[:, 0, :]), s_v)
                prog.wait("sync", s_v, prog.cur(s_v))
                dma(dd_y1[c2 * 128:(c2 + 1) * 128, :], ld_a, sS[0])
                prog.wait("vector", sS[0], prog.cur(sS[0]))
                dma(qf_ch[:, 0, :], d_xq[c2, :, :], sQ[0])
                prog.wait("vector", sQ[0], prog.cur(sQ[0]))
                vop(lambda e: e.tensor_copy(out=ld_a, in_=qf_ch[:, 0, :]), s_v)
                prog.wait("sync", s_v, prog.cur(s_v))
                dma(dd_xq[c2, :, :], ld_a, sS[0])
                prog.wait("vector", sS[0], prog.cur(sS[0]))

        if stage < 3:
            _finish(prog)
            return nc, dbg

        prog.wait("vector", s_sc, wm_scal)
        vch(lambda e: e.tensor_reduce(
            out=wm_acc[:, 40:41], in_=wm_acc[:, 0:32], axis=AX.X, op=OP.add))
        vch(lambda e: e.tensor_reduce(
            out=wm_acc[:, 41:42], in_=wm_acc[:, 32:40], axis=AX.X, op=OP.add))
        vch(lambda e: e.memset(junk[:, 0:1], 1.0), s_v)
        prog.wait("tensor", s_v, prog.cur(s_v))
        mm(R(0, 0, 1)[0:1, :], wm_acc[:, 40:41], junk[:, 0:1], sem=s_t)
        mm(R(0, 512, 513)[0:1, :], wm_acc[:, 41:42], junk[:, 0:1], sem=s_t)
        prog.wait("vector", s_t, prog.cur(s_t))
        vch(lambda e: e.tensor_scalar(
            out=stat[0:1, 61:62], in0=R(0, 0, 1)[0:1, :], scalar1=1.0 / (DH * D),
            scalar2=None, op0=OP.mult), None)
        vch(lambda e: e.tensor_scalar(
            out=stat[0:1, 62:63], in0=R(0, 512, 513)[0:1, :], scalar1=1.0 / (DH * D),
            scalar2=None, op0=OP.mult), s_v)
        prog.wait("vector", s_v, prog.cur(s_v))
        vch(lambda e: e.tensor_copy(out=xq_st[0:1, 0:2], in_=stat[0:1, 61:63]), s_v)
        prog.wait("tensor", s_v, prog.cur(s_v))
        mm(R(0, 1024, 1026), onesb[0:1, :], xq_st[0:1, 0:2], sem=s_t)
        prog.wait("vector", s_t, prog.cur(s_t))
        vch(lambda e: e.tensor_copy(out=wthr[:, 2:3], in_=R(0, 1024, 1025)), None)
        vch(lambda e: e.tensor_copy(out=wthr[:, 5:6], in_=R(0, 1025, 1026)), None)
        vch(lambda e: e.tensor_scalar(out=wthr[:, 0:1], in0=R(0, 1024, 1025),
                                      scalar1=0.5, scalar2=None, op0=OP.mult), None)
        vch(lambda e: e.tensor_scalar(out=wthr[:, 1:2], in0=R(0, 1024, 1025),
                                      scalar1=-0.5, scalar2=None, op0=OP.mult), None)
        vch(lambda e: e.tensor_scalar(out=wthr[:, 3:4], in0=R(0, 1025, 1026),
                                      scalar1=0.5, scalar2=None, op0=OP.mult), None)
        vch(lambda e: e.tensor_scalar(out=wthr[:, 4:5], in0=R(0, 1025, 1026),
                                      scalar1=-0.5, scalar2=None, op0=OP.mult), s_v)
        prog.wait("vector", s_v, prog.cur(s_v))
        wthr_ready = prog.cur(s_v)

        cw_t = dma(ld_b[:, 0:96].rearrange("p (c j) -> p c j", j=3),
                   d_cw.rearrange("(c p) j -> p c j", p=128), sM[4], slow=True)
        cb_t = dma(cwb[:, 3, :], d_cb.rearrange("a (c p) -> p c a", p=128)[
            :, :, 0], sM[5], slow=True)
        prog.wait("vector", sM[4], cw_t)
        prog.wait("vector", sM[5], cb_t)
        vch(lambda e: e.tensor_copy(
            out=cwb[:, 0:3, :], in_=ld_b[:, 0:96].rearrange("p (c j) -> p j c", j=3)))

        prog.wait("sync", s_stx, prog.cur(s_stx))
        prog.wait("sync", s_sc, wm_scal)
        prog.wait("sync", s_gl2[0], prog.cur(s_gl2[0]))
        prog.wait("sync", s_gl2[1], prog.cur(s_gl2[1]))
        xq_tp = []
        for i in range(NT):
            par = i % 2
            if i >= 2:
                prog.wait("sync", *xq_tp[i - 2])
            ldt = dma(qf_ch[:, par, 0:1024], d_xq[i, :, :], sQ[par])
            prog.wait("sync", sQ[par], ldt)
            xq_tp.append(transpose(xqf[:, :, i * 128:(i + 1) * 128],
                                   qf_ch[:, par, 0:1024]))
        for smx in sT:
            prog.wait("tensor", smx, prog.cur(smx))

        vop(lambda e: e.memset(hfc[:, 0, 0:1], 0.0), None)
        vop(lambda e: e.memset(hfc[:, 0, 2049:2050], 0.0), None)
        vop(lambda e: e.memset(hfc[:, 1, 0:1], 0.0), None)
        vop(lambda e: e.memset(hfc[:, 1, 2049:2050], 0.0), s_v)
        prog.wait("tensor", s_v, prog.cur(s_v))

        w1q_tp = [None, None]
        w1q_used = [None, None]
        hfc_free = [None, None]
        h2st_free = [None, None]
        w2cast = []
        w2_tp = []
        conv_done_v = None
        mm_ev = {}

        def w2_mtile(mt):
            for j4 in range(4):
                par = (mt * 4 + j4) % 2
                buf = wld[:, par, :][:, 0:1024]
                idx = mt * 4 + j4
                if idx >= 2:
                    prog.wait("sync", s_v, w2cast[idx - 2])
                dma(buf, d_w2.rearrange("(a p) k -> p a k", p=128)[:, mt,
                                                                  j4 * 1024:(j4 + 1) * 1024], sL[par])
                prog.wait("vector", sL[par], prog.cur(sL[par]))
                vch(lambda e, buf=buf: e.tensor_scalar(
                    out=wqa, in0=buf, scalar1=wthr[:, 3:4], scalar2=wthr[:, 5:6],
                    op0=OP.is_ge, op1=OP.mult), None)
                vch(lambda e, buf=buf: e.tensor_scalar(
                    out=wqb, in0=buf, scalar1=wthr[:, 4:5], scalar2=wthr[:, 5:6],
                    op0=OP.is_le, op1=OP.mult), None)
                vch(lambda e, j4=j4: e.tensor_tensor(
                    out=w2q_m[:, j4 * 1024:(j4 + 1) * 1024], in0=wqa, in1=wqb,
                    op=OP.subtract), s_v)
                w2cast.append(prog.cur(s_v))
            prog.wait("sync", s_v, w2cast[-1])
            if len(w2_tp) >= 2:
                prog.wait("sync", *w2_tp[-2])
            half, r0_ = mt // 4, (mt % 4) * 128
            w2_tp.append(transpose(w2qt[half][:, :, r0_:r0_ + 128], w2q_m[:]))

        prog.wait("vector", s_sty, prog.cur(s_sty))
        prog.wait("vector", s_stx, prog.cur(s_stx))
        prog.wait("vector", s_sc, prog.cur(s_sc))
        for cc in range(NCC):
            par = cc % 2
            buf = wld[:, par, :][:, 0:1024]
            if cc >= 2:
                prog.wait("sync", s_v, mm_ev.get(("w1c", cc - 2)))
            dma(buf, d_w1[cc * 128:(cc + 1) * 128, :], sL[par])
            prog.wait("vector", sL[par], prog.cur(sL[par]))
            if cc == 0:
                prog.wait("vector", s_v, wthr_ready)
            vch(lambda e, buf=buf: e.tensor_scalar(
                out=wqa, in0=buf, scalar1=wthr[:, 0:1], scalar2=wthr[:, 2:3],
                op0=OP.is_ge, op1=OP.mult), None)
            vch(lambda e, buf=buf: e.tensor_scalar(
                out=wqb, in0=buf, scalar1=wthr[:, 1:2], scalar2=wthr[:, 2:3],
                op0=OP.is_le, op1=OP.mult), None)
            if w1q_tp[par] is not None:
                prog.wait("vector", *w1q_tp[par])
            vch(lambda e, par=par: e.tensor_tensor(
                out=w1q_cc[:, par, :], in0=wqa, in1=wqb, op=OP.subtract), s_v)
            mm_ev[("w1c", cc)] = prog.cur(s_v)
            prog.wait("sync", s_v, mm_ev[("w1c", cc)])
            if w1q_used[par] is not None:
                prog.wait("sync", s_t, w1q_used[par])
            w1q_tp[par] = transpose(
                w1t_cc[:, par, :, :].rearrange("p c d -> p (c d)").rearrange(
                    "p (m r) -> p m r", r=128), w1q_cc[:, par, :])
            prog.wait("tensor", *w1q_tp[par])
            if hfc_free[par] is not None:
                for sm, tg in hfc_free[par]:
                    prog.wait("scalar", sm, tg)
            for t4 in range(4):
                bk = bank(par * 2 + t4 % 2)
                if t4 >= 2:
                    prog.wait("tensor", s_sc, mm_ev[("hfe", cc, t4 - 2)])
                for dr in range(NDC):
                    mm(bk, w1t_cc[:, par, dr, :], xqf[:, dr, t4 * 512:(t4 + 1) * 512],
                       start=(dr == 0), stop=(dr == NDC - 1),
                       sem=s_t if dr == NDC - 1 else None)
                prog.wait("scalar", s_t, prog.cur(s_t))
                scop(lambda e, par=par, t4=t4, bk=bk: e.copy(
                    out=hfc[:, par, 1 + t4 * 512:1 + (t4 + 1) * 512], in_=bk), s_sc)
                mm_ev[("hfe", cc, t4)] = prog.cur(s_sc)
            w1q_used[par] = prog.cur(s_t)
            prog.wait("vector", s_sc, mm_ev[("hfe", cc, 3)])
            if cc == 0:
                prog.wait("vector", sM[4], cw_t)
                prog.wait("vector", sM[5], cb_t)
            if conv_done_v is not None:
                for smx, tgx in conv_done_v:
                    prog.wait("vector", smx, tgx)
            vch(lambda e, par=par, cc=cc: e.tensor_scalar(
                out=conv_y, in0=hfc[:, par, 1:2049], scalar1=cwb[:, 1, cc:cc + 1],
                scalar2=cwb[:, 3, cc:cc + 1], op0=OP.mult, op1=OP.add), None)
            vch(lambda e, par=par, cc=cc: e.scalar_tensor_tensor(
                out=conv_y, in0=hfc[:, par, 0:2048], scalar=cwb[:, 0, cc:cc + 1],
                in1=conv_y, op0=OP.mult, op1=OP.add), None)
            vch(lambda e, par=par, cc=cc: e.scalar_tensor_tensor(
                out=conv_y, in0=hfc[:, par, 2:2050], scalar=cwb[:, 2, cc:cc + 1],
                in1=conv_y, op0=OP.mult, op1=OP.add), s_v)
            conv_t = prog.cur(s_v)
            prog.wait("scalar", s_v, conv_t)
            scop(lambda e: e.activation(
                out=conv_y, in_=conv_y, func=AF.Silu), s_sc)
            silu_t = prog.cur(s_sc)
            prog.wait("sync", s_sc, silu_t)
            tps = []
            tps.append(transpose(hq[:, :, cc * 128:(cc + 1) * 128],
                                 conv_y[:, 0:512]))
            if h2st_free[0] is not None:
                prog.wait("sync", s_sth, prog.cur(s_sth))
            tqs = []
            for q in range(1, NQ):
                tq = transpose(h2st[:, q - 1, 0:512].rearrange(
                    "p (m r) -> p m r", r=128),
                    conv_y[:, q * 512:(q + 1) * 512])
                tqs.append(tq)
                tps.append(tq)
            for q in range(1, NQ):
                prog.wait("gpsimd", *tqs[q - 1])
                h2st_free[0] = gdma(
                    d_h2[q - 1, :, :, cc * 128:(cc + 1) * 128].rearrange(
                        "i p r -> p i r"),
                    h2st[:, q - 1, 0:512].rearrange("p (i r) -> p i r", r=128), s_sth)
            hfc_free[par] = tps
            conv_done_v = tps
            if cc % 4 == 3:
                w2_mtile(cc // 4)

        w2qt_done = w2_tp[-1]
        hq0_done = hfc_free

        if "p3" in dump:
            dd_hf = dbg_out("hfd", [QT * 128, DH], F32)
            for sm_tg in hfc_free:
                if sm_tg:
                    for sm, tg in sm_tg:
                        prog.wait("vector", sm, tg)
            for i in range(QT):
                for hhalf in range(4):
                    vop(lambda e, i=i, hhalf=hhalf: e.tensor_copy(
                        out=ld_a, in_=hq[:, i, hhalf * 1024:(hhalf + 1) * 1024]), s_v)
                    prog.wait("sync", s_v, prog.cur(s_v))
                    dma(dd_hf[i * 128:(i + 1) * 128, hhalf * 1024:(hhalf + 1) * 1024],
                        ld_a, sS[0])
                    prog.wait("vector", sS[0], prog.cur(sS[0]))

        if stage < 4:
            _finish(prog)
            return nc, dbg

        prog.wait("tensor", *w2qt_done)
        prog.wait("vector", *w2qt_done)
        prog.wait("sync", s_sth, prog.cur(s_sth))
        hload = [None, None, None, None]
        y1ld = [None, None]
        sif_tp = [None, None]
        sif_used = [None, None]
        out_prev = None

        for q in range(NQ):
            if q > 0:
                for i in range(QT):
                    prog.wait("sync", s_v, prog.cur(s_v))
                    hload[i] = dma(hq[:, i, :], d_h2[q - 1, i, :, :], sX[i])
            for i in range(QT):
                ti = q * QT + i
                prog.wait("vector", s_sc, prog.cur(s_sc))
                prog.wait("vector", s_sto, prog.cur(s_sto))
                if q > 0:
                    prog.wait("vector", sX[i], hload[i])
                else:
                    for sm_tg in hfc_free:
                        if sm_tg:
                            for sm, tg in sm_tg:
                                prog.wait("vector", sm, tg)
                vch(lambda e, i=i: e.tensor_reduce(
                    out=stat[:, 63:64], in_=hq[:, i, :], axis=AX.X, op=OP.max,
                    apply_absolute_value=True), None)
                vch(lambda e: e.tensor_scalar(
                    out=stat[:, 64:65], in0=stat[:, 63:64], scalar1=1e-5, scalar2=None,
                    op0=OP.max), s_v)
                prog.wait("vector", s_v, prog.cur(s_v))
                vch(lambda e: e.reciprocal(out=stat[:, 65:66], in_=stat[:, 64:65]), None)
                vch(lambda e: e.tensor_scalar(
                    out=stat[:, 66:67], in0=stat[:, 65:66], scalar1=127.0, scalar2=None,
                    op0=OP.mult), None)
                vch(lambda e: e.tensor_scalar(
                    out=stat[:, 67:68], in0=stat[:, 64:65], scalar1=1.0 / 127.0,
                    scalar2=None, op0=OP.mult), s_v)
                prog.wait("vector", s_v, prog.cur(s_v))
                for hf2 in range(2):
                    sl = hq[:, i, hf2 * 2048:(hf2 + 1) * 2048]
                    prog.wait("scalar", s_v, prog.cur(s_v))
                    scop(lambda e, sl=sl: e.activation(
                        out=ysc, in_=sl, func=AF.Identity,
                        scale=stat[:, 66:67]), s_sc)
                    prog.wait("vector", s_sc, prog.cur(s_sc))
                    vch(lambda e: e.tensor_scalar(
                        out=q_i32[:, 0:2048], in0=ysc, scalar1=-0.5, scalar2=None,
                        op0=OP.add), None)
                    vch(lambda e: e.scalar_tensor_tensor(
                        out=usc, in0=q_i32[:, 0:2048], scalar=0.5, in1=ysc,
                        op0=OP.add, op1=OP.is_le), None)
                    vch(lambda e: e.scalar_tensor_tensor(
                        out=u2s, in0=q_i32[:, 0:2048], scalar=1.5, in1=ysc,
                        op0=OP.add, op1=OP.is_le), None)
                    vch(lambda e: e.tensor_tensor(
                        out=usc, in0=usc, in1=u2s, op=OP.add), None)
                    vch(lambda e: e.tensor_tensor(
                        out=usc, in0=usc, in1=q_i32[:, 0:2048], op=OP.add), s_v)
                    prog.wait("scalar", s_v, prog.cur(s_v))
                    scop(lambda e, sl=sl: e.activation(
                        out=sl, in_=usc, func=AF.Identity,
                        scale=stat[:, 67:68]), s_sc)
                si_done = prog.cur(s_sc)
                spar = ti % 2
                if sif_used[spar] is not None:
                    prog.wait("sync", s_t, sif_used[spar])
                prog.wait("sync", s_sc, si_done)
                sif_tp[spar] = transpose(
                    sif[:, spar, :, :].rearrange("p c d -> p (c d)").rearrange(
                        "p (m r) -> p m r", r=128), hq[:, i, :])
                ypar = ti % 2
                if ti >= 2:
                    prog.wait("sync", s_v, mm_ev.get(("y3", ti - 2)))
                y1ld[ypar] = dma(y1in[:, ypar, :], d_y1[ti * 128:(ti + 1) * 128, :],
                                 sQ[ypar])
                prog.wait("tensor", *sif_tp[spar])
                for mh in range(2):
                    bk = bank(4 * (ti % 2) + mh)
                    if ("f2e", ti - 1, mh) in mm_ev:
                        pass
                    if ("f2e", ti - 2, mh) in mm_ev:
                        prog.wait("tensor", s_sc, mm_ev[("f2e", ti - 2, mh)])
                    for c2 in range(NCC):
                        mm(bk, sif[:, spar, c2, :], w2qt[mh][:, c2, :],
                           start=(c2 == 0), stop=(c2 == NCC - 1),
                           sem=s_t if c2 == NCC - 1 else None)
                    prog.wait("scalar", s_t, prog.cur(s_t))
                    scop(lambda e, mh=mh, bk=bk: e.copy(
                        out=y2_t[:, mh * 512:(mh + 1) * 512], in_=bk), s_sc)
                    mm_ev[("f2e", ti, mh)] = prog.cur(s_sc)
                sif_used[spar] = prog.cur(s_t)
                prog.wait("vector", s_sc, mm_ev[("f2e", ti, 1)])
                prog.wait("vector", sQ[ypar], y1ld[ypar])
                vch(lambda e, ypar=ypar: e.tensor_tensor(
                    out=ysc[:, 0:1024], in0=y1in[:, ypar, :], in1=y2_t, op=OP.add), s_v)
                mm_ev[("y3", ti)] = prog.cur(s_v)
                vch(lambda e: e.tensor_reduce(
                    out=stat[:, 70:71], in_=ysc[:, 0:1024], axis=AX.X, op=OP.add), None)
                vch(lambda e: e.scalar_tensor_tensor(
                    out=usc[:, 0:1024], in0=ysc[:, 0:1024], scalar=1.0,
                    in1=ysc[:, 0:1024], op0=OP.mult, op1=OP.mult), None)
                vch(lambda e: e.tensor_reduce(
                    out=stat[:, 71:72], in_=usc[:, 0:1024], axis=AX.X, op=OP.add), s_v)
                vch(lambda e: e.tensor_scalar(
                    out=stat[:, 72:73], in0=stat[:, 70:71], scalar1=1.0 / D,
                    scalar2=None, op0=OP.mult), None)
                vch(lambda e: e.scalar_tensor_tensor(
                    out=stat[:, 73:74], in0=stat[:, 72:73], scalar=-1.0,
                    in1=stat[:, 72:73], op0=OP.mult, op1=OP.mult), None)
                vch(lambda e: e.scalar_tensor_tensor(
                    out=stat[:, 74:75], in0=stat[:, 71:72], scalar=1.0 / D,
                    in1=stat[:, 73:74], op0=OP.mult, op1=OP.add), s_v)
                prog.wait("scalar", s_v, prog.cur(s_v))
                vch(lambda e: e.tensor_scalar(out=stat[:, 74:75], in0=stat[:, 74:75],
                                            scalar1=float(EPS), scalar2=None, op0=OP.add))
                prog.wait("scalar", s_v, prog.cur(s_v))
                scop(lambda e: e.activation(out=stat[:, 75:76], in_=stat[:, 74:75],
                                            func=AF.Sqrt), s_sc)
                prog.wait("vector", s_sc, prog.cur(s_sc))
                vch(lambda e: e.reciprocal(out=stat[:, 76:77], in_=stat[:, 75:76]), None)
                vch(lambda e: e.scalar_tensor_tensor(
                    out=stat[:, 77:78], in0=stat[:, 72:73], scalar=-1.0,
                    in1=stat[:, 76:77], op0=OP.mult, op1=OP.mult), s_v)
                prog.wait("scalar", s_v, prog.cur(s_v))
                if out_prev is not None:
                    prog.wait("scalar", s_sto, out_prev)
                scop(lambda e: e.activation(
                    out=usc[:, 0:1024], in_=ysc[:, 0:1024], func=AF.Identity,
                    bias=stat[:, 77:78], scale=stat[:, 76:77]), s_sc)
                prog.wait("sync", s_sc, prog.cur(s_sc))
                out_prev = dma(d_out[ti * 128:(ti + 1) * 128, :], usc[:, 0:1024], s_sto)

        _finish(prog)
        return nc, dbg


def _finish(prog):
    nc = prog.nc
    for nm, h in prog.sems.items():
        if nm in ("s_gld", "s_gl0", "s_gl1"):
            prog.wait("gpsimd", h, prog.cur(h))
        elif nm[0] == "s" and nm[1] in "LQXTMS" or nm.startswith("s_st"):
            prog.wait("sync", h, prog.cur(h))
    with nc.Block() as block0:
        @block0.gpsimd
        def _(g):
            for nm in list(prog.count):
                g.sem_clear(prog.sems[nm])
    with nc.Block() as block:
        @block.sync
        def _(eng):
            for f in prog.ops["sync"]:
                f(eng)

        @block.scalar
        def _(eng):
            for f in prog.ops["scalar"]:
                f(eng)

        @block.vector
        def _(eng):
            for f in prog.ops["vector"]:
                f(eng)

        @block.tensor
        def _(eng):
            for f in prog.ops["tensor"]:
                f(eng)

        @block.gpsimd
        def _(eng):
            for f in prog.ops["gpsimd"]:
                f(eng)


def make_consts():
    import ml_dtypes
    iu = np.triu(np.ones((128, 128), np.float32), 1)
    il = np.tril(np.ones((128, 128), np.float32), -1)
    ui = np.triu(np.ones((128, 128), np.float32), 0)
    ident = np.eye(128, dtype=np.float32)
    base = np.concatenate([iu, il, ui, ident], 1)
    rep = np.concatenate([np.tile(m, (1, 4)) for m in (iu, il, ui, ident)], 1)
    return np.concatenate([base, rep], 1).astype(ml_dtypes.bfloat16)


def in_map_for(inputs, b):
    return {
        "x": np.ascontiguousarray(np.asarray(inputs["L0_attn_x"])[b]),
        "wq": np.asarray(inputs["L0_attn_Wq"]), "wk": np.asarray(inputs["L0_attn_Wk"]),
        "wv": np.asarray(inputs["L0_attn_Wv"]), "wb": np.asarray(inputs["L0_attn_Wb"]),
        "lnw": np.asarray(inputs["L0_attn_ln_w"]).reshape(1, -1),
        "lnb": np.asarray(inputs["L0_attn_ln_b"]).reshape(1, -1),
        "w1": np.asarray(inputs["L0_ffn_w1"]),
        "cw": np.asarray(inputs["L0_ffn_conv_w"]).reshape(DH, 3),
        "cb": np.asarray(inputs["L0_ffn_conv_b"]).reshape(1, DH),
        "w2": np.asarray(inputs["L0_ffn_w2"]),
        "masks": make_consts(),
    }


def kernel(**inputs):
    from concourse.bass_utils import run_bass_kernel_spmd
    nc, _ = build(stage=99)
    in_maps = [in_map_for(inputs, b) for b in range(8)]
    res = run_bass_kernel_spmd(nc, in_maps, core_ids=list(range(8)))
    out = np.stack([np.asarray(res.results[b]["out"]) for b in range(8)], 0)
    return out.astype(np.float32, copy=False)
